# version 05
# baseline (speedup 1.0000x reference)
import math
import numpy as np
from contextlib import ExitStack

import concourse.bass as bass
import concourse.mybir as mybir
from concourse.bass_utils import run_bass_kernel_spmd

F32 = mybir.dt.float32
BF16 = mybir.dt.bfloat16
I32 = mybir.dt.int32
AF = mybir.ActivationFunctionType
ALU = mybir.AluOpType
AX = mybir.AxisListType

D = 1024
NTOK = 2048
BLK = 1024
CH = 128
NCH = D // CH
NCORE = 8


_ENGS = ("sync", "gpsimd", "scalar", "vector", "tensor")


class Prog:

    SELF_RAW = True
    SELF_WAW = True

    def __init__(self, nc):
        self.nc = nc
        self.ops = []
        self.cum = {e: 0 for e in _ENGS}
        self.last_write = {}
        self.readers = {}
        self._barrier = {}

    def barrier(self):
        snap = dict(self.cum)
        for e in _ENGS:
            self._barrier[e] = snap

    def op(self, eng, fn, reads=(), writes=(), inc=1):
        waits = {}
        bar = self._barrier.pop(eng, None)
        if bar:
            for peng, pval in bar.items():
                if peng != eng and pval > 0:
                    waits[peng] = max(waits.get(peng, 0), pval)

        def add_wait(peng, pval):
            waits[peng] = max(waits.get(peng, 0), pval)

        for key in reads:
            lw = self.last_write.get(key)
            if lw is not None:
                peng, pval = lw
                if peng != eng or self.SELF_RAW:
                    add_wait(peng, pval)
        for key in writes:
            lw = self.last_write.get(key)
            if lw is not None and (lw[0] != eng or self.SELF_WAW):
                add_wait(*lw)
            for reng, rval in self.readers.get(key, ()):
                if reng != eng:
                    add_wait(reng, rval)

        newval = self.cum[eng] + inc
        self.cum[eng] = newval
        self.ops.append((eng, fn, dict(waits), inc, newval))
        for key in reads:
            self.readers.setdefault(key, []).append((eng, newval))
        for key in writes:
            self.last_write[key] = (eng, newval)
            self.readers[key] = []

    def dma(self, eng, out, in_, reads=(), writes=(), **kw):
        self.op(eng, lambda e: e.dma_start(out=out, in_=in_, **kw),
                reads=reads, writes=writes, inc=16)

    def mm_group(self, mms, reads=(), writes=()):
        def fn(e):
            last = None
            for (o, l, r, st, sp) in mms:
                last = e.matmul(out=o, lhsT=l, rhs=r, start=st, stop=sp)
            return last
        self.op("tensor", fn, reads=reads, writes=writes, inc=1)

    def trans(self, out, in_, ident, reads=(), writes=()):
        self.op("tensor", lambda e: e.transpose(out=out, in_=in_, identity=ident),
                reads=reads, writes=writes, inc=1)

    def act(self, out, in_, func, bias=0.0, scale=1.0, accum_out=None,
            reads=(), writes=()):
        self.op("scalar",
                lambda e: e.activation(out=out, in_=in_, func=func, bias=bias,
                                       scale=scale, accum_out=accum_out),
                reads=reads, writes=writes, inc=1)

    def v(self, name, *args, reads=(), writes=(), eng="vector", **kw):
        self.op(eng, lambda e: getattr(e, name)(*args, **kw),
                reads=reads, writes=writes, inc=1)

    def build(self, block, final_waits=True):
        nc = self.nc
        sems = {}
        self._sem_stack = ExitStack()
        for e in _ENGS:
            if self.cum[e] > 0:
                sems[e] = self._sem_stack.enter_context(nc.semaphore("sem_" + e))
        per_eng = {e: [] for e in _ENGS}
        for (eng, fn, waits, inc, newval) in self.ops:
            per_eng[eng].append((fn, waits, inc, newval))

        def runner(eng_name):
            def run(eng):
                waited = {}
                for (fn, waits, inc, newval) in per_eng[eng_name]:
                    for peng, pval in waits.items():
                        if waited.get(peng, 0) >= pval:
                            continue
                        eng.wait_ge(sems[peng], pval)
                        waited[peng] = pval
                    ins = fn(eng)
                    ins.then_inc(sems[eng_name], inc)
                if final_waits and eng_name == "sync":
                    for peng in _ENGS:
                        if self.cum[peng] > 0 and peng != "sync":
                            eng.wait_ge(sems[peng], self.cum[peng])
                    if self.cum["sync"] > 0:
                        eng.wait_ge(sems["sync"], self.cum["sync"])
            return run

        if "sync" not in sems and final_waits:
            sems["sync"] = self._sem_stack.enter_context(nc.semaphore("sem_sync"))
        for e in _ENGS:
            if per_eng[e] or (e == "sync" and final_waits):
                getattr(block, e)(runner(e))


def ceil_div(a, b):
    return (a + b - 1) // b



def alloc(es, nc, name, shape, dtype):
    return es.enter_context(nc.sbuf_tensor(name, shape, dtype))


def palloc(es, nc, name, shape, dtype=F32):
    return es.enter_context(nc.psum_tensor(name, shape, dtype))


class Consts:

    def __init__(self, p, es):
        self.p, self.es, self.nc = p, es, p.nc
        self.loaded = {}

    def get(self, name, arr, dtype=None):
        if name in self.loaded:
            return self.loaded[name]
        nc = self.nc
        if dtype is BF16:
            import ml_dtypes
            arr = arr.astype(ml_dtypes.bfloat16)
        t = self.es.enter_context(nc.sbuf_tensor("c_" + name, list(arr.shape),
                  BF16 if dtype is BF16 else F32, side="right"))
        dr = nc.inline_tensor(np.ascontiguousarray(arr), "ci_" + name)
        self.p.dma("sync", t[:, :], dr.ap(), writes=(("c_" + name),))
        self.loaded[name] = t
        return t

    def ones_col(self):
        return self.get("ones_col", np.ones((128, 1), np.float32))

    def ones_row(self):
        return self.get("ones_row", np.ones((1, 128), np.float32))

    def ident_bf(self):
        return self.get("ident_bf", np.eye(128, dtype=np.float32), BF16)

    def ident_f32(self):
        return self.get("ident_f32", np.eye(128, dtype=np.float32))

    def colv(self, value, parts=128):
        name = f"cv_{parts}_{value!r}".replace("-", "m").replace(".", "_")
        if name in self.loaded:
            return self.loaded[name]
        t = self.es.enter_context(self.nc.sbuf_tensor(name, [parts, 1], F32, side="right"))
        self.p.v("memset", t[:, :], float(value), writes=(t.name,))
        self.loaded[name] = t
        return t



def fmslice(t, W, fc, t0, n):
    return t[:, fc * W + t0: fc * W + t0 + n]


def emit_ln(p, es, cst, x_t, xn_t, W, eps=1e-5, tag="", stats=None):
    nc = p.nc
    es = ExitStack()
    kx, kxn = x_t.name, xn_t.name
    sq = alloc(es, nc, f"ln_sq{tag}", [128, 512], F32)
    t1 = alloc(es, nc, f"ln_t1{tag}", [128, 512], F32)
    r_mean = alloc(es, nc, f"ln_rm{tag}", [1, 512], F32)
    r_var = alloc(es, nc, f"ln_rv{tag}", [1, 512], F32)
    r_rstd = alloc(es, nc, f"ln_rr{tag}", [1, 512], F32)
    ps_sx = palloc(es, nc, f"ln_psx{tag}", [1, 512])
    ps_sq = palloc(es, nc, f"ln_psq{tag}", [1, 512])
    ps_mb = palloc(es, nc, f"ln_pmb{tag}", [128, 512])
    ps_rb = palloc(es, nc, f"ln_prb{tag}", [128, 512])
    oc = cst.ones_col(); orow = cst.ones_row()
    for j in range(W // 512):
        t0 = j * 512
        mms_x, mms_q = [], []
        for fc in range(NCH):
            xs = fmslice(x_t, W, fc, t0, 512)
            p.act(sq[:, :], xs, AF.Square, reads=(kx,), writes=(sq.name,))
            p.mm_group([(ps_sx[:, :], oc[:, :], xs, fc == 0, fc == NCH - 1)],
                       reads=(kx, oc.name), writes=(ps_sx.name,))
            p.mm_group([(ps_sq[:, :], oc[:, :], sq[:, :], fc == 0, fc == NCH - 1)],
                       reads=(sq.name, oc.name), writes=(ps_sq.name,))
        p.v("tensor_scalar_mul", r_mean[:, :], ps_sx[:, :], 1.0 / D,
            reads=(ps_sx.name,), writes=(r_mean.name,))
        p.v("tensor_mul", r_var[:, :], r_mean[:, :], r_mean[:, :],
            reads=(r_mean.name,), writes=(r_var.name,))
        p.v("scalar_tensor_tensor", r_var[:, :], ps_sq[:, :], 1.0 / D,
            r_var[:, :], reads=(ps_sq.name, r_var.name), writes=(r_var.name,),
            op0=ALU.mult, op1=ALU.subtract)
        if stats is not None:
            mrow_t, vrow_t = stats[0], stats[1]
            p.v("tensor_copy", mrow_t[:, t0:t0 + 512], r_mean[:, :],
                reads=(r_mean.name,), writes=(mrow_t.name,))
            if vrow_t is not None:
                p.v("tensor_copy", vrow_t[:, t0:t0 + 512], r_var[:, :],
                    reads=(r_var.name,), writes=(vrow_t.name,))
        epsc = cst.colv(eps, 1)
        p.act(r_var[:, :], r_var[:, :], AF.Sqrt, bias=epsc[0:1, :],
              reads=(r_var.name, epsc.name), writes=(r_var.name,))
        p.v("reciprocal", r_rstd[:, :], r_var[:, :],
            reads=(r_var.name,), writes=(r_rstd.name,))
        if stats is not None and len(stats) == 3:
            p.v("tensor_copy", stats[2][:, t0:t0 + 512], r_rstd[:, :],
                reads=(r_rstd.name,), writes=(stats[2].name,))
        p.mm_group([(ps_mb[:, :], orow[:, :], r_mean[:, :], True, True)],
                   reads=(r_mean.name, orow.name), writes=(ps_mb.name,))
        p.mm_group([(ps_rb[:, :], orow[:, :], r_rstd[:, :], True, True)],
                   reads=(r_rstd.name, orow.name), writes=(ps_rb.name,))
        for fc in range(NCH):
            xs = fmslice(x_t, W, fc, t0, 512)
            xns = fmslice(xn_t, W, fc, t0, 512)
            p.v("tensor_sub", t1[:, :], xs, ps_mb[:, :],
                reads=(kx, ps_mb.name), writes=(t1.name,))
            p.v("tensor_mul", xns, t1[:, :], ps_rb[:, :],
                reads=(t1.name, ps_rb.name), writes=(kxn,))
    es.close()
    p.barrier()


class WBatch:

    def __init__(self, p, es, name, max_cols, dtype=F32, cvt_eng="vector",
                 double=True):
        self.p = p
        self.max_cols = max_cols
        self.cvt_eng = cvt_eng
        n = 2 if double else 1
        self.stage = [alloc(es, p.nc, f"wb_{name}{i}", [128, max_cols], dtype)
                      for i in range(n)]
        self.i = 0

    def load(self, dst_bf, dst_c0, dram2d, rows, cols, eng="sync"):
        st = self.stage[self.i]
        self.i = (self.i + 1) % len(self.stage)
        nch = rows // 128
        self.p.dma(eng, st[:, 0:nch * cols],
                   dram2d.rearrange("(k q) c -> q k c", q=128),
                   writes=(st.name,))
        self.p.v("tensor_copy", dst_bf[:, dst_c0:dst_c0 + nch * cols],
                 st[:, 0:nch * cols], reads=(st.name,), writes=(dst_bf.name,),
                 eng=self.cvt_eng)


class WStage:

    def __init__(self, p, es, name, max_cols):
        self.p, self.nc = p, p.nc
        self.stage = alloc(es, p.nc, f"wst_{name}", [128, max_cols], F32)

    def load(self, dst_bf, dst_c0, dram_ap, cols, dma_eng="sync"):
        self.p.dma(dma_eng, self.stage[:, 0:cols], dram_ap,
                   writes=(self.stage.name,))
        self.p.v("tensor_copy", dst_bf[:, dst_c0:dst_c0 + cols],
                 self.stage[:, 0:cols],
                 reads=(self.stage.name,), writes=(dst_bf.name,))


class WStageI:

    def __init__(self, p, es, name, max_cols):
        self.p, self.nc = p, p.nc
        self.stage = alloc(es, p.nc, f"wsi_{name}", [128, max_cols], I32)

    def load(self, dst_bf, dst_c0, dram_ap, cols, dma_eng="sync"):
        self.p.dma(dma_eng, self.stage[:, 0:cols], dram_ap,
                   writes=(self.stage.name,))
        self.p.v("tensor_copy", dst_bf[:, dst_c0:dst_c0 + cols],
                 self.stage[:, 0:cols],
                 reads=(self.stage.name,), writes=(dst_bf.name,))


def load_w_fm(p, es, cst, dram, name, rows, cols, stager=None, cls=None):
    nc = p.nc
    nr = rows // 128
    t = alloc(es, nc, f"w_{name}", [128, nr * cols], BF16)
    step = max(1, stager.max_cols // cols)
    for k0 in range(0, nr, step):
        k1 = min(nr, k0 + step)
        stager.load(t, k0 * cols, dram[k0 * 128:k1 * 128, 0:cols],
                    (k1 - k0) * 128, cols)
    return t



def emit_elu_evac(p, ps_ap, out_ap, sc_r, sc_m, kps, kout):
    p.act(sc_r[:, :], ps_ap, AF.Relu, reads=(kps,), writes=(sc_r.name,))
    p.v("tensor_sub", sc_m[:, :], ps_ap, sc_r[:, :],
        reads=(kps, sc_r.name), writes=(sc_m.name,))
    p.act(sc_m[:, :], sc_m[:, :], AF.Exp, reads=(sc_m.name,), writes=(sc_m.name,))
    p.v("tensor_add", out_ap, sc_m[:, :], sc_r[:, :],
        reads=(sc_m.name, sc_r.name), writes=(kout,))


def build_l0a(dbg=None):
    nc = bass.Bass("TRN2", target_bir_lowering=False, debug=False)
    HID = 3072
    NHC = HID // 128
    x_d = nc.dram_tensor("x", [D, NTOK], F32, kind="ExternalInput").ap()
    wq_d = nc.dram_tensor("wq", [D, D], F32, kind="ExternalInput").ap()
    wk_d = nc.dram_tensor("wk", [D, D], F32, kind="ExternalInput").ap()
    wv_d = nc.dram_tensor("wv", [D, D], F32, kind="ExternalInput").ap()
    wgt_d = nc.dram_tensor("wgt", [D, HID], I32, kind="ExternalInput").ap()
    wut_d = nc.dram_tensor("wut", [D, HID], I32, kind="ExternalInput").ap()
    wdt_d = nc.dram_tensor("wdt", [HID, D], I32, kind="ExternalInput").ap()

    g0_d = nc.dram_tensor("g0", [D, NTOK], BF16, kind="ExternalOutput").ap()
    qelu_d = nc.dram_tensor("qelu", [2, D, BLK], BF16, kind="ExternalOutput").ap()
    states_d = nc.dram_tensor("states", [2, 16, 64, 65], F32, kind="ExternalOutput").ap()
    acc_d = nc.dram_tensor("acc", [2, 16, 65, BLK], BF16, kind="ExternalOutput").ap()

    p = Prog(nc)
    es0 = ExitStack()
    cst = Consts(p, es0)
    cst.ones_col(); cst.ones_row(); cst.ident_bf()
    mask_np = (np.arange(128)[:, None] <= np.arange(128)[None, :]).astype(np.float32)
    maskT = cst.get("l0mask", np.tile(mask_np, (1, 16)))
    xn = alloc(es0, nc, "xn", [128, NCH * NTOK], BF16)

    with ExitStack() as es1:
        x_t = alloc(es1, nc, "x_t", [128, NCH * NTOK], F32)
        for fc in range(NCH):
            p.dma("sync", x_t[:, fc * NTOK:(fc + 1) * NTOK],
                  x_d[fc * 128:(fc + 1) * 128, :], writes=(x_t.name,))
        emit_ln(p, es1, cst, x_t, xn, NTOK)
    p.barrier()
    if dbg == "ln":
        xn_d = nc.dram_tensor("xn_dbg", [D, NTOK], BF16, kind="ExternalOutput").ap()
        for fc in range(NCH):
            p.dma("gpsimd", xn_d[fc * 128:(fc + 1) * 128, :],
                  xn[:, fc * NTOK:(fc + 1) * NTOK], reads=(xn.name,))
        return nc, p

    with ExitStack() as es2:
        h_all = alloc(es2, nc, "h_all", [128, NHC * NTOK], BF16)
        wg_t = alloc(es2, nc, "wg_t", [128, NCH * 128], BF16)
        wu_t = alloc(es2, nc, "wu_t", [128, NCH * 128], BF16)
        sgate = alloc(es2, nc, "sgate", [128, 512], F32)
        sti = WBatch(p, es2, "ffn", 24 * 128, dtype=I32)
        ps_gs = [palloc(es2, nc, f"ps_g{i}", [128, 512]) for i in range(2)]
        ps_us = [palloc(es2, nc, f"ps_u{i}", [128, 512]) for i in range(2)]
        for hc in range(NHC):
            sti.load(wg_t, 0, wgt_d[0:D, hc * 128:(hc + 1) * 128], D, 128)
            sti.load(wu_t, 0, wut_d[0:D, hc * 128:(hc + 1) * 128], D, 128,
                     eng="scalar")
            for tt in range(NTOK // 512):
                t0 = tt * 512
                ps_g, ps_u = ps_gs[tt % 2], ps_us[tt % 2]
                mg, mu = [], []
                for fc in range(NCH):
                    xs = fmslice(xn, NTOK, fc, t0, 512)
                    mg.append((ps_g[:, :], wg_t[:, fc * 128:(fc + 1) * 128], xs,
                               fc == 0, fc == NCH - 1))
                    mu.append((ps_u[:, :], wu_t[:, fc * 128:(fc + 1) * 128], xs,
                               fc == 0, fc == NCH - 1))
                p.mm_group(mg, reads=(wg_t.name, xn.name), writes=(ps_g.name,))
                p.mm_group(mu, reads=(wu_t.name, xn.name), writes=(ps_u.name,))
                p.act(sgate[:, :], ps_g[:, :], AF.Silu, scale=1.0 / 127.0,
                      reads=(ps_g.name,), writes=(sgate.name,))
                p.v("tensor_mul", h_all[:, hc * NTOK + t0: hc * NTOK + t0 + 512],
                    sgate[:, :], ps_u[:, :],
                    reads=(sgate.name, ps_u.name), writes=(h_all.name,))
        wd_t = alloc(es2, nc, "wd_t", [128, NHC * 128], BF16)
        g0t = alloc(es2, nc, "g0t", [128, 512], BF16)
        ps_os = [palloc(es2, nc, f"ps_o{i}", [128, 512]) for i in range(2)]
        for oc in range(NCH):
            sti.load(wd_t, 0, wdt_d[0:HID, oc * 128:(oc + 1) * 128], HID, 128)
            for tt in range(NTOK // 512):
                t0 = tt * 512
                ps_o = ps_os[tt % 2]
                mo = [(ps_o[:, :], wd_t[:, hc * 128:(hc + 1) * 128],
                       h_all[:, hc * NTOK + t0: hc * NTOK + t0 + 512],
                       hc == 0, hc == NHC - 1) for hc in range(NHC)]
                p.mm_group(mo, reads=(wd_t.name, h_all.name), writes=(ps_o.name,))
                p.act(g0t[:, :], ps_o[:, :], AF.Silu, scale=1.0 / (127.0 * 127.0),
                      reads=(ps_o.name,), writes=(g0t.name,))
                p.dma("gpsimd", g0_d[oc * 128:(oc + 1) * 128, t0:t0 + 512],
                      g0t[:, :], reads=(g0t.name,))
    p.barrier()
    if dbg == "ffn":
        return nc, p
    import os as _os
    _skip = set((_os.environ.get("L0SKIP") or "").split(","))
    _nch_lim = int(_os.environ.get("L0NCHUNK") or 8)
    _nblk = int(_os.environ.get("L0NBLK") or 2)

    with ExitStack() as es3:
        wstg = WBatch(p, es3, "qkv", 4 * D, double=False)
        wq_t = load_w_fm(p, es3, cst, wq_d, "wq", D, D, stager=wstg)
        wk_t = load_w_fm(p, es3, cst, wk_d, "wk", D, D, stager=wstg)
        wv_t = load_w_fm(p, es3, cst, wv_d, "wv", D, D, stager=wstg)
        qelu_t = alloc(es3, nc, "qelu_t", [128, NCH * BLK], BF16)
        kelu_t = alloc(es3, nc, "kelu_t", [128, NCH * BLK], BF16)
        vaug = alloc(es3, nc, "vaug", [128, 1040], BF16)
        ktm = alloc(es3, nc, "ktm", [128, 1024], BF16)
        abf = alloc(es3, nc, "abf", [128, 2048], BF16)
        acc_sb = alloc(es3, nc, "acc_sb", [65, 2048], BF16)
        state_f = alloc(es3, nc, "state_f", [128, 520], F32)
        state_b = alloc(es3, nc, "state_b", [128, 520], BF16)
        sc_r = alloc(es3, nc, "sc_r", [128, 512], F32)
        sc_m = alloc(es3, nc, "sc_m", [128, 512], F32)
        big1 = palloc(es3, nc, "big1", [128, 2048])
        big2 = palloc(es3, nc, "big2", [128, 1536])
        ps_tr = palloc(es3, nc, "ps_tr", [128, 128], BF16)
        p.v("memset", vaug[:, :], 1.0, writes=(vaug.name,))

        for b in range(_nblk):
            tb = b * BLK
            for (w_t, dst) in ((wq_t, qelu_t), (wk_t, kelu_t)):
                for oc in range(NCH):
                    for hh in range(2):
                        t0 = tb + hh * 512
                        mm = [(big1[:, 0:512],
                               w_t[:, fc * D + oc * 128: fc * D + (oc + 1) * 128],
                               fmslice(xn, NTOK, fc, t0, 512),
                               fc == 0, fc == NCH - 1) for fc in range(NCH)]
                        p.mm_group(mm, reads=(w_t.name, xn.name), writes=(big1.name,))
                        emit_elu_evac(p, big1[:, 0:512],
                                      dst[:, oc * BLK + hh * 512: oc * BLK + (hh + 1) * 512],
                                      sc_r, sc_m, big1.name, dst.name)
            p.v("memset", state_f[:, :], 0.0, writes=(state_f.name,))
            p.v("memset", state_b[:, :], 0.0, writes=(state_b.name,))
            if dbg == "proj":
                for fc in range(NCH):
                    p.dma("gpsimd", qelu_d[b, fc * 128:(fc + 1) * 128, :],
                          qelu_t[:, fc * BLK:(fc + 1) * BLK], reads=(qelu_t.name,))
                continue
            HORDER = [h for h in range(16) if h % 2 == 0] + \
                     [h for h in range(16) if h % 2 == 1]
            acol = lambda h: (h % 2) * 1024 + (h // 2) * 128
            for c in range(_nch_lim):
                tl = tb + c * 128
                for hh in range(2 if "v" not in _skip else 0):
                    mm = [(big2[:, hh * 512:(hh + 1) * 512],
                           fmslice(xn, NTOK, fc, tl, 128),
                           wv_t[:, fc * D + hh * 512: fc * D + (hh + 1) * 512],
                           fc == 0, fc == NCH - 1) for fc in range(NCH)]
                    p.mm_group(mm, reads=(xn.name, wv_t.name), writes=(big2.name,))
                if "vaug" in _skip:
                    pass
                else:
                    p.v("tensor_copy",
                        vaug[:, :].rearrange("p (h e) -> p h e", e=65)[:, :, 0:64],
                        big2[:, 0:1024].rearrange("p (h d) -> p h d", d=64),
                        reads=(big2.name,), writes=(vaug.name,))
                for fc in range(NCH if "ktm" not in _skip else 0):
                    p.trans(ps_tr[:, :], kelu_t[:, fc * BLK + c * 128: fc * BLK + (c + 1) * 128],
                            cst.ident_bf()[:, :],
                            reads=(kelu_t.name, cst.ident_bf().name), writes=(ps_tr.name,))
                    p.v("tensor_copy", ktm[:, fc * 128:(fc + 1) * 128], ps_tr[:, :],
                        reads=(ps_tr.name,), writes=(ktm.name,))
                if "at" in _skip:
                    continue
                mm = []
                horder = [h for h in range(16) if h % 2 == 0] +                          [h for h in range(16) if h % 2 == 1]
                acol = lambda h: (h % 2) * 1024 + (h // 2) * 128
                for h in horder:
                    fc, r0 = h // 2, 64 * (h % 2)
                    lhs = kelu_t[r0:r0 + 64, (fc) * BLK + c * 128: (fc) * BLK + (c + 1) * 128]
                    rhs = qelu_t[r0:r0 + 64, (fc) * BLK + c * 128: (fc) * BLK + (c + 1) * 128]
                    mm.append((big1[:, acol(h): acol(h) + 128], lhs, rhs, True, True))
                p.mm_group(mm, reads=(kelu_t.name, qelu_t.name), writes=(big1.name,))
                if "atmask" not in _skip:
                    p.v("tensor_mul", abf[:, :], big1[:, :], maskT[:, :],
                        reads=(big1.name, maskT.name), writes=(abf.name,))
                if "out" in _skip:
                    continue
                mm = []
                for h in range(16):
                    fc, r0 = h // 2, 64 * (h % 2)
                    mm.append((big1[0:65, acol(h): acol(h) + 128],
                               vaug[:, h * 65:(h + 1) * 65],
                               abf[:, acol(h): acol(h) + 128], True, False))
                    mm.append((big1[0:65, acol(h): acol(h) + 128],
                               state_b[r0:r0 + 64, (h // 2) * 65:(h // 2) * 65 + 65],
                               qelu_t[r0:r0 + 64, fc * BLK + c * 128: fc * BLK + (c + 1) * 128],
                               False, True))
                p.mm_group(mm, reads=(vaug.name, abf.name, state_b.name, qelu_t.name),
                           writes=(big1.name,))
                p.v("tensor_copy", acc_sb[:, :], big1[0:65, :],
                    reads=(big1.name,), writes=(acc_sb.name,))
                if "accdma" not in _skip:
                    for par in (0, 1):
                        p.dma("gpsimd",
                              acc_d[b, par::2, :, c * 128:(c + 1) * 128].rearrange("j r q -> r j q"),
                              acc_sb[:, par * 1024:(par + 1) * 1024].rearrange("r (j q) -> r j q", q=128),
                              reads=(acc_sb.name,))
                if "st" in _skip:
                    continue
                mm = []
                for h in range(16):
                    r0 = 64 * (h % 2)
                    j = h // 2
                    mm.append((big2[r0:r0 + 64, j * 65:(j + 1) * 65],
                               ktm[:, h * 64:(h + 1) * 64],
                               vaug[:, h * 65:(h + 1) * 65], True, True))
                p.mm_group(mm, reads=(ktm.name, vaug.name), writes=(big2.name,))
                p.v("tensor_add", state_f[:, :], state_f[:, :], big2[:, 0:520],
                    reads=(state_f.name, big2.name), writes=(state_f.name,))
                p.v("tensor_copy", state_b[:, :], state_f[:, :],
                    reads=(state_f.name,), writes=(state_b.name,))
            if "stdma" in _skip:
                continue
            p.dma("gpsimd", states_d[b, 0::2].rearrange("h d e -> d h e"),
                  state_f[0:64, :].rearrange("d (j e) -> d j e", e=65),
                  reads=(state_f.name,))
            p.dma("gpsimd", states_d[b, 1::2].rearrange("h d e -> d h e"),
                  state_f[64:128, :].rearrange("d (j e) -> d j e", e=65),
                  reads=(state_f.name,))
            for fc in range(NCH):
                p.dma("gpsimd", qelu_d[b, fc * 128:(fc + 1) * 128, :],
                      qelu_t[:, fc * BLK:(fc + 1) * BLK], reads=(qelu_t.name,))
    p.barrier()
    return nc, p


def run_launch(nc_p, in_map_list, core_ids=None):
    nc, p = nc_p
    with nc.Block() as block:
        p.build(block)
    return run_bass_kernel_spmd(nc, in_map_list,
                                core_ids=core_ids or list(range(NCORE)))



def emit_fm_load(p, t, dram, W, name=None):
    for fc in range(NCH):
        p.dma("sync", t[:, fc * W:(fc + 1) * W], dram[fc * 128:(fc + 1) * 128, :],
              writes=(t.name,))


def emit_fm_store(p, dram, t, W, eng="gpsimd"):
    for fc in range(NCH):
        p.dma(eng, dram[fc * 128:(fc + 1) * 128, :], t[:, fc * W:(fc + 1) * W],
              reads=(t.name,))




def emit_l1ffn_gating(p, es, cst, nc, x_t, mrow, vrow, sim_d, gates_d,
                      rw_fm, rwb_fm, rw_d):
    simn = alloc(es, nc, "g_simn", [128, NCH * 16], F32)
    sim_t = alloc(es, nc, "g_sim", [128, NCH * 16], F32)
    for fc in range(NCH):
        p.dma("sync", sim_t[:, fc * 16:(fc + 1) * 16],
              sim_d[fc * 128:(fc + 1) * 128, :], writes=(sim_t.name,))
    sq16 = alloc(es, nc, "g_sq16", [128, 16], F32)
    row16a = alloc(es, nc, "g_r16", [1, 16], F32)
    negs1 = alloc(es, nc, "g_ns1", [1, 16], F32)
    rb16 = alloc(es, nc, "g_rb16", [128, 16], F32)
    grow = alloc(es, nc, "g_grow", [1, 16], F32)
    sgb = alloc(es, nc, "g_sgb", [128, 16], F32)
    normsq = alloc(es, nc, "g_nsq", [1, NTOK], F32)
    st17 = alloc(es, nc, "g_st17", [17, NTOK], F32)
    tm17 = alloc(es, nc, "g_tm17", [128, 17], F32)
    colA = alloc(es, nc, "g_colA", [128, 1], F32)
    colB = alloc(es, nc, "g_colB", [128, 1], F32)
    t16a = alloc(es, nc, "g_t16a", [128, 16], F32)
    t16b = alloc(es, nc, "g_t16b", [128, 16], F32)
    t16c = alloc(es, nc, "g_t16c", [128, 16], F32)
    eqa = alloc(es, nc, "g_eqa", [128, 16], F32)
    eqb = alloc(es, nc, "g_eqb", [128, 16], F32)
    esp = ExitStack()
    ps16 = palloc(esp, nc, "g_ps16", [1, 16])
    ps128 = palloc(esp, nc, "g_ps128", [128, 16])
    ps_l = palloc(esp, nc, "g_psl", [16, 512])
    ps_t17 = palloc(esp, nc, "g_pst17", [128, 17])
    ps_t16 = palloc(esp, nc, "g_pst16", [16, 128])
    for fc in range(NCH):
        p.act(sq16[:, :], sim_t[:, fc * 16:(fc + 1) * 16], AF.Square,
              reads=(sim_t.name,), writes=(sq16.name,))
        p.mm_group([(ps16[:, :], cst.ones_col()[:, :], sq16[:, :],
                     fc == 0, fc == NCH - 1)],
                   reads=(sq16.name, cst.ones_col().name), writes=(ps16.name,))
    p.act(row16a[:, :], ps16[:, :], AF.Sqrt, reads=(ps16.name,), writes=(row16a.name,))
    p.v("tensor_scalar_max", row16a[:, :], row16a[:, :], 1e-12,
        reads=(row16a.name,), writes=(row16a.name,))
    p.v("reciprocal", row16a[:, :], row16a[:, :],
        reads=(row16a.name,), writes=(row16a.name,))
    p.mm_group([(ps128[:, :], cst.ones_row()[:, :], row16a[:, :], True, True)],
               reads=(cst.ones_row().name, row16a.name), writes=(ps128.name,))
    p.v("tensor_copy", rb16[:, :], ps128[:, :], reads=(ps128.name,), writes=(rb16.name,))
    for fc in range(NCH):
        p.v("tensor_mul", simn[:, fc * 16:(fc + 1) * 16],
            sim_t[:, fc * 16:(fc + 1) * 16], rb16[:, :],
            reads=(sim_t.name, rb16.name), writes=(simn.name,))
    for fc in range(NCH):
        p.mm_group([(ps16[:, :], cst.ones_col()[:, :], simn[:, fc * 16:(fc + 1) * 16],
                     fc == 0, fc == NCH - 1)],
                   reads=(simn.name, cst.ones_col().name), writes=(ps16.name,))
    p.v("tensor_scalar_mul", negs1[:, :], ps16[:, :], -1.0,
        reads=(ps16.name,), writes=(negs1.name,))
    p.dma("sync", grow[:, :], gates_d, writes=(grow.name,))
    p.act(grow[:, :], grow[:, :], AF.Sigmoid, reads=(grow.name,), writes=(grow.name,))
    p.mm_group([(ps128[:, :], cst.ones_row()[:, :], grow[:, :], True, True)],
               reads=(cst.ones_row().name, grow.name), writes=(ps128.name,))
    p.v("tensor_copy", sgb[:, :], ps128[:, :], reads=(ps128.name,), writes=(sgb.name,))
    p.v("tensor_scalar_mul", normsq[:, :], vrow[:, :], float(D),
        reads=(vrow.name,), writes=(normsq.name,))
    for tt in range(NTOK // 512):
        t0 = tt * 512
        mm = [(ps_l[:, :], simn[:, fc * 16:(fc + 1) * 16],
               fmslice(x_t, NTOK, fc, t0, 512), fc == 0, False)
              for fc in range(NCH)]
        mm.append((ps_l[:, :], negs1[:, :], mrow[:, t0:t0 + 512], False, True))
        p.mm_group(mm, reads=(simn.name, x_t.name, negs1.name, mrow.name),
                   writes=(ps_l.name,))
        p.v("tensor_copy", st17[0:16, t0:t0 + 512], ps_l[:, :],
            reads=(ps_l.name,), writes=(st17.name,))
    p.dma("gpsimd", st17[16:17, :], normsq[:, :],
          reads=(normsq.name,), writes=(st17.name,))
    for tc in range(NTOK // 128):
        t0 = tc * 128
        p.trans(ps_t17[:, :], st17[:, t0:t0 + 128], cst.ident_f32()[0:17, 0:17],
                reads=(st17.name, cst.ident_f32().name), writes=(ps_t17.name,))
        p.v("tensor_copy", tm17[:, :], ps_t17[:, :],
            reads=(ps_t17.name,), writes=(tm17.name,))
        p.act(colA[:, :], tm17[:, 16:17], AF.Sqrt, reads=(tm17.name,), writes=(colA.name,))
        p.v("tensor_scalar_max", colA[:, :], colA[:, :], 1e-12,
            reads=(colA.name,), writes=(colA.name,))
        p.v("reciprocal", colA[:, :], colA[:, :], reads=(colA.name,), writes=(colA.name,))
        p.v("tensor_scalar_mul", t16a[:, :], tm17[:, 0:16], colA[:, :],
            reads=(tm17.name, colA.name), writes=(t16a.name,))
        p.v("tensor_sub", t16a[:, :], t16a[:, :], sgb[:, :],
            reads=(t16a.name, sgb.name), writes=(t16a.name,))
        p.v("tensor_relu", t16b[:, :], t16a[:, :],
            reads=(t16a.name,), writes=(t16b.name,))
        p.v("tensor_single_scalar", t16c[:, :], t16b[:, :], 0.0, op=ALU.is_gt,
            reads=(t16b.name,), writes=(t16c.name,))
        p.v("reduce_sum", colA[:, :], t16c[:, :], axis=AX.X,
            reads=(t16c.name,), writes=(colA.name,))
        p.v("tensor_single_scalar", colA[:, :], colA[:, :], 0.0, op=ALU.is_equal,
            reads=(colA.name,), writes=(colA.name,))
        p.v("reduce_max", colB[:, :], t16a[:, :], axis=AX.X,
            reads=(t16a.name,), writes=(colB.name,))
        p.v("tensor_single_scalar", eqa[:, :], t16a[:, :], colB[:, :], op=ALU.is_equal,
            reads=(t16a.name, colB.name), writes=(eqa.name,))
        p.v("scalar_tensor_tensor", eqb[:, :], eqa[:, :], -1e30, t16a[:, :],
            op0=ALU.mult, op1=ALU.add,
            reads=(eqa.name, t16a.name), writes=(eqb.name,))
        p.v("reduce_max", colB[:, :], eqb[:, :], axis=AX.X,
            reads=(eqb.name,), writes=(colB.name,))
        p.v("tensor_single_scalar", eqb[:, :], eqb[:, :], colB[:, :], op=ALU.is_equal,
            reads=(eqb.name, colB.name), writes=(eqb.name,))
        p.v("tensor_add", eqa[:, :], eqa[:, :], eqb[:, :],
            reads=(eqa.name, eqb.name), writes=(eqa.name,))
        p.v("tensor_scalar_mul", eqa[:, :], eqa[:, :], colA[:, :],
            reads=(eqa.name, colA.name), writes=(eqa.name,))
        p.v("tensor_max", t16c[:, :], t16c[:, :], eqa[:, :],
            reads=(t16c.name, eqa.name), writes=(t16c.name,))
        p.act(t16b[:, :], t16b[:, :], AF.Exp, reads=(t16b.name,), writes=(t16b.name,))
        p.v("tensor_mul", t16b[:, :], t16b[:, :], t16c[:, :],
            reads=(t16b.name, t16c.name), writes=(t16b.name,))
        p.v("reduce_sum", colA[:, :], t16b[:, :], axis=AX.X,
            reads=(t16b.name,), writes=(colA.name,))
        p.v("reciprocal", colA[:, :], colA[:, :], reads=(colA.name,), writes=(colA.name,))
        p.v("tensor_scalar_mul", t16b[:, :], t16b[:, :], colA[:, :],
            reads=(t16b.name, colA.name), writes=(t16b.name,))
        p.trans(ps_t16[:, :], t16b[:, :], cst.ident_f32()[:, :],
                reads=(t16b.name, cst.ident_f32().name), writes=(ps_t16.name,))
        p.v("tensor_copy", rw_fm[:, t0:t0 + 128], ps_t16[:, :],
            reads=(ps_t16.name,), writes=(rw_fm.name,))
    p.v("tensor_copy", rwb_fm[:, :], rw_fm[:, :],
        reads=(rw_fm.name,), writes=(rwb_fm.name,))
    for hh in range(NTOK // 512):
        p.dma("gpsimd", rw_d[:, hh * 512:(hh + 1) * 512],
              rw_fm[:, hh * 512:(hh + 1) * 512], reads=(rw_fm.name,))
    esp.close()
    p.barrier()


def build_l0b():
    nc = bass.Bass("TRN2", target_bir_lowering=False, debug=False)
    x_d = nc.dram_tensor("x", [D, NTOK], F32, kind="ExternalInput").ap()
    g0_d = nc.dram_tensor("g0", [D, NTOK], BF16, kind="ExternalInput").ap()
    acc_d = nc.dram_tensor("acc", [2, 16, 65, BLK], BF16, kind="ExternalInput").ap()
    qelu_d = nc.dram_tensor("qelu", [2, D, BLK], BF16, kind="ExternalInput").ap()
    allst_d = nc.dram_tensor("allst", [16, 128, 520], F32, kind="ExternalInput").ap()
    pmask_d = nc.dram_tensor("pmask", [2, 16, 128, 1], F32, kind="ExternalInput").ap()
    wo_d = nc.dram_tensor("wo", [D, D], F32, kind="ExternalInput").ap()
    win_d = nc.dram_tensor("win", [D, 3072], F32, kind="ExternalInput").ap()
    bin_d = nc.dram_tensor("bin", [3072, 1], F32, kind="ExternalInput").ap()
    bv_row_d = nc.dram_tensor("bv_row", [1, D], F32, kind="ExternalInput").ap()
    ropeq_d = nc.dram_tensor("ropeq", [2, 128, NTOK], F32, kind="ExternalInput").ap()
    ropek_d = nc.dram_tensor("ropek", [2, 128, NTOK], F32, kind="ExternalInput").ap()
    sim_d = nc.dram_tensor("sim", [D, 16], F32, kind="ExternalInput").ap()
    gates_d = nc.dram_tensor("gates", [1, 16], F32, kind="ExternalInput").ap()
    qall_d = nc.dram_tensor("qall", [D, D], F32, kind="ExternalInput").ap()
    kall_d = nc.dram_tensor("kall", [D, D], F32, kind="ExternalInput").ap()
    vall_d = nc.dram_tensor("vall", [D, D], F32, kind="ExternalInput").ap()

    x1_d = nc.dram_tensor("x1", [D, NTOK], F32, kind="ExternalOutput").ap()
    q1_d = nc.dram_tensor("q1", [D, NTOK], BF16, kind="ExternalOutput").ap()
    k1_d = nc.dram_tensor("k1", [D, NTOK], BF16, kind="ExternalOutput").ap()
    v1_d = nc.dram_tensor("v1", [NTOK, 1040], BF16, kind="ExternalOutput").ap()
    qm_d = nc.dram_tensor("qm", [64, NTOK], BF16, kind="ExternalOutput").ap()
    km_d = nc.dram_tensor("km", [64, NTOK], BF16, kind="ExternalOutput").ap()
    vm_d = nc.dram_tensor("vm", [NTOK, 65], BF16, kind="ExternalOutput").ap()
    rw_d = nc.dram_tensor("rw", [16, NTOK], F32, kind="ExternalOutput").ap()

    p = Prog(nc)
    es0 = ExitStack()
    cst = Consts(p, es0)
    cst.ones_col(); cst.ones_row(); cst.ident_bf(); cst.ident_f32()
    cst.colv(1e-5, 1); cst.colv(1e-6, 1)
    perm = np.zeros((128, 128), np.float32)
    for blk in range(2):
        for i in range(32):
            perm[blk * 64 + i, blk * 64 + 32 + i] = 1.0
            perm[blk * 64 + 32 + i, blk * 64 + i] = 1.0
    permT = cst.get("rope_perm", perm, BF16)
    ehot_np = np.zeros((16, 16 * 64), np.float32)
    for e in range(16):
        ehot_np[e, e * 64:(e + 1) * 64] = 1.0
    ehot = cst.get("ehot16", ehot_np, BF16)

    xn1 = alloc(es0, nc, "xn1", [128, NCH * NTOK], BF16)
    rw_fm = alloc(es0, nc, "rw_fm", [16, NTOK], F32)
    rwb_fm = alloc(es0, nc, "rwb_fm", [16, NTOK], BF16)
    es_mid = ExitStack()
    x_t = alloc(es_mid, nc, "x_t", [128, NCH * NTOK], F32)
    emit_fm_load(p, x_t, x_d, NTOK)

    attn_fm = alloc(es_mid, nc, "attn_fm", [128, NCH * NTOK], BF16)
    with ExitStack() as es1:
        pf_f = alloc(es1, nc, "pf_f", [128, 520], F32)
        pf_b = alloc(es1, nc, "pf_b", [128, 520], BF16)
        gst = alloc(es1, nc, "gst", [128, 520], F32)
        pm = alloc(es1, nc, "pm", [128, 1], F32)
        qb_t = alloc(es1, nc, "qb_t", [128, NCH * BLK], BF16)
        acc_t = alloc(es1, nc, "acc_t", [65, BLK], BF16)
        sc65 = alloc(es1, nc, "sc65", [65, BLK], F32)
        drow = alloc(es1, nc, "drow", [1, BLK], F32)
        amh = alloc(es1, nc, "amh", [64, BLK], BF16)
        ps2 = palloc(es1, nc, "ps2", [65, BLK])
        psb = palloc(es1, nc, "psb", [64, BLK])
        for ob in range(2):
            p.v("memset", pf_f[:, :], 0.0, writes=(pf_f.name,))
            for gb in range(16):
                p.dma("sync", gst[:, :], allst_d[gb], writes=(gst.name,))
                p.dma("sync", pm[:, :], pmask_d[ob, gb], writes=(pm.name,))
                p.v("scalar_tensor_tensor", pf_f[:, :], gst[:, :], pm[:, :],
                    pf_f[:, :], op0=ALU.mult, op1=ALU.add,
                    reads=(gst.name, pm.name, pf_f.name), writes=(pf_f.name,))
            p.v("tensor_copy", pf_b[:, :], pf_f[:, :],
                reads=(pf_f.name,), writes=(pf_b.name,))
            for fc in range(NCH):
                p.dma("sync", qb_t[:, fc * BLK:(fc + 1) * BLK],
                      qelu_d[ob, fc * 128:(fc + 1) * 128, :], writes=(qb_t.name,))
            for h in range(16):
                fc, r0 = h // 2, 64 * (h % 2)
                p.dma("sync", acc_t[:, :], acc_d[ob, h], writes=(acc_t.name,))
                mm = []
                for hh in range(2):
                    mm.append((ps2[:, hh * 512:(hh + 1) * 512],
                               pf_b[r0:r0 + 64, (h // 2) * 65:(h // 2) * 65 + 65],
                               qb_t[r0:r0 + 64, fc * BLK + hh * 512: fc * BLK + (hh + 1) * 512],
                               True, True))
                p.mm_group(mm, reads=(pf_b.name, qb_t.name), writes=(ps2.name,))
                p.v("tensor_add", sc65[:, :], acc_t[:, :], ps2[:, :],
                    reads=(acc_t.name, ps2.name), writes=(sc65.name,))
                p.v("tensor_scalar_add", drow[:, :], sc65[64:65, :], 1e-6,
                    reads=(sc65.name,), writes=(drow.name,))
                p.v("reciprocal", drow[:, :], drow[:, :],
                    reads=(drow.name,), writes=(drow.name,))
                mm = [(psb[:, hh * 512:(hh + 1) * 512],
                       cst.ones_row()[:, 0:64],
                       drow[:, hh * 512:(hh + 1) * 512], True, True)
                      for hh in range(2)]
                p.mm_group(mm, reads=(cst.ones_row().name, drow.name), writes=(psb.name,))
                p.v("tensor_mul", amh[:, :], sc65[0:64, :], psb[:, :],
                    reads=(sc65.name, psb.name), writes=(amh.name,))
                p.dma("gpsimd",
                      attn_fm[r0:r0 + 64, fc * NTOK + ob * BLK: fc * NTOK + (ob + 1) * BLK],
                      amh[:, :], reads=(amh.name,), writes=(attn_fm.name,))
    p.barrier()

    with ExitStack() as es2:
        mrow = alloc(es2, nc, "mrow", [1, NTOK], F32)
        vrow = alloc(es2, nc, "vrow", [1, NTOK], F32)
        es2a = ExitStack()
        wstg = WBatch(p, es2a, "wo", 4 * D, double=False)
        wo_t = load_w_fm(p, es2a, cst, wo_d, "wo", D, D, stager=wstg)
        g0_t = alloc(es2a, nc, "g0_t", [128, 512], BF16)
        sc_a = alloc(es2a, nc, "sc_a", [128, 512], F32)
        ps_s = palloc(es2a, nc, "ps_s", [128, 512])
        for oc in range(NCH):
            for tt in range(NTOK // 512):
                t0 = tt * 512
                mm = [(ps_s[:, :], wo_t[:, fc * D + oc * 128: fc * D + (oc + 1) * 128],
                       fmslice(attn_fm, NTOK, fc, t0, 512), fc == 0, fc == NCH - 1)
                      for fc in range(NCH)]
                p.mm_group(mm, reads=(wo_t.name, attn_fm.name), writes=(ps_s.name,))
                p.dma("gpsimd", g0_t[:, :], g0_d[oc * 128:(oc + 1) * 128, t0:t0 + 512],
                      writes=(g0_t.name,))
                p.v("tensor_mul", sc_a[:, :], ps_s[:, :], g0_t[:, :],
                    reads=(ps_s.name, g0_t.name), writes=(sc_a.name,))
                xs = fmslice(x_t, NTOK, oc, t0, 512)
                p.v("tensor_add", xs, xs, sc_a[:, :],
                    reads=(x_t.name, sc_a.name), writes=(x_t.name,))
        emit_fm_store(p, x1_d, x_t, NTOK)
        es2a.close()
        p.barrier()
        emit_ln(p, es2, cst, x_t, xn1, NTOK, tag="1", stats=(mrow, vrow))
        emit_l1ffn_gating(p, es2, cst, nc, x_t, mrow, vrow, sim_d, gates_d,
                          rw_fm, rwb_fm, rw_d)
    es_mid.close()
    p.barrier()

    with ExitStack() as es3:
        wstg = WBatch(p, es3, "win", 3072, double=False)
        win_t = alloc(es3, nc, "win_t", [128, NCH * 3072], BF16)
        for fc in range(NCH):
            wstg.load(win_t, fc * 3072, win_d[fc * 128:(fc + 1) * 128, 0:3072],
                      128, 3072)
        bin_t = alloc(es3, nc, "bin_t", [128, 24], F32)
        for bc in range(24):
            p.dma("sync", bin_t[:, bc:bc + 1], bin_d[bc * 128:(bc + 1) * 128, :],
                  writes=(bin_t.name,))
        ropq = alloc(es3, nc, "ropq", [128, 2 * NTOK], F32)
        ropk = alloc(es3, nc, "ropk", [128, 2 * NTOK], F32)
        for i in range(2):
            p.dma("sync", ropq[:, i * NTOK:(i + 1) * NTOK], ropeq_d[i], writes=(ropq.name,))
            p.dma("sync", ropk[:, i * NTOK:(i + 1) * NTOK], ropek_d[i], writes=(ropk.name,))
        prj = alloc(es3, nc, "prj", [128, 512], BF16)
        sc_1 = alloc(es3, nc, "sc_1", [128, 512], F32)
        sc_2 = alloc(es3, nc, "sc_2", [128, 512], F32)
        qk_t = alloc(es3, nc, "qk_t", [128, 512], BF16)
        ps_p = palloc(es3, nc, "ps_p", [128, 512])
        ps_r = palloc(es3, nc, "ps_r", [128, 512])
        for (part, rop, out_d) in ((0, ropq, q1_d), (1, ropk, k1_d)):
            for oc in range(NCH):
                bc = part * 8 + oc
                for tt in range(NTOK // 512):
                    t0 = tt * 512
                    mm = [(ps_p[:, :],
                           win_t[:, fc * 3072 + part * 1024 + oc * 128:
                                 fc * 3072 + part * 1024 + (oc + 1) * 128],
                           fmslice(xn1, NTOK, fc, t0, 512), fc == 0, fc == NCH - 1)
                          for fc in range(NCH)]
                    p.mm_group(mm, reads=(win_t.name, xn1.name), writes=(ps_p.name,))
                    p.act(prj[:, :], ps_p[:, :], AF.Identity, bias=bin_t[:, bc:bc + 1],
                          reads=(ps_p.name, bin_t.name), writes=(prj.name,))
                    p.mm_group([(ps_r[:, :], permT[:, :], prj[:, :], True, True)],
                               reads=(permT.name, prj.name), writes=(ps_r.name,))
                    p.v("tensor_mul", sc_1[:, :], prj[:, :], rop[:, t0:t0 + 512],
                        reads=(prj.name, rop.name), writes=(sc_1.name,))
                    p.v("tensor_mul", sc_2[:, :], ps_r[:, :], rop[:, NTOK + t0:NTOK + t0 + 512],
                        reads=(ps_r.name, rop.name), writes=(sc_2.name,))
                    p.v("tensor_add", qk_t[:, :], sc_1[:, :], sc_2[:, :],
                        reads=(sc_1.name, sc_2.name), writes=(qk_t.name,))
                    p.dma("gpsimd", out_d[oc * 128:(oc + 1) * 128, t0:t0 + 512],
                          qk_t[:, :], reads=(qk_t.name,))
        vb_t = alloc(es3, nc, "vb_t", [128, D], F32)
        bvr = alloc(es3, nc, "bvr", [1, D], F32)
        p.dma("sync", bvr[:, :], bv_row_d, writes=(bvr.name,))
        ps_vb = palloc(es3, nc, "ps_vb", [128, 512])
        for hh in range(2):
            p.mm_group([(ps_vb[:, :], cst.ones_row()[:, :], bvr[:, hh * 512:(hh + 1) * 512],
                         True, True)], reads=(cst.ones_row().name, bvr.name),
                       writes=(ps_vb.name,))
            p.v("tensor_copy", vb_t[:, hh * 512:(hh + 1) * 512], ps_vb[:, :],
                reads=(ps_vb.name,), writes=(vb_t.name,))
        va_t = alloc(es3, nc, "va_t", [128, 1040], BF16)
        p.v("memset", va_t[:, :], 1.0, writes=(va_t.name,))
        ps_v = palloc(es3, nc, "ps_v", [128, 1024])
        for tc in range(NTOK // 128):
            t0 = tc * 128
            for hh in range(2):
                mm = [(ps_v[:, hh * 512:(hh + 1) * 512],
                       fmslice(xn1, NTOK, fc, t0, 128),
                       win_t[:, fc * 3072 + 2048 + hh * 512: fc * 3072 + 2048 + (hh + 1) * 512],
                       fc == 0, fc == NCH - 1) for fc in range(NCH)]
                p.mm_group(mm, reads=(xn1.name, win_t.name), writes=(ps_v.name,))
            p.v("tensor_add",
                va_t[:, :].rearrange("p (h e) -> p h e", e=65)[:, :, 0:64],
                ps_v[:, :].rearrange("p (h d) -> p h d", d=64),
                vb_t[:, :].rearrange("p (h d) -> p h d", d=64),
                reads=(ps_v.name, vb_t.name), writes=(va_t.name,))
            p.dma("gpsimd", v1_d[t0:t0 + 128, :], va_t[:, :], reads=(va_t.name,))
    p.barrier()

    with ExitStack() as es4:
        wstg2 = WBatch(p, es4, "eproj", 4 * D, double=False)
        pe_t = alloc(es4, nc, "pe_t", [64, 16 * NTOK], BF16)
        w_t = alloc(es4, nc, "w_t", [128, NCH * D], BF16)
        macc = alloc(es4, nc, "macc", [64, NTOK], F32)
        tmix = alloc(es4, nc, "tmix", [64, 512], F32)
        mix_b = alloc(es4, nc, "mix_b", [64, NTOK], BF16)
        vtile = alloc(es4, nc, "vtile", [128, 65], BF16)
        p.v("memset", vtile[:, :], 1.0, writes=(vtile.name,))
        ps_e = palloc(es4, nc, "ps_e", [128, 512])
        ps_rbm = palloc(es4, nc, "ps_rbm", [64, 512])
        ps_vt = palloc(es4, nc, "ps_vt", [128, 64], BF16)
        for (pi, w_d, out_d) in ((0, qall_d, qm_d), (1, kall_d, km_d), (2, vall_d, vm_d)):
            wstg2.load(w_t, 0, w_d[0:512, 0:D], 512, D)
            wstg2.load(w_t, 4 * D, w_d[512:D, 0:D], 512, D)
            for e in range(16):
                for tt in range(NTOK // 512):
                    t0 = tt * 512
                    mm = [(ps_e[0:64, :], w_t[:, fc * D + e * 64: fc * D + (e + 1) * 64],
                           fmslice(xn1, NTOK, fc, t0, 512), fc == 0, fc == NCH - 1)
                          for fc in range(NCH)]
                    p.mm_group(mm, reads=(w_t.name, xn1.name), writes=(ps_e.name,))
                    p.v("tensor_copy", pe_t[:, e * NTOK + t0: e * NTOK + t0 + 512],
                        ps_e[0:64, :], reads=(ps_e.name,), writes=(pe_t.name,))
            for tt in range(NTOK // 512):
                t0 = tt * 512
                for e in range(16):
                    p.mm_group([(ps_rbm[:, :], ehot[:, e * 64:(e + 1) * 64],
                                 rwb_fm[:, t0:t0 + 512], True, True)],
                               reads=(ehot.name, rwb_fm.name), writes=(ps_rbm.name,))
                    src = pe_t[:, e * NTOK + t0: e * NTOK + t0 + 512]
                    if e == 0:
                        p.v("tensor_mul", macc[:, t0:t0 + 512], src, ps_rbm[:, :],
                            reads=(pe_t.name, ps_rbm.name), writes=(macc.name,))
                    else:
                        p.v("tensor_mul", tmix[:, :], src, ps_rbm[:, :],
                            reads=(pe_t.name, ps_rbm.name), writes=(tmix.name,))
                        p.v("tensor_add", macc[:, t0:t0 + 512], macc[:, t0:t0 + 512],
                            tmix[:, :], reads=(macc.name, tmix.name), writes=(macc.name,))
            if pi == 0:
                p.v("tensor_scalar_mul", mix_b[:, :], macc[:, :], 0.125,
                    reads=(macc.name,), writes=(mix_b.name,))
            else:
                p.v("tensor_copy", mix_b[:, :], macc[:, :],
                    reads=(macc.name,), writes=(mix_b.name,))
            if pi < 2:
                for hh in range(NTOK // 512):
                    p.dma("gpsimd", out_d[:, hh * 512:(hh + 1) * 512],
                          mix_b[:, hh * 512:(hh + 1) * 512], reads=(mix_b.name,))
            else:
                for tc in range(NTOK // 128):
                    t0 = tc * 128
                    p.trans(ps_vt[:, :], mix_b[:, t0:t0 + 128],
                            cst.ident_bf()[0:64, 0:64],
                            reads=(mix_b.name, cst.ident_bf().name), writes=(ps_vt.name,))
                    p.v("tensor_copy", vtile[:, 0:64], ps_vt[:, 0:64],
                        reads=(ps_vt.name,), writes=(vtile.name,))
                    p.dma("gpsimd", out_d[t0:t0 + 128, :], vtile[:, :],
                          reads=(vtile.name,))
    p.barrier()
    return nc, p



def build_l1f():
    nc = bass.Bass("TRN2", target_bir_lowering=False, debug=False)
    S = 8192
    kh_d = nc.dram_tensor("kh", [256, S], BF16, kind="ExternalInput").ap()
    qh_d = nc.dram_tensor("qh", [256, S], BF16, kind="ExternalInput").ap()
    vh_d = nc.dram_tensor("vh", [S, 4 * 65], BF16, kind="ExternalInput").ap()
    kmall_d = nc.dram_tensor("kmall", [64, S], BF16, kind="ExternalInput").ap()
    vmall_d = nc.dram_tensor("vmall", [S, 65], BF16, kind="ExternalInput").ap()
    qm_d = nc.dram_tensor("qm", [64, NTOK], BF16, kind="ExternalInput").ap()
    mstrips_d = nc.dram_tensor("mstrips", [2, 8, 128, 1920], BF16, kind="ExternalInput").ap()

    ao_d = nc.dram_tensor("ao", [256, S], BF16, kind="ExternalOutput").ap()
    fao_d = nc.dram_tensor("fao", [64, NTOK], BF16, kind="ExternalOutput").ap()

    p = Prog(nc)
    es0 = ExitStack()
    cst = Consts(p, es0)
    cst.ones_col(); cst.ones_row()
    parc = np.zeros((128, 2), np.float32)
    parc[0:64, 0] = 1.0; parc[64:128, 1] = 1.0
    parcol = cst.get("parcol", parc)
    mast = np.full((128, 896), -30000.0, np.float32)
    for kl in range(128):
        mast[kl, kl + 384:] = 0.0
    master = cst.get("master", mast)

    kh_t = alloc(es0, nc, "kh_t", [128, 2 * S], BF16)
    qh_t = alloc(es0, nc, "qh_t", [128, 2 * S], BF16)
    vh_t = alloc(es0, nc, "vh_t", [128, (S // 128) * 260], BF16)
    for hp in range(2):
        p.dma("sync", kh_t[:, hp * S:(hp + 1) * S], kh_d[hp * 128:(hp + 1) * 128, :],
              writes=(kh_t.name,))
        p.dma("sync", qh_t[:, hp * S:(hp + 1) * S], qh_d[hp * 128:(hp + 1) * 128, :],
              writes=(qh_t.name,))
    for tc in range(S // 128):
        p.dma("sync", vh_t[:, tc * 260:(tc + 1) * 260], vh_d[tc * 128:(tc + 1) * 128, :],
              writes=(vh_t.name,))

    bias_col = alloc(es0, nc, "bias_col", [128, 4], F32)
    with ExitStack() as esb:
        sqb = alloc(esb, nc, "sqb", [128, S], F32)
        nrow = alloc(esb, nc, "nrow", [1, S], F32)
        mx = alloc(esb, nc, "mx", [1, 8], F32)
        ps_nb = palloc(esb, nc, "ps_nb", [1, 512])
        ps_bc = palloc(esb, nc, "ps_bc", [128, 4])
        for (src_t, off) in ((kh_t, 0), (qh_t, 4)):
            for hp in range(2):
                p.act(sqb[:, :], src_t[:, hp * S:(hp + 1) * S], AF.Square,
                      reads=(src_t.name,), writes=(sqb.name,))
                for par in range(2):
                    hb = hp * 2 + par
                    for ss in range(S // 512):
                        p.mm_group([(ps_nb[:, :], parcol[:, par:par + 1],
                                     sqb[:, ss * 512:(ss + 1) * 512], True, True)],
                                   reads=(parcol.name, sqb.name), writes=(ps_nb.name,))
                        if ss == 0:
                            p.v("reduce_max", mx[:, off + hb:off + hb + 1], ps_nb[:, :],
                                axis=AX.X, reads=(ps_nb.name,), writes=(mx.name,))
                        else:
                            p.v("reduce_max", nrow[:, 0:1], ps_nb[:, :], axis=AX.X,
                                reads=(ps_nb.name,), writes=(nrow.name,))
                            p.v("tensor_max", mx[:, off + hb:off + hb + 1],
                                mx[:, off + hb:off + hb + 1], nrow[:, 0:1],
                                reads=(mx.name, nrow.name), writes=(mx.name,))
        p.v("tensor_mul", mx[:, 0:4], mx[:, 0:4], mx[:, 4:8],
            reads=(mx.name,), writes=(mx.name,))
        p.act(mx[:, 0:4], mx[:, 0:4], AF.Sqrt, reads=(mx.name,), writes=(mx.name,))
        p.v("tensor_scalar_mul", mx[:, 0:4], mx[:, 0:4], -1.0,
            reads=(mx.name,), writes=(mx.name,))
        p.mm_group([(ps_bc[:, :], cst.ones_row()[:, :], mx[:, 0:4], True, True)],
                   reads=(cst.ones_row().name, mx.name), writes=(ps_bc.name,))
        p.v("tensor_copy", bias_col[:, :], ps_bc[:, :],
            reads=(ps_bc.name,), writes=(bias_col.name,))
    p.barrier()

    with ExitStack() as es1:
        pt = [alloc(es1, nc, f"pt{i}", [128, 512], BF16) for i in range(3)]
        sc64 = alloc(es1, nc, "sc64", [64, 512], F32)
        drow = alloc(es1, nc, "drow", [1, 512], F32)
        aot = alloc(es1, nc, "aot", [64, 512], BF16)
        ps_s = [palloc(es1, nc, f"ps_s{i}", [128, 512]) for i in range(3)]
        ps_o = palloc(es1, nc, "ps_o", [65, 512])
        ps_b = palloc(es1, nc, "ps_b", [64, 512])

        def emit_qk(hb, qt, j):
            hp, par = hb // 2, hb % 2
            r0 = 64 * par
            q_ap = qh_t[r0:r0 + 64, hp * S + qt * 512: hp * S + (qt + 1) * 512]
            k_ap = kh_t[r0:r0 + 64, hp * S + j * 128: hp * S + (j + 1) * 128]
            psx = ps_s[j % 3]
            p.mm_group([(psx[:, :], k_ap, q_ap, True, True)],
                       reads=(kh_t.name, qh_t.name), writes=(psx.name,))

        def emit_softmax_pv(hb, qt, j, nchunks):
            psx = ps_s[j % 3]
            ptx = pt[j % 3]
            jp = j - 4 * qt
            if jp >= 0:
                off = 384 - 128 * jp
                p.v("tensor_add", psx[:, :], psx[:, :], master[:, off:off + 512],
                    reads=(psx.name, master.name), writes=(psx.name,))
            p.act(ptx[:, :], psx[:, :], AF.Exp, bias=bias_col[:, hb:hb + 1],
                  reads=(psx.name, bias_col.name), writes=(ptx.name,))
            p.mm_group([(ps_o[:, :], vh_t[:, j * 260 + 65 * hb: j * 260 + 65 * hb + 65],
                         ptx[:, :], j == 0, j == nchunks - 1)],
                       reads=(vh_t.name, ptx.name), writes=(ps_o.name,))

        for hb in range(4):
            for qt in range(16):
                nchunks = 4 * qt + 4
                emit_qk(hb, qt, 0)
                if nchunks > 1:
                    emit_qk(hb, qt, 1)
                for j in range(nchunks):
                    if j + 2 < nchunks:
                        emit_qk(hb, qt, j + 2)
                    emit_softmax_pv(hb, qt, j, nchunks)
                p.v("reciprocal", drow[:, :], ps_o[64:65, :],
                    reads=(ps_o.name,), writes=(drow.name,))
                p.mm_group([(ps_b[:, :], cst.ones_row()[:, 0:64], drow[:, :], True, True)],
                           reads=(cst.ones_row().name, drow.name), writes=(ps_b.name,))
                p.v("tensor_copy", sc64[:, :], ps_o[0:64, :],
                    reads=(ps_o.name,), writes=(sc64.name,))
                p.v("tensor_mul", aot[:, :], sc64[:, :], ps_b[:, :],
                    reads=(sc64.name, ps_b.name), writes=(aot.name,))
                p.dma("gpsimd", ao_d[hb * 64:(hb + 1) * 64, qt * 512:(qt + 1) * 512],
                      aot[:, :], reads=(aot.name,))
    p.barrier()

    with ExitStack() as es2:
        km_t = alloc(es2, nc, "km_t", [64, S], BF16)
        vm_t = alloc(es2, nc, "vm_t", [128, (S // 128) * 65], BF16)
        qm_t = alloc(es2, nc, "qm_t", [64, NTOK], BF16)
        p.dma("sync", km_t[:, :], kmall_d, writes=(km_t.name,))
        p.dma("sync", qm_t[:, :], qm_d, writes=(qm_t.name,))
        for tc in range(S // 128):
            p.dma("sync", vm_t[:, tc * 65:(tc + 1) * 65], vmall_d[tc * 128:(tc + 1) * 128, :],
                  writes=(vm_t.name,))
        fb_col = alloc(es2, nc, "fb_col", [128, 1], F32)
        with ExitStack() as esb2:
            sqf = alloc(esb2, nc, "sqf", [64, S], F32)
            nrow2 = alloc(esb2, nc, "nrow2", [1, S], F32)
            mx2 = alloc(esb2, nc, "mx2", [1, 2], F32)
            ps_n2 = palloc(esb2, nc, "ps_n2", [1, 512])
            ps_b2 = palloc(esb2, nc, "ps_b2", [128, 1])
            for (src_t, W, idx) in ((km_t, S, 0), (qm_t, NTOK, 1)):
                p.act(sqf[:, 0:W], src_t[:, :], AF.Square,
                      reads=(src_t.name,), writes=(sqf.name,))
                for ss in range(W // 512):
                    p.mm_group([(ps_n2[:, :], cst.ones_col()[0:64, :],
                                 sqf[:, ss * 512:(ss + 1) * 512], True, True)],
                               reads=(cst.ones_col().name, sqf.name), writes=(ps_n2.name,))
                    if ss == 0:
                        p.v("reduce_max", mx2[:, idx:idx + 1], ps_n2[:, :], axis=AX.X,
                            reads=(ps_n2.name,), writes=(mx2.name,))
                    else:
                        p.v("reduce_max", nrow2[:, 0:1], ps_n2[:, :], axis=AX.X,
                            reads=(ps_n2.name,), writes=(nrow2.name,))
                        p.v("tensor_max", mx2[:, idx:idx + 1], mx2[:, idx:idx + 1],
                            nrow2[:, 0:1], reads=(mx2.name, nrow2.name), writes=(mx2.name,))
            p.v("tensor_mul", mx2[:, 0:1], mx2[:, 0:1], mx2[:, 1:2],
                reads=(mx2.name,), writes=(mx2.name,))
            p.act(mx2[:, 0:1], mx2[:, 0:1], AF.Sqrt, reads=(mx2.name,), writes=(mx2.name,))
            p.v("tensor_scalar_mul", mx2[:, 0:1], mx2[:, 0:1], -1.0,
                reads=(mx2.name,), writes=(mx2.name,))
            p.mm_group([(ps_b2[:, :], cst.ones_row()[:, :], mx2[:, 0:1], True, True)],
                       reads=(cst.ones_row().name, mx2.name), writes=(ps_b2.name,))
            p.v("tensor_copy", fb_col[:, :], ps_b2[:, :],
                reads=(ps_b2.name,), writes=(fb_col.name,))
        p.barrier()
        strip = [alloc(es2, nc, f"strip{i}", [128, 1920], BF16) for i in range(2)]
        ptf = [alloc(es2, nc, f"ptf{i}", [128, 512], BF16) for i in range(3)]
        sc64f = alloc(es2, nc, "sc64f", [64, 512], F32)
        drowf = alloc(es2, nc, "drowf", [1, 512], F32)
        faot = alloc(es2, nc, "faot", [64, 512], BF16)
        ps_sf = [palloc(es2, nc, f"ps_sf{i}", [128, 512]) for i in range(3)]
        ps_of = palloc(es2, nc, "ps_of", [65, 512])
        ps_bf = palloc(es2, nc, "ps_bf", [64, 512])
        for ob in range(2):
            for qt in range(2):
                q_ap = qm_t[:, ob * BLK + qt * 512: ob * BLK + (qt + 1) * 512]

                def emit_qk_f(idx):
                    s, kc = idx // 8, idx % 8
                    if kc == 0:
                        p.dma("gpsimd", strip[s % 2][:, :], mstrips_d[ob, s],
                              writes=(strip[s % 2].name,))
                    k_ap = km_t[:, s * 1024 + kc * 128: s * 1024 + (kc + 1) * 128]
                    psx = ps_sf[idx % 3]
                    p.mm_group([(psx[:, :], k_ap, q_ap, True, True)],
                               reads=(km_t.name, qm_t.name), writes=(psx.name,))

                def emit_sm_pv_f(idx):
                    s, kc = idx // 8, idx % 8
                    psx = ps_sf[idx % 3]
                    ptx = ptf[idx % 3]
                    off = 896 - 128 * kc + 512 * qt
                    p.v("tensor_add", psx[:, :], psx[:, :],
                        strip[s % 2][:, off:off + 512],
                        reads=(psx.name, strip[s % 2].name), writes=(psx.name,))
                    p.act(ptx[:, :], psx[:, :], AF.Exp, bias=fb_col[:, 0:1],
                          reads=(psx.name, fb_col.name), writes=(ptx.name,))
                    tc = idx
                    p.mm_group([(ps_of[:, :], vm_t[:, tc * 65:(tc + 1) * 65],
                                 ptx[:, :], idx == 0, idx == 63)],
                               reads=(vm_t.name, ptx.name), writes=(ps_of.name,))

                emit_qk_f(0)
                emit_qk_f(1)
                for idx in range(64):
                    if idx + 2 < 64:
                        emit_qk_f(idx + 2)
                    emit_sm_pv_f(idx)
                p.v("reciprocal", drowf[:, :], ps_of[64:65, :],
                    reads=(ps_of.name,), writes=(drowf.name,))
                p.mm_group([(ps_bf[:, :], cst.ones_row()[:, 0:64], drowf[:, :], True, True)],
                           reads=(cst.ones_row().name, drowf.name), writes=(ps_bf.name,))
                p.v("tensor_copy", sc64f[:, :], ps_of[0:64, :],
                    reads=(ps_of.name,), writes=(sc64f.name,))
                p.v("tensor_mul", faot[:, :], sc64f[:, :], ps_bf[:, :],
                    reads=(sc64f.name, ps_bf.name), writes=(faot.name,))
                p.dma("gpsimd", fao_d[:, ob * BLK + qt * 512: ob * BLK + (qt + 1) * 512],
                      faot[:, :], reads=(faot.name,))
    p.barrier()
    return nc, p



def build_l1c():
    nc = bass.Bass("TRN2", target_bir_lowering=False, debug=False)
    x_d = nc.dram_tensor("x", [D, NTOK], F32, kind="ExternalInput").ap()
    aoh_d = nc.dram_tensor("aoh", [D, NTOK], BF16, kind="ExternalInput").ap()
    fao_d = nc.dram_tensor("fao", [64, NTOK], BF16, kind="ExternalInput").ap()
    rw_d = nc.dram_tensor("rw", [16, NTOK], F32, kind="ExternalInput").ap()
    oall_d = nc.dram_tensor("oall", [D, D], F32, kind="ExternalInput").ap()
    woutT_d = nc.dram_tensor("woutT", [D, D], F32, kind="ExternalInput").ap()
    bout_d = nc.dram_tensor("bout", [D, 1], F32, kind="ExternalInput").ap()
    wq2_d = nc.dram_tensor("wq2", [D, 512], F32, kind="ExternalInput").ap()
    wk2_d = nc.dram_tensor("wk2", [D, 512], F32, kind="ExternalInput").ap()
    wv2_d = nc.dram_tensor("wv2", [D, 512], F32, kind="ExternalInput").ap()
    gateW_d = nc.dram_tensor("gateW", [D, 8], F32, kind="ExternalInput").ap()
    x2_d = nc.dram_tensor("x2", [D, NTOK], F32, kind="ExternalOutput").ap()
    part2_d = nc.dram_tensor("part2", [512, 66], F32, kind="ExternalOutput").ap()
    qsm_d = nc.dram_tensor("qsm", [512, NTOK], BF16, kind="ExternalOutput").ap()
    xn2_d = nc.dram_tensor("xn2", [D, NTOK], BF16, kind="ExternalOutput").ap()
    comb2_d = nc.dram_tensor("comb2", [8, NTOK], BF16, kind="ExternalOutput").ap()

    p = Prog(nc)
    es0 = ExitStack()
    cst = Consts(p, es0)
    cst.ones_col(); cst.ones_row(); cst.ident_bf(); cst.ident_f32()
    cst.colv(1e-5, 1)
    ehot_np = np.zeros((16, 16 * 64), np.float32)
    for e in range(16):
        ehot_np[e, e * 64:(e + 1) * 64] = 1.0
    ehot = cst.get("ehot16", ehot_np, BF16)
    ehot8_np = np.zeros((8, 8 * 128), np.float32)
    for e in range(8):
        ehot8_np[e, e * 128:(e + 1) * 128] = 1.0
    ehot8 = cst.get("ehot8", ehot8_np, BF16)

    xn2 = alloc(es0, nc, "xn2s", [128, NCH * NTOK], BF16)
    mrow2 = alloc(es0, nc, "mrow2", [1, NTOK], F32)
    rrow2 = alloc(es0, nc, "rrow2", [1, NTOK], F32)
    combine_bf = alloc(es0, nc, "combine_bf", [8, NTOK], BF16)

    es_mid = ExitStack()
    x_t = alloc(es_mid, nc, "x_t", [128, NCH * NTOK], F32)
    emit_fm_load(p, x_t, x_d, NTOK)
    with ExitStack() as es1:
        aoh_t = alloc(es1, nc, "aoh_t", [128, NCH * NTOK], BF16)
        emit_fm_load(p, aoh_t, aoh_d, NTOK)
        fao_t = alloc(es1, nc, "fao_t", [128, NTOK], BF16)
        p.dma("sync", fao_t[0:64, :], fao_d, writes=(fao_t.name,))
        p.dma("sync", fao_t[64:128, :], fao_d, writes=(fao_t.name,))
        rwb = alloc(es1, nc, "rwb", [16, NTOK], BF16)
        rwst = alloc(es1, nc, "rwst", [16, 512], F32)
        for tt in range(NTOK // 512):
            p.dma("sync", rwst[:, :], rw_d[:, tt * 512:(tt + 1) * 512],
                  writes=(rwst.name,))
            p.v("tensor_copy", rwb[:, tt * 512:(tt + 1) * 512], rwst[:, :],
                reads=(rwst.name,), writes=(rwb.name,))
        bout_t = alloc(es1, nc, "bout_t", [128, 8], F32)
        for fc in range(NCH):
            p.dma("sync", bout_t[:, fc:fc + 1], bout_d[fc * 128:(fc + 1) * 128, :],
                  writes=(bout_t.name,))
        wstg = WBatch(p, es1, "wout", NCH * 128, double=False)
        wout_c = alloc(es1, nc, "wout_c", [128, NCH * 128], BF16)
        oall_c = alloc(es1, nc, "oall_c", [128, NCH * 128], BF16)
        a_t = alloc(es1, nc, "a_t", [128, 8 * NTOK], BF16)
        ps_rb = palloc(es1, nc, "ps_rb", [128, 512])
        ps_s1 = palloc(es1, nc, "ps_s1", [128, 512])
        ps_g1 = palloc(es1, nc, "ps_g1", [128, 512])
        sc_s = alloc(es1, nc, "sc_s", [128, 512], F32)
        sc_g = alloc(es1, nc, "sc_g", [128, 512], F32)
        for tt in range(NTOK // 512):
            t0 = tt * 512
            for e in range(16):
                r0 = 64 * (e % 2)
                p.mm_group([(ps_rb[r0:r0 + 64, :], ehot[:, e * 64:(e + 1) * 64],
                             rwb[:, t0:t0 + 512], True, True)],
                           reads=(ehot.name, rwb.name), writes=(ps_rb.name,))
                asl = a_t[r0:r0 + 64, (e // 2) * NTOK + t0:(e // 2) * NTOK + t0 + 512]
                p.v("tensor_mul", asl,
                    fao_t[r0:r0 + 64, t0:t0 + 512], ps_rb[r0:r0 + 64, :],
                    reads=(fao_t.name, ps_rb.name), writes=(a_t.name,))
        for oc in range(NCH):
            wstg.load(wout_c, 0, woutT_d[0:D, oc * 128:(oc + 1) * 128], D, 128)
            wstg.load(oall_c, 0, oall_d[0:D, oc * 128:(oc + 1) * 128], D, 128)
            for tt in range(NTOK // 512):
                t0 = tt * 512
                mm = []
                for e in range(0, 16, 2):
                    mm.append((ps_g1[:, :],
                               oall_c[0:64, (e // 2) * 128:(e // 2) * 128 + 128],
                               a_t[0:64, (e // 2) * NTOK + t0:(e // 2) * NTOK + t0 + 512],
                               e == 0, False))
                for fc in range(NCH):
                    mm.append((ps_s1[:, :], wout_c[:, fc * 128:(fc + 1) * 128],
                               fmslice(aoh_t, NTOK, fc, t0, 512), fc == 0, fc == NCH - 1))
                for e in range(1, 16, 2):
                    mm.append((ps_g1[:, :],
                               oall_c[64:128, (e // 2) * 128:(e // 2) * 128 + 128],
                               a_t[64:128, (e // 2) * NTOK + t0:(e // 2) * NTOK + t0 + 512],
                               False, e == 15))
                p.mm_group(mm, reads=(oall_c.name, a_t.name, wout_c.name, aoh_t.name),
                           writes=(ps_g1.name, ps_s1.name))
                p.act(sc_g[:, :], ps_g1[:, :], AF.Silu,
                      reads=(ps_g1.name,), writes=(sc_g.name,))
                p.act(sc_s[:, :], ps_s1[:, :], AF.Identity, bias=bout_t[:, oc:oc + 1],
                      reads=(ps_s1.name, bout_t.name), writes=(sc_s.name,))
                p.v("tensor_mul", sc_s[:, :], sc_s[:, :], sc_g[:, :],
                    reads=(sc_s.name, sc_g.name), writes=(sc_s.name,))
                xs = fmslice(x_t, NTOK, oc, t0, 512)
                p.v("tensor_add", xs, xs, sc_s[:, :],
                    reads=(x_t.name, sc_s.name), writes=(x_t.name,))
        emit_fm_store(p, x2_d, x_t, NTOK)
    p.barrier()
    emit_ln(p, None, cst, x_t, xn2, NTOK, tag="2", stats=(mrow2, None, rrow2))

    with ExitStack() as esg:
        emit_topk_gating(p, esg, cst, nc, x_t, mrow2, rrow2, gateW_d, 8,
                         combine_bf, mode="l2")
    es_mid.close()
    p.barrier()

    with ExitStack() as es2:
        wstg = WBatch(p, es2, "wqkv2", NCH * 512, double=False)
        wq2_t = load_w_fm(p, es2, cst, wq2_d, "wq2", D, 512, stager=wstg)
        wk2_t = load_w_fm(p, es2, cst, wk2_d, "wk2", D, 512, stager=wstg)
        wv2_t = load_w_fm(p, es2, cst, wv2_d, "wv2", D, 512, stager=wstg)
        qsm_fm = alloc(es2, nc, "qsm_fm", [128, 4 * NTOK], BF16)
        qe = alloc(es2, nc, "qe", [128, 512], F32)
        qsm_t = alloc(es2, nc, "qsm_t", [128, 512], BF16)
        ncol = alloc(es2, nc, "ncol", [128, 2], F32)
        ps_q = palloc(es2, nc, "ps_q", [128, 512])
        ps_tq = palloc(es2, nc, "ps_tq", [128, 128], BF16)
        for tc in range(NTOK // 128):
            t0 = tc * 128
            mm = [(ps_q[:, :], fmslice(xn2, NTOK, fc, t0, 128),
                   wq2_t[:, fc * 512:(fc + 1) * 512], fc == 0, fc == NCH - 1)
                  for fc in range(NCH)]
            p.mm_group(mm, reads=(xn2.name, wq2_t.name), writes=(ps_q.name,))
            for h in range(8):
                hsl = slice(h * 64, (h + 1) * 64)
                p.v("reduce_max", ncol[:, 0:1], ps_q[:, hsl], axis=AX.X, negate=True,
                    reads=(ps_q.name,), writes=(ncol.name,))
                p.act(qe[:, hsl], ps_q[:, hsl], AF.Exp, bias=ncol[:, 0:1],
                      accum_out=ncol[:, 1:2],
                      reads=(ps_q.name, ncol.name), writes=(qe.name, ncol.name))
                p.v("reciprocal", ncol[:, 1:2], ncol[:, 1:2],
                    reads=(ncol.name,), writes=(ncol.name,))
                p.v("tensor_scalar", qsm_t[:, hsl], qe[:, hsl], ncol[:, 1:2], 0.125,
                    op0=ALU.mult, op1=ALU.mult,
                    reads=(qe.name, ncol.name), writes=(qsm_t.name,))
            for h in range(8):
                r0 = 64 * (h % 2)
                out_ps = ps_tq[r0:r0 + 64, 0:128]
                p.trans(out_ps, qsm_t[:, h * 64:(h + 1) * 64], cst.ident_bf()[:, :],
                        reads=(qsm_t.name, cst.ident_bf().name), writes=(ps_tq.name,))
                p.v("tensor_copy",
                    qsm_fm[r0:r0 + 64, (h // 2) * NTOK + t0:(h // 2) * NTOK + t0 + 128],
                    out_ps, reads=(ps_tq.name,), writes=(qsm_fm.name,))
        for c4 in range(4):
            p.dma("gpsimd", qsm_d[c4 * 128:(c4 + 1) * 128, :],
                  qsm_fm[:, c4 * NTOK:(c4 + 1) * NTOK], reads=(qsm_fm.name,))
        k_t = alloc(es2, nc, "k_t", [128, 4 * NTOK], F32)
        ekbf = alloc(es2, nc, "ekbf", [128, 4 * NTOK], BF16)
        mlocs = alloc(es2, nc, "mlocs", [128, 4], F32)
        zlocs = alloc(es2, nc, "zlocs", [128, 4], F32)
        negm = alloc(es2, nc, "negm", [128, 1], F32)
        ps_k = palloc(es2, nc, "ps_k", [128, 512])
        for c4 in range(4):
            for tt in range(NTOK // 512):
                t0 = tt * 512
                mm = [(ps_k[:, :], fmslice(xn2, NTOK, fc, t0, 512),
                       wk2_t[:, fc * 512 + c4 * 128: fc * 512 + (c4 + 1) * 128],
                       fc == 0, fc == NCH - 1) for fc in range(NCH)]
                mm = [(ps_k[:, :], wk2_t[:, fc * 512 + c4 * 128: fc * 512 + (c4 + 1) * 128],
                       fmslice(xn2, NTOK, fc, t0, 512), fc == 0, fc == NCH - 1)
                      for fc in range(NCH)]
                p.mm_group(mm, reads=(wk2_t.name, xn2.name), writes=(ps_k.name,))
                p.v("tensor_copy", k_t[:, c4 * NTOK + t0: c4 * NTOK + t0 + 512],
                    ps_k[:, :], reads=(ps_k.name,), writes=(k_t.name,))
            p.v("reduce_max", negm[:, 0:1], k_t[:, c4 * NTOK:(c4 + 1) * NTOK],
                axis=AX.X, negate=True, reads=(k_t.name,), writes=(negm.name,))
            p.v("tensor_scalar_mul", mlocs[:, c4:c4 + 1], negm[:, 0:1], -1.0,
                reads=(negm.name,), writes=(mlocs.name,))
            p.act(ekbf[:, c4 * NTOK:(c4 + 1) * NTOK], k_t[:, c4 * NTOK:(c4 + 1) * NTOK],
                  AF.Exp, bias=negm[:, 0:1], accum_out=zlocs[:, c4:c4 + 1],
                  reads=(k_t.name, negm.name), writes=(ekbf.name, zlocs.name))
        vtm = alloc(es2, nc, "vtm", [128, (NTOK // 128) * 512], BF16)
        ps_v = palloc(es2, nc, "ps_v", [128, 512])
        for tc in range(NTOK // 128):
            t0 = tc * 128
            mm = [(ps_v[:, :], fmslice(xn2, NTOK, fc, t0, 128),
                   wv2_t[:, fc * 512:(fc + 1) * 512], fc == 0, fc == NCH - 1)
                  for fc in range(NCH)]
            p.mm_group(mm, reads=(xn2.name, wv2_t.name), writes=(ps_v.name,))
            p.v("tensor_copy", vtm[:, tc * 512:(tc + 1) * 512], ps_v[:, :],
                reads=(ps_v.name,), writes=(vtm.name,))
        ektm = alloc(es2, nc, "ektm", [128, (NTOK // 128) * 512], BF16)
        ps_te = palloc(es2, nc, "ps_te", [128, 128], BF16)
        for c4 in range(4):
            for tc in range(NTOK // 128):
                p.trans(ps_te[:, :], ekbf[:, c4 * NTOK + tc * 128: c4 * NTOK + (tc + 1) * 128],
                        cst.ident_bf()[:, :],
                        reads=(ekbf.name, cst.ident_bf().name), writes=(ps_te.name,))
                p.v("tensor_copy", ektm[:, tc * 512 + c4 * 128: tc * 512 + (c4 + 1) * 128],
                    ps_te[:, :], reads=(ps_te.name,), writes=(ektm.name,))
        ctx_t = alloc(es2, nc, "ctx_t", [64, 512], F32)
        ps_cx = palloc(es2, nc, "ps_cx", [64, 512])
        mm = []
        for h in range(8):
            for tc in range(NTOK // 128):
                mm.append((ps_cx[:, h * 64:(h + 1) * 64],
                           ektm[:, tc * 512 + h * 64: tc * 512 + (h + 1) * 64],
                           vtm[:, tc * 512 + h * 64: tc * 512 + (h + 1) * 64],
                           tc == 0, tc == NTOK // 128 - 1))
        p.mm_group(mm, reads=(ektm.name, vtm.name), writes=(ps_cx.name,))
        p.v("tensor_copy", ctx_t[:, :], ps_cx[:, :], reads=(ps_cx.name,), writes=(ctx_t.name,))
        for h in range(8):
            p.dma("gpsimd", part2_d[h * 64:(h + 1) * 64, 0:64],
                  ctx_t[:, h * 64:(h + 1) * 64], reads=(ctx_t.name,))
        for c4 in range(4):
            p.dma("gpsimd", part2_d[c4 * 128:(c4 + 1) * 128, 64:65],
                  mlocs[:, c4:c4 + 1], reads=(mlocs.name,),
                  allow_slow_non_contiguous=True)
            p.dma("gpsimd", part2_d[c4 * 128:(c4 + 1) * 128, 65:66],
                  zlocs[:, c4:c4 + 1], reads=(zlocs.name,),
                  allow_slow_non_contiguous=True)
    p.barrier()

    emit_fm_store(p, xn2_d, xn2, NTOK)
    for hh in range(NTOK // 512):
        p.dma("gpsimd", comb2_d[:, hh * 512:(hh + 1) * 512],
              combine_bf[:, hh * 512:(hh + 1) * 512], reads=(combine_bf.name,))
    p.barrier()
    return nc, p


def emit_topk_gating(p, es, cst, nc, x_t, mrow, rrow, gateW_d, E, out_bf, mode):
    gw_t = alloc(es, nc, "tg_gw", [128, NCH * E], F32)
    for fc in range(NCH):
        p.dma("sync", gw_t[:, fc * E:(fc + 1) * E],
              gateW_d[fc * 128:(fc + 1) * 128, :], writes=(gw_t.name,))
    negg = alloc(es, nc, "tg_negg", [1, E], F32)
    stE = alloc(es, nc, "tg_stE", [E + 1, NTOK], F32)
    tmE = alloc(es, nc, "tg_tmE", [128, E + 1], F32)
    cA = alloc(es, nc, "tg_cA", [128, 1], F32)
    cB = alloc(es, nc, "tg_cB", [128, 1], F32)
    cC = alloc(es, nc, "tg_cC", [128, 1], F32)
    tEa = alloc(es, nc, "tg_tEa", [128, E], F32)
    tEb = alloc(es, nc, "tg_tEb", [128, E], F32)
    eqa = alloc(es, nc, "tg_eqa", [128, E], F32)
    eqb = alloc(es, nc, "tg_eqb", [128, E], F32)
    outf = alloc(es, nc, "tg_outf", [E, NTOK], F32)
    esp = ExitStack()
    ps_E = palloc(esp, nc, "tg_psE", [E, 512])
    ps_e1 = palloc(esp, nc, "tg_pse1", [1, E])
    ps_tE = palloc(esp, nc, "tg_pstE", [128, E + 1])
    ps_bE = palloc(esp, nc, "tg_psbE", [E, 128])
    for fc in range(NCH):
        p.mm_group([(ps_e1[:, :], cst.ones_col()[:, :], gw_t[:, fc * E:(fc + 1) * E],
                     fc == 0, fc == NCH - 1)],
                   reads=(cst.ones_col().name, gw_t.name), writes=(ps_e1.name,))
    p.v("tensor_scalar_mul", negg[:, :], ps_e1[:, :], -1.0,
        reads=(ps_e1.name,), writes=(negg.name,))
    for tt in range(NTOK // 512):
        t0 = tt * 512
        mm = [(ps_E[:, :], gw_t[:, fc * E:(fc + 1) * E],
               fmslice(x_t, NTOK, fc, t0, 512), fc == 0, False)
              for fc in range(NCH)]
        mm.append((ps_E[:, :], negg[:, :], mrow[:, t0:t0 + 512], False, True))
        p.mm_group(mm, reads=(gw_t.name, x_t.name, negg.name, mrow.name),
                   writes=(ps_E.name,))
        p.v("tensor_copy", stE[0:E, t0:t0 + 512], ps_E[:, :],
            reads=(ps_E.name,), writes=(stE.name,))
    p.dma("gpsimd", stE[E:E + 1, :], rrow[:, :], reads=(rrow.name,), writes=(stE.name,))
    for tc in range(NTOK // 128):
        t0 = tc * 128
        p.trans(ps_tE[:, :], stE[:, t0:t0 + 128], cst.ident_f32()[0:E + 1, 0:E + 1],
                reads=(stE.name, cst.ident_f32().name), writes=(ps_tE.name,))
        p.v("tensor_copy", tmE[:, :], ps_tE[:, :], reads=(ps_tE.name,), writes=(tmE.name,))
        p.v("tensor_scalar_mul", tEa[:, :], tmE[:, 0:E], tmE[:, E:E + 1],
            reads=(tmE.name,), writes=(tEa.name,))
        if mode == "l2":
            p.v("reduce_max", cA[:, :], tEa[:, :], axis=AX.X, negate=True,
                reads=(tEa.name,), writes=(cA.name,))
            p.act(tEb[:, :], tEa[:, :], AF.Exp, bias=cA[:, :], accum_out=cB[:, :],
                  reads=(tEa.name, cA.name), writes=(tEb.name, cB.name))
            p.v("reciprocal", cB[:, :], cB[:, :], reads=(cB.name,), writes=(cB.name,))
            p.v("tensor_scalar_mul", tEb[:, :], tEb[:, :], cB[:, :],
                reads=(tEb.name, cB.name), writes=(tEb.name,))
            p.v("reduce_max", cA[:, :], tEb[:, :], axis=AX.X,
                reads=(tEb.name,), writes=(cA.name,))
            p.v("tensor_single_scalar", eqa[:, :], tEb[:, :], cA[:, :], op=ALU.is_equal,
                reads=(tEb.name, cA.name), writes=(eqa.name,))
            p.v("scalar_tensor_tensor", eqb[:, :], eqa[:, :], -1e30, tEb[:, :],
                op0=ALU.mult, op1=ALU.add, reads=(eqa.name, tEb.name), writes=(eqb.name,))
            p.v("reduce_max", cB[:, :], eqb[:, :], axis=AX.X,
                reads=(eqb.name,), writes=(cB.name,))
            p.v("tensor_single_scalar", eqb[:, :], eqb[:, :], cB[:, :], op=ALU.is_equal,
                reads=(eqb.name, cB.name), writes=(eqb.name,))
            p.v("tensor_add", cC[:, :], cA[:, :], cB[:, :],
                reads=(cA.name, cB.name), writes=(cC.name,))
            p.v("reciprocal", cC[:, :], cC[:, :], reads=(cC.name,), writes=(cC.name,))
            p.v("tensor_add", eqa[:, :], eqa[:, :], eqb[:, :],
                reads=(eqa.name, eqb.name), writes=(eqa.name,))
            p.v("tensor_mul", eqa[:, :], eqa[:, :], tEb[:, :],
                reads=(eqa.name, tEb.name), writes=(eqa.name,))
            p.v("tensor_scalar_mul", eqa[:, :], eqa[:, :], cC[:, :],
                reads=(eqa.name, cC.name), writes=(eqa.name,))
        else:
            p.v("reduce_max", cA[:, :], tEa[:, :], axis=AX.X,
                reads=(tEa.name,), writes=(cA.name,))
            p.v("tensor_single_scalar", eqa[:, :], tEa[:, :], cA[:, :], op=ALU.is_equal,
                reads=(tEa.name, cA.name), writes=(eqa.name,))
            p.v("scalar_tensor_tensor", eqb[:, :], eqa[:, :], -1e30, tEa[:, :],
                op0=ALU.mult, op1=ALU.add, reads=(eqa.name, tEa.name), writes=(eqb.name,))
            p.v("reduce_max", cB[:, :], eqb[:, :], axis=AX.X,
                reads=(eqb.name,), writes=(cB.name,))
            p.v("tensor_single_scalar", eqb[:, :], eqb[:, :], cB[:, :], op=ALU.is_equal,
                reads=(eqb.name, cB.name), writes=(eqb.name,))
            p.v("tensor_sub", cC[:, :], cA[:, :], cB[:, :],
                reads=(cA.name, cB.name), writes=(cC.name,))
            p.act(cC[:, :], cC[:, :], AF.Sigmoid, reads=(cC.name,), writes=(cC.name,))
            p.v("tensor_scalar_mul", eqa[:, :], eqa[:, :], cC[:, :],
                reads=(eqa.name, cC.name), writes=(eqa.name,))
            p.v("tensor_scalar", cC[:, :], cC[:, :], -1.0, 1.0, op0=ALU.mult, op1=ALU.add,
                reads=(cC.name,), writes=(cC.name,))
            p.v("tensor_scalar_mul", eqb[:, :], eqb[:, :], cC[:, :],
                reads=(eqb.name, cC.name), writes=(eqb.name,))
            p.v("tensor_add", eqa[:, :], eqa[:, :], eqb[:, :],
                reads=(eqa.name, eqb.name), writes=(eqa.name,))
        p.trans(ps_bE[:, :], eqa[:, :], cst.ident_f32()[:, :],
                reads=(eqa.name, cst.ident_f32().name), writes=(ps_bE.name,))
        p.v("tensor_copy", outf[:, t0:t0 + 128], ps_bE[:, :],
            reads=(ps_bE.name,), writes=(outf.name,))
    p.v("tensor_copy", out_bf[:, :], outf[:, :], reads=(outf.name,), writes=(out_bf.name,))
    esp.close()
    p.barrier()



def emit_moe_experts(p, es, cst, nc, xn, comb_bf, w1e_d, w2e_d, acc,
                     act_fn, b1col_t=None, E=8, HID=4096):
    NHC = HID // 128
    ehot8 = cst.get("ehot8", None) if False else cst.loaded.get("ehot8")
    st = WBatch(p, es, "moe", 16 * 128)
    w1t = alloc(es, nc, "moe_w1t", [128, NCH * 512], BF16)
    w2c = alloc(es, nc, "moe_w2c", [128, NHC * 128], BF16)
    h_t = alloc(es, nc, "moe_ht", [128, NHC * 1024], BF16)
    cbs = alloc(es, nc, "moe_cbs", [128, 1024], BF16)
    sc = alloc(es, nc, "moe_sc", [128, 512], F32)
    ps_h = palloc(es, nc, "moe_psh", [128, 512])
    ps_h2 = palloc(es, nc, "moe_psh2", [128, 512])
    ps_o = palloc(es, nc, "moe_pso", [128, 512])
    ps_o2 = palloc(es, nc, "moe_pso2", [128, 512])
    ps_cb = palloc(es, nc, "moe_pscb", [128, 512])
    qs = ["sync", "scalar"]
    for e in range(E):
        for tp in range(2):
            tbase = tp * 1024
            for tq in range(2):
                t0 = tbase + tq * 512
                p.mm_group([(ps_cb[:, :], ehot8[:, e * 128:(e + 1) * 128],
                             comb_bf[:, t0:t0 + 512], True, True)],
                           reads=(ehot8.name, comb_bf.name), writes=(ps_cb.name,))
                p.v("tensor_copy", cbs[:, tq * 512:(tq + 1) * 512], ps_cb[:, :],
                    reads=(ps_cb.name,), writes=(cbs.name,))
            for hg in range(HID // 512):
                st.load(w1t, 0, w1e_d[e, 0:512, hg * 512:(hg + 1) * 512], 512, 512,
                        eng=qs[hg % 2])
                st.load(w1t, 4 * 512, w1e_d[e, 512:D, hg * 512:(hg + 1) * 512],
                        512, 512, eng=qs[(hg + 1) % 2])
                for hc4 in range(4):
                    hc = hg * 4 + hc4
                    for tq in range(2):
                        t0 = tbase + tq * 512
                        psx = ps_h if (hc4 + tq) % 2 == 0 else ps_h2
                        mm = [(psx[:, :],
                               w1t[:, fc * 512 + hc4 * 128: fc * 512 + (hc4 + 1) * 128],
                               fmslice(xn, NTOK, fc, t0, 512), fc == 0, fc == NCH - 1)
                              for fc in range(NCH)]
                        p.mm_group(mm, reads=(w1t.name, xn.name), writes=(psx.name,))
                        if b1col_t is not None:
                            p.act(h_t[:, hc * 1024 + tq * 512: hc * 1024 + (tq + 1) * 512],
                                  psx[:, :], act_fn,
                                  bias=b1col_t[:, e * NHC + hc: e * NHC + hc + 1],
                                  reads=(psx.name, b1col_t.name), writes=(h_t.name,))
                        else:
                            p.act(h_t[:, hc * 1024 + tq * 512: hc * 1024 + (tq + 1) * 512],
                                  psx[:, :], act_fn,
                                  reads=(psx.name,), writes=(h_t.name,))
            for oc in range(NCH):
                st.load(w2c, 0, w2e_d[e, 0:2048, oc * 128:(oc + 1) * 128], 2048, 128,
                        eng=qs[oc % 2])
                st.load(w2c, 16 * 128, w2e_d[e, 2048:HID, oc * 128:(oc + 1) * 128],
                        2048, 128, eng=qs[(oc + 1) % 2])
                for tq in range(2):
                    t0 = tbase + tq * 512
                    psx = ps_o if (oc + tq) % 2 == 0 else ps_o2
                    mm = [(psx[:, :], w2c[:, hc * 128:(hc + 1) * 128],
                           h_t[:, hc * 1024 + tq * 512: hc * 1024 + (tq + 1) * 512],
                           hc == 0, hc == NHC - 1) for hc in range(NHC)]
                    p.mm_group(mm, reads=(w2c.name, h_t.name), writes=(psx.name,))
                    p.v("tensor_mul", sc[:, :], psx[:, :], cbs[:, tq * 512:(tq + 1) * 512],
                        reads=(psx.name, cbs.name), writes=(sc.name,))
                    accs = fmslice(acc, NTOK, oc, t0, 512)
                    p.v("tensor_add", accs, accs, sc[:, :],
                        reads=(acc.name + f":{oc}", sc.name),
                        writes=(acc.name + f":{oc}",))



def build_l2f():
    nc = bass.Bass("TRN2", target_bir_lowering=False, debug=False)
    xn2_d = nc.dram_tensor("xn2", [D, NTOK], BF16, kind="ExternalInput").ap()
    comb_d = nc.dram_tensor("comb", [8, NTOK], BF16, kind="ExternalInput").ap()
    sharedW_d = nc.dram_tensor("sharedW", [D, D], F32, kind="ExternalInput").ap()
    sharedb_d = nc.dram_tensor("sharedb", [D, 1], F32, kind="ExternalInput").ap()
    w1e_d = nc.dram_tensor("w1e", [8, D, 4096], F32, kind="ExternalInput").ap()
    b1c_d = nc.dram_tensor("b1c", [8 * 4096, 1], F32, kind="ExternalInput").ap()
    w2e_d = nc.dram_tensor("w2e", [8, 4096, D], F32, kind="ExternalInput").ap()
    b2r_d = nc.dram_tensor("b2r", [8, D], F32, kind="ExternalInput").ap()
    gamma_d = nc.dram_tensor("gamma", [D, 1], F32, kind="ExternalInput").ap()
    beta_d = nc.dram_tensor("beta", [D, 1], F32, kind="ExternalInput").ap()
    g2_d = nc.dram_tensor("g2", [D, NTOK], BF16, kind="ExternalOutput").ap()

    p = Prog(nc)
    es0 = ExitStack()
    cst = Consts(p, es0)
    cst.ones_col(); cst.ones_row()
    cst.colv(1e-5, 1)
    ehot8_np = np.zeros((8, 8 * 128), np.float32)
    for e in range(8):
        ehot8_np[e, e * 128:(e + 1) * 128] = 1.0
    cst.get("ehot8", ehot8_np, BF16)

    xn2 = alloc(es0, nc, "xn2s", [128, NCH * NTOK], BF16)
    emit_fm_load(p, xn2, xn2_d, NTOK)
    comb_bf = alloc(es0, nc, "comb_bf", [8, NTOK], BF16)
    p.dma("sync", comb_bf[:, :], comb_d, writes=(comb_bf.name,))
    acc = alloc(es0, nc, "acc", [128, NCH * NTOK], F32)

    with ExitStack() as es1:
        wstg = WBatch(p, es1, "shared", 4 * D, double=False)
        shw_t = load_w_fm(p, es1, cst, sharedW_d, "sharedW", D, D, stager=wstg)
        shb_t = alloc(es1, nc, "shb_t", [128, 8], F32)
        for fc in range(NCH):
            p.dma("sync", shb_t[:, fc:fc + 1], sharedb_d[fc * 128:(fc + 1) * 128, :],
                  writes=(shb_t.name,))
        b2f = alloc(es1, nc, "b2f", [8, D], F32)
        b2t = alloc(es1, nc, "b2t", [8, D], BF16)
        p.dma("sync", b2f[:, :], b2r_d, writes=(b2f.name,))
        p.v("tensor_copy", b2t[:, :], b2f[:, :], reads=(b2f.name,), writes=(b2t.name,))
        gsc = alloc(es1, nc, "gsc", [128, 512], F32)
        ps_sh = palloc(es1, nc, "ps_sh", [128, 512])
        ps_bt = palloc(es1, nc, "ps_bt", [128, 512])
        for oc in range(NCH):
            for tt in range(NTOK // 512):
                t0 = tt * 512
                mm = [(ps_sh[:, :], shw_t[:, fc * D + oc * 128: fc * D + (oc + 1) * 128],
                       fmslice(xn2, NTOK, fc, t0, 512), fc == 0, fc == NCH - 1)
                      for fc in range(NCH)]
                p.mm_group(mm, reads=(shw_t.name, xn2.name), writes=(ps_sh.name,))
                p.act(gsc[:, :], ps_sh[:, :], AF.Gelu, bias=shb_t[:, oc:oc + 1],
                      reads=(ps_sh.name, shb_t.name), writes=(gsc.name,))
                p.mm_group([(ps_bt[:, :], b2t[:, oc * 128:(oc + 1) * 128],
                             comb_bf[:, t0:t0 + 512], True, True)],
                           reads=(b2t.name, comb_bf.name), writes=(ps_bt.name,))
                p.v("scalar_tensor_tensor", fmslice(acc, NTOK, oc, t0, 512),
                    gsc[:, :], 0.5, ps_bt[:, :], op0=ALU.mult, op1=ALU.add,
                    reads=(gsc.name, ps_bt.name), writes=(acc.name,))
    p.barrier()

    with ExitStack() as es2:
        b1col = alloc(es2, nc, "b1col", [128, 8 * 32], F32)
        for e in range(8):
            for hc in range(32):
                p.dma("sync", b1col[:, e * 32 + hc: e * 32 + hc + 1],
                      b1c_d[e * 4096 + hc * 128: e * 4096 + (hc + 1) * 128, :],
                      writes=(b1col.name,))
        emit_moe_experts(p, es2, cst, nc, xn2, comb_bf, w1e_d, w2e_d, acc,
                         AF.Gelu, b1col_t=b1col)
    p.barrier()

    with ExitStack() as es3:
        gam_t = alloc(es3, nc, "gam_t", [128, 8], F32)
        bet_t = alloc(es3, nc, "bet_t", [128, 8], F32)
        for fc in range(NCH):
            p.dma("sync", gam_t[:, fc:fc + 1], gamma_d[fc * 128:(fc + 1) * 128, :],
                  writes=(gam_t.name,))
            p.dma("sync", bet_t[:, fc:fc + 1], beta_d[fc * 128:(fc + 1) * 128, :],
                  writes=(bet_t.name,))
        sq = alloc(es3, nc, "lnf_sq", [128, 512], F32)
        t1 = alloc(es3, nc, "lnf_t1", [128, 512], F32)
        t2 = alloc(es3, nc, "lnf_t2", [128, 512], F32)
        g2t = alloc(es3, nc, "g2t", [128, 512], BF16)
        r_mean = alloc(es3, nc, "lnf_rm", [1, 512], F32)
        r_var = alloc(es3, nc, "lnf_rv", [1, 512], F32)
        r_rstd = alloc(es3, nc, "lnf_rr", [1, 512], F32)
        ps_sx = palloc(es3, nc, "lnf_psx", [1, 512])
        ps_sq = palloc(es3, nc, "lnf_psq", [1, 512])
        ps_mb = palloc(es3, nc, "lnf_pmb", [128, 512])
        ps_rb = palloc(es3, nc, "lnf_prb", [128, 512])
        oc1 = cst.ones_col(); orow = cst.ones_row()
        epsc = cst.colv(1e-5, 1)
        for tt in range(NTOK // 512):
            t0 = tt * 512
            for fc in range(NCH):
                xs = fmslice(acc, NTOK, fc, t0, 512)
                p.act(sq[:, :], xs, AF.Square, reads=(acc.name,), writes=(sq.name,))
                p.mm_group([(ps_sx[:, :], oc1[:, :], xs, fc == 0, fc == NCH - 1)],
                           reads=(acc.name, oc1.name), writes=(ps_sx.name,))
                p.mm_group([(ps_sq[:, :], oc1[:, :], sq[:, :], fc == 0, fc == NCH - 1)],
                           reads=(sq.name, oc1.name), writes=(ps_sq.name,))
            p.v("tensor_scalar_mul", r_mean[:, :], ps_sx[:, :], 1.0 / D,
                reads=(ps_sx.name,), writes=(r_mean.name,))
            p.v("tensor_mul", r_var[:, :], r_mean[:, :], r_mean[:, :],
                reads=(r_mean.name,), writes=(r_var.name,))
            p.v("scalar_tensor_tensor", r_var[:, :], ps_sq[:, :], 1.0 / D,
                r_var[:, :], op0=ALU.mult, op1=ALU.subtract,
                reads=(ps_sq.name, r_var.name), writes=(r_var.name,))
            p.act(r_var[:, :], r_var[:, :], AF.Sqrt, bias=epsc[0:1, :],
                  reads=(r_var.name, epsc.name), writes=(r_var.name,))
            p.v("reciprocal", r_rstd[:, :], r_var[:, :],
                reads=(r_var.name,), writes=(r_rstd.name,))
            p.mm_group([(ps_mb[:, :], orow[:, :], r_mean[:, :], True, True)],
                       reads=(r_mean.name, orow.name), writes=(ps_mb.name,))
            p.mm_group([(ps_rb[:, :], orow[:, :], r_rstd[:, :], True, True)],
                       reads=(r_rstd.name, orow.name), writes=(ps_rb.name,))
            for fc in range(NCH):
                xs = fmslice(acc, NTOK, fc, t0, 512)
                p.v("tensor_sub", t1[:, :], xs, ps_mb[:, :],
                    reads=(acc.name, ps_mb.name), writes=(t1.name,))
                p.v("tensor_mul", t2[:, :], t1[:, :], ps_rb[:, :],
                    reads=(t1.name, ps_rb.name), writes=(t2.name,))
                p.v("tensor_scalar_mul", t2[:, :], t2[:, :], gam_t[:, fc:fc + 1],
                    reads=(t2.name, gam_t.name), writes=(t2.name,))
                p.act(g2t[:, :], t2[:, :], AF.Silu, bias=bet_t[:, fc:fc + 1],
                      reads=(t2.name, bet_t.name), writes=(g2t.name,))
                p.dma("gpsimd", g2_d[fc * 128:(fc + 1) * 128, t0:t0 + 512],
                      g2t[:, :], reads=(g2t.name,))
    p.barrier()
    return nc, p



def build_l2b():
    nc = bass.Bass("TRN2", target_bir_lowering=False, debug=False)
    x_d = nc.dram_tensor("x", [D, NTOK], F32, kind="ExternalInput").ap()
    partall_d = nc.dram_tensor("partall", [4, 512, 66], F32, kind="ExternalInput").ap()
    qsm_d = nc.dram_tensor("qsm", [512, NTOK], BF16, kind="ExternalInput").ap()
    g2_d = nc.dram_tensor("g2", [D, NTOK], BF16, kind="ExternalInput").ap()
    wo2_d = nc.dram_tensor("wo2", [512, D], F32, kind="ExternalInput").ap()
    bo_d = nc.dram_tensor("bo", [D, 1], F32, kind="ExternalInput").ap()
    g3g_d = nc.dram_tensor("g3g", [D, 1], F32, kind="ExternalInput").ap()
    wqkv3_d = nc.dram_tensor("wqkv3", [D, 3072], F32, kind="ExternalInput").ap()
    gate3_d = nc.dram_tensor("gate3", [D, 8], F32, kind="ExternalInput").ap()

    x3_d = nc.dram_tensor("x3", [D, NTOK], F32, kind="ExternalOutput").ap()
    xn3_d = nc.dram_tensor("xn3", [D, NTOK], BF16, kind="ExternalOutput").ap()
    q3_d = nc.dram_tensor("q3", [D, NTOK], BF16, kind="ExternalOutput").ap()
    k3_d = nc.dram_tensor("k3", [D, NTOK], BF16, kind="ExternalOutput").ap()
    v3_d = nc.dram_tensor("v3", [NTOK, 1040], BF16, kind="ExternalOutput").ap()
    comb3_d = nc.dram_tensor("comb3", [8, NTOK], BF16, kind="ExternalOutput").ap()

    p = Prog(nc)
    es0 = ExitStack()
    cst = Consts(p, es0)
    cst.ones_col(); cst.ones_row(); cst.ident_bf(); cst.ident_f32()
    cst.colv(1e-5, 1); cst.colv(1e-6, 1)

    xn3 = alloc(es0, nc, "xn3s", [128, NCH * NTOK], BF16)
    mrow3 = alloc(es0, nc, "mrow3", [1, NTOK], F32)
    rrow3 = alloc(es0, nc, "rrow3", [1, NTOK], F32)
    comb3 = alloc(es0, nc, "comb3s", [8, NTOK], BF16)

    es_mid = ExitStack()
    x_t = alloc(es_mid, nc, "x_t", [128, NCH * NTOK], F32)
    emit_fm_load(p, x_t, x_d, NTOK)

    with ExitStack() as es1:
        pr = [alloc(es1, nc, f"pr{r}", [128, 4 * 66], F32) for r in range(4)]
        for r in range(4):
            for c4 in range(4):
                p.dma("sync", pr[r][:, c4 * 66:(c4 + 1) * 66],
                      partall_d[r, c4 * 128:(c4 + 1) * 128, :], writes=(pr[r].name,))
        ctxn = alloc(es1, nc, "ctxn", [128, 4 * 64], BF16)
        mg = alloc(es1, nc, "mg", [128, 1], F32)
        zg = alloc(es1, nc, "zg", [128, 1], F32)
        sr = alloc(es1, nc, "sr", [128, 1], F32)
        cacc = alloc(es1, nc, "cacc", [128, 64], F32)
        for c4 in range(4):
            p.v("tensor_copy", mg[:, :], pr[0][:, c4 * 66 + 64: c4 * 66 + 65],
                reads=(pr[0].name,), writes=(mg.name,))
            for r in range(1, 4):
                p.v("tensor_max", mg[:, :], mg[:, :], pr[r][:, c4 * 66 + 64: c4 * 66 + 65],
                    reads=(mg.name, pr[r].name), writes=(mg.name,))
            p.v("memset", zg[:, :], 0.0, writes=(zg.name,))
            p.v("memset", cacc[:, :], 0.0, writes=(cacc.name,))
            for r in range(4):
                p.v("tensor_sub", sr[:, :], pr[r][:, c4 * 66 + 64: c4 * 66 + 65], mg[:, :],
                    reads=(pr[r].name, mg.name), writes=(sr.name,))
                p.act(sr[:, :], sr[:, :], AF.Exp, reads=(sr.name,), writes=(sr.name,))
                p.v("scalar_tensor_tensor", zg[:, :],
                    pr[r][:, c4 * 66 + 65: c4 * 66 + 66], sr[:, :], zg[:, :],
                    op0=ALU.mult, op1=ALU.add,
                    reads=(pr[r].name, sr.name, zg.name), writes=(zg.name,))
                p.v("scalar_tensor_tensor", cacc[:, :],
                    pr[r][:, c4 * 66: c4 * 66 + 64], sr[:, :], cacc[:, :],
                    op0=ALU.mult, op1=ALU.add,
                    reads=(pr[r].name, sr.name, cacc.name), writes=(cacc.name,))
            p.v("reciprocal", zg[:, :], zg[:, :], reads=(zg.name,), writes=(zg.name,))
            p.v("tensor_scalar_mul", ctxn[:, c4 * 64:(c4 + 1) * 64], cacc[:, :], zg[:, :],
                reads=(cacc.name, zg.name), writes=(ctxn.name,))
        qsm_t = alloc(es1, nc, "qsm_t", [128, 4 * NTOK], BF16)
        for c4 in range(4):
            p.dma("sync", qsm_t[:, c4 * NTOK:(c4 + 1) * NTOK],
                  qsm_d[c4 * 128:(c4 + 1) * 128, :], writes=(qsm_t.name,))
        outfm = alloc(es1, nc, "outfm", [128, 4 * NTOK], BF16)
        ps_of = [palloc(es1, nc, f"ps_of{j}", [128, 512]) for j in range(4)]
        for tt in range(NTOK // 512):
            t0 = tt * 512
            mm = []
            for par in range(2):
                for j in range(4):
                    h = 2 * j + par
                    r0 = 64 * par
                    mm.append((ps_of[j][r0:r0 + 64, :],
                               ctxn[r0:r0 + 64, j * 64:(j + 1) * 64],
                               qsm_t[r0:r0 + 64, j * NTOK + t0: j * NTOK + t0 + 512],
                               True, True))
            p.mm_group(mm, reads=(ctxn.name, qsm_t.name),
                       writes=tuple(t.name for t in ps_of))
            for j in range(4):
                p.v("tensor_copy", outfm[:, j * NTOK + t0: j * NTOK + t0 + 512],
                    ps_of[j][:, :], reads=(ps_of[j].name,), writes=(outfm.name,))
        wstg = WBatch(p, es1, "wo2", 4 * D, double=False)
        wo2_t = alloc(es1, nc, "wo2_t", [128, 4 * D], BF16)
        wstg.load(wo2_t, 0, wo2_d[0:512, 0:D], 512, D)
        bo_t = alloc(es1, nc, "bo_t", [128, 8], F32)
        for fc in range(NCH):
            p.dma("sync", bo_t[:, fc:fc + 1], bo_d[fc * 128:(fc + 1) * 128, :],
                  writes=(bo_t.name,))
        g2_t = alloc(es1, nc, "g2_t", [128, 512], BF16)
        sc_s = alloc(es1, nc, "sc_s", [128, 512], F32)
        sc_g = alloc(es1, nc, "sc_g", [128, 512], F32)
        ps_s = palloc(es1, nc, "ps_s", [128, 512])
        for oc in range(NCH):
            for tt in range(NTOK // 512):
                t0 = tt * 512
                mm = [(ps_s[:, :], wo2_t[:, hc * D + oc * 128: hc * D + (oc + 1) * 128],
                       outfm[:, hc * NTOK + t0: hc * NTOK + t0 + 512],
                       hc == 0, hc == 3) for hc in range(4)]
                p.mm_group(mm, reads=(wo2_t.name, outfm.name), writes=(ps_s.name,))
                p.act(sc_s[:, :], ps_s[:, :], AF.Identity, bias=bo_t[:, oc:oc + 1],
                      reads=(ps_s.name, bo_t.name), writes=(sc_s.name,))
                p.dma("sync", g2_t[:, :], g2_d[oc * 128:(oc + 1) * 128, t0:t0 + 512],
                      writes=(g2_t.name,))
                p.v("tensor_mul", sc_g[:, :], sc_s[:, :], g2_t[:, :],
                    reads=(sc_s.name, g2_t.name), writes=(sc_g.name,))
                xs = fmslice(x_t, NTOK, oc, t0, 512)
                p.v("tensor_add", xs, xs, sc_g[:, :],
                    reads=(x_t.name, sc_g.name), writes=(x_t.name,))
        emit_fm_store(p, x3_d, x_t, NTOK)
    p.barrier()
    emit_ln(p, None, cst, x_t, xn3, NTOK, tag="3", stats=(mrow3, None, rrow3))
    emit_fm_store(p, xn3_d, xn3, NTOK)

    with ExitStack() as esg:
        emit_topk_gating(p, esg, cst, nc, x_t, mrow3, rrow3, gate3_d, 8,
                         comb3, mode="l3")
    for hh in range(NTOK // 512):
        p.dma("gpsimd", comb3_d[:, hh * 512:(hh + 1) * 512],
              comb3[:, hh * 512:(hh + 1) * 512], reads=(comb3.name,))
    es_mid.close()
    p.barrier()

    with ExitStack() as es2:
        xn3b = alloc(es2, nc, "xn3b", [128, NCH * NTOK], BF16)
        g3c = alloc(es2, nc, "g3c", [128, 8], F32)
        for fc in range(NCH):
            p.dma("sync", g3c[:, fc:fc + 1], g3g_d[fc * 128:(fc + 1) * 128, :],
                  writes=(g3c.name,))
        sq = alloc(es2, nc, "r_sq", [128, 512], F32)
        rrow = alloc(es2, nc, "r_row", [1, 512], F32)
        t1 = alloc(es2, nc, "r_t1", [128, 512], F32)
        ps_sx = palloc(es2, nc, "r_psx", [1, 512])
        ps_rb = palloc(es2, nc, "r_prb", [128, 512])
        epsc = cst.colv(1e-6, 1)
        for tt in range(NTOK // 512):
            t0 = tt * 512
            for fc in range(NCH):
                p.act(sq[:, :], fmslice(xn3, NTOK, fc, t0, 512), AF.Square,
                      reads=(xn3.name,), writes=(sq.name,))
                p.mm_group([(ps_sx[:, :], cst.ones_col()[:, :], sq[:, :],
                             fc == 0, fc == NCH - 1)],
                           reads=(sq.name, cst.ones_col().name), writes=(ps_sx.name,))
            p.v("tensor_scalar_mul", rrow[:, :], ps_sx[:, :], 1.0 / D,
                reads=(ps_sx.name,), writes=(rrow.name,))
            p.act(rrow[:, :], rrow[:, :], AF.Sqrt, bias=epsc[0:1, :],
                  reads=(rrow.name, epsc.name), writes=(rrow.name,))
            p.v("reciprocal", rrow[:, :], rrow[:, :], reads=(rrow.name,), writes=(rrow.name,))
            p.mm_group([(ps_rb[:, :], cst.ones_row()[:, :], rrow[:, :], True, True)],
                       reads=(cst.ones_row().name, rrow.name), writes=(ps_rb.name,))
            for fc in range(NCH):
                p.v("tensor_mul", t1[:, :], fmslice(xn3, NTOK, fc, t0, 512), ps_rb[:, :],
                    reads=(xn3.name, ps_rb.name), writes=(t1.name,))
                p.v("tensor_scalar_mul", fmslice(xn3b, NTOK, fc, t0, 512),
                    t1[:, :], g3c[:, fc:fc + 1],
                    reads=(t1.name, g3c.name), writes=(xn3b.name,))
        p.barrier()
        wstg = WBatch(p, es2, "wqkv3", 3072, double=False)
        w3_t = alloc(es2, nc, "w3_t", [128, NCH * 3072], BF16)
        for fc in range(NCH):
            wstg.load(w3_t, fc * 3072, wqkv3_d[fc * 128:(fc + 1) * 128, 0:3072],
                      128, 3072)
        qk_t = alloc(es2, nc, "qk_t", [128, 512], BF16)
        ps_p = palloc(es2, nc, "ps_p", [128, 512])
        for (part, out_d) in ((0, q3_d), (1, k3_d)):
            for oc in range(NCH):
                for tt in range(NTOK // 512):
                    t0 = tt * 512
                    mm = [(ps_p[:, :],
                           w3_t[:, fc * 3072 + part * 1024 + oc * 128:
                                fc * 3072 + part * 1024 + (oc + 1) * 128],
                           fmslice(xn3b, NTOK, fc, t0, 512), fc == 0, fc == NCH - 1)
                          for fc in range(NCH)]
                    p.mm_group(mm, reads=(w3_t.name, xn3b.name), writes=(ps_p.name,))
                    p.v("tensor_copy", qk_t[:, :], ps_p[:, :],
                        reads=(ps_p.name,), writes=(qk_t.name,))
                    p.dma("gpsimd", out_d[oc * 128:(oc + 1) * 128, t0:t0 + 512],
                          qk_t[:, :], reads=(qk_t.name,))
        va_t = alloc(es2, nc, "va_t", [128, 1040], BF16)
        p.v("memset", va_t[:, :], 1.0, writes=(va_t.name,))
        ps_v = palloc(es2, nc, "ps_v", [128, 1024])
        for tc in range(NTOK // 128):
            t0 = tc * 128
            for hh in range(2):
                mm = [(ps_v[:, hh * 512:(hh + 1) * 512],
                       fmslice(xn3b, NTOK, fc, t0, 128),
                       w3_t[:, fc * 3072 + 2048 + hh * 512: fc * 3072 + 2048 + (hh + 1) * 512],
                       fc == 0, fc == NCH - 1) for fc in range(NCH)]
                p.mm_group(mm, reads=(xn3b.name, w3_t.name), writes=(ps_v.name,))
            p.v("tensor_copy",
                va_t[:, :].rearrange("p (h e) -> p h e", e=65)[:, :, 0:64],
                ps_v[:, :].rearrange("p (h d) -> p h d", d=64),
                reads=(ps_v.name,), writes=(va_t.name,))
            p.dma("gpsimd", v3_d[t0:t0 + 128, :], va_t[:, :], reads=(va_t.name,))
    p.barrier()
    return nc, p



def build_l3f():
    nc = bass.Bass("TRN2", target_bir_lowering=False, debug=False)
    xn3_d = nc.dram_tensor("xn3", [D, NTOK], BF16, kind="ExternalInput").ap()
    comb_d = nc.dram_tensor("comb", [8, NTOK], BF16, kind="ExternalInput").ap()
    sw1_d = nc.dram_tensor("sw1", [D, 4096], F32, kind="ExternalInput").ap()
    sw2_d = nc.dram_tensor("sw2", [4096, D], F32, kind="ExternalInput").ap()
    w1e_d = nc.dram_tensor("w1e", [8, D, 4096], F32, kind="ExternalInput").ap()
    w2e_d = nc.dram_tensor("w2e", [8, 4096, D], F32, kind="ExternalInput").ap()
    g3_d = nc.dram_tensor("g3", [D, NTOK], BF16, kind="ExternalOutput").ap()

    p = Prog(nc)
    es0 = ExitStack()
    cst = Consts(p, es0)
    cst.ones_col(); cst.ones_row()
    ehot8_np = np.zeros((8, 8 * 128), np.float32)
    for e in range(8):
        ehot8_np[e, e * 128:(e + 1) * 128] = 1.0
    cst.get("ehot8", ehot8_np, BF16)

    xn3 = alloc(es0, nc, "xn3s", [128, NCH * NTOK], BF16)
    emit_fm_load(p, xn3, xn3_d, NTOK)
    comb_bf = alloc(es0, nc, "comb_bf", [8, NTOK], BF16)
    p.dma("sync", comb_bf[:, :], comb_d, writes=(comb_bf.name,))
    acc = alloc(es0, nc, "acc", [128, NCH * NTOK], F32)

    with ExitStack() as es1:
        st = WBatch(p, es1, "sh3", 16 * 128)
        w1t = alloc(es1, nc, "sh_w1t", [128, NCH * 512], BF16)
        w2c = alloc(es1, nc, "sh_w2c", [128, 32 * 128], BF16)
        h_t = alloc(es1, nc, "sh_ht", [128, 32 * 1024], BF16)
        ps_h = palloc(es1, nc, "sh_psh", [128, 512])
        ps_h2 = palloc(es1, nc, "sh_psh2", [128, 512])
        ps_o = palloc(es1, nc, "sh_pso", [128, 512])
        ps_o2 = palloc(es1, nc, "sh_pso2", [128, 512])
        qs = ["sync", "scalar"]
        for tp in range(2):
            tbase = tp * 1024
            for hg in range(8):
                st.load(w1t, 0, sw1_d[0:512, hg * 512:(hg + 1) * 512], 512, 512,
                        eng=qs[hg % 2])
                st.load(w1t, 4 * 512, sw1_d[512:D, hg * 512:(hg + 1) * 512], 512, 512,
                        eng=qs[(hg + 1) % 2])
                for hc4 in range(4):
                    hc = hg * 4 + hc4
                    for tq in range(2):
                        t0 = tbase + tq * 512
                        psx = ps_h if (hc4 + tq) % 2 == 0 else ps_h2
                        mm = [(psx[:, :],
                               w1t[:, fc * 512 + hc4 * 128: fc * 512 + (hc4 + 1) * 128],
                               fmslice(xn3, NTOK, fc, t0, 512), fc == 0, fc == NCH - 1)
                              for fc in range(NCH)]
                        p.mm_group(mm, reads=(w1t.name, xn3.name), writes=(psx.name,))
                        p.act(h_t[:, hc * 1024 + tq * 512: hc * 1024 + (tq + 1) * 512],
                              psx[:, :], AF.Silu, reads=(psx.name,), writes=(h_t.name,))
            for oc in range(NCH):
                st.load(w2c, 0, sw2_d[0:2048, oc * 128:(oc + 1) * 128], 2048, 128,
                        eng=qs[oc % 2])
                st.load(w2c, 16 * 128, sw2_d[2048:4096, oc * 128:(oc + 1) * 128],
                        2048, 128, eng=qs[(oc + 1) % 2])
                for tq in range(2):
                    t0 = tbase + tq * 512
                    psx = ps_o if (oc + tq) % 2 == 0 else ps_o2
                    mm = [(psx[:, :], w2c[:, hc * 128:(hc + 1) * 128],
                           h_t[:, hc * 1024 + tq * 512: hc * 1024 + (tq + 1) * 512],
                           hc == 0, hc == 31) for hc in range(32)]
                    p.mm_group(mm, reads=(w2c.name, h_t.name), writes=(psx.name,))
                    p.v("tensor_copy", fmslice(acc, NTOK, oc, t0, 512), psx[:, :],
                        reads=(psx.name,), writes=(acc.name + f":{oc}",))
    p.barrier()
    with ExitStack() as es2:
        emit_moe_experts(p, es2, cst, nc, xn3, comb_bf, w1e_d, w2e_d, acc,
                         AF.Silu, b1col_t=None)
    p.barrier()
    with ExitStack() as es3:
        g3t = alloc(es3, nc, "g3t", [128, 512], BF16)
        for oc in range(NCH):
            for tt in range(NTOK // 512):
                t0 = tt * 512
                p.act(g3t[:, :], fmslice(acc, NTOK, oc, t0, 512), AF.Silu,
                      reads=(acc.name,), writes=(g3t.name,))
                p.dma("gpsimd", g3_d[oc * 128:(oc + 1) * 128, t0:t0 + 512],
                      g3t[:, :], reads=(g3t.name,))
    p.barrier()
    return nc, p



def build_l3b():
    nc = bass.Bass("TRN2", target_bir_lowering=False, debug=False)
    x_d = nc.dram_tensor("x", [D, NTOK], F32, kind="ExternalInput").ap()
    xn3_d = nc.dram_tensor("xn3", [D, NTOK], BF16, kind="ExternalInput").ap()
    g3_d = nc.dram_tensor("g3", [D, NTOK], BF16, kind="ExternalInput").ap()
    q3_d = nc.dram_tensor("q3", [D, NTOK], BF16, kind="ExternalInput").ap()
    k3_d = nc.dram_tensor("k3", [D, NTOK], BF16, kind="ExternalInput").ap()
    v3_d = nc.dram_tensor("v3", [NTOK, 1040], BF16, kind="ExternalInput").ap()
    kb_d = nc.dram_tensor("kb", [D, 256], BF16, kind="ExternalInput").ap()
    vb_d = nc.dram_tensor("vb", [256, 1040], BF16, kind="ExternalInput").ap()
    pmask_d = nc.dram_tensor("pmask", [2, 128, 128], F32, kind="ExternalInput").ap()
    wo3_d = nc.dram_tensor("wo3", [D, D], F32, kind="ExternalInput").ap()
    x4_d = nc.dram_tensor("x4", [D, NTOK], F32, kind="ExternalOutput").ap()

    p = Prog(nc)
    es0 = ExitStack()
    cst = Consts(p, es0)
    cst.ones_col(); cst.ones_row()
    parc = np.zeros((128, 2), np.float32)
    parc[0:64, 0] = 1.0; parc[64:128, 1] = 1.0
    parcol = cst.get("parcol", parc)
    curm = np.where(np.arange(128)[:, None] <= np.arange(128)[None, :],
                    0.0, -30000.0).astype(np.float32)
    curmask = cst.get("l3cur", np.tile(curm, (1, 8)))

    outfm = alloc(es0, nc, "outfm", [128, NCH * NTOK], BF16)
    es_attn = ExitStack()
    q_t = alloc(es_attn, nc, "q_t", [128, NCH * NTOK], BF16)
    k_t = alloc(es_attn, nc, "k_t", [128, NCH * NTOK], BF16)
    v_t = alloc(es_attn, nc, "v_t", [128, (NTOK // 128) * 1040], BF16)
    kb_t = alloc(es_attn, nc, "kb_t", [128, NCH * 256], BF16)
    vb_t = alloc(es_attn, nc, "vb_t", [128, 2 * 1040], BF16)
    emit_fm_load(p, q_t, q3_d, NTOK)
    emit_fm_load(p, k_t, k3_d, NTOK)
    emit_fm_load(p, kb_t, kb_d, 256)
    for tc in range(NTOK // 128):
        p.dma("sync", v_t[:, tc * 1040:(tc + 1) * 1040], v3_d[tc * 128:(tc + 1) * 128, :],
              writes=(v_t.name,))
    for bb in range(2):
        p.dma("sync", vb_t[:, bb * 1040:(bb + 1) * 1040], vb_d[bb * 128:(bb + 1) * 128, :],
              writes=(vb_t.name,))
    pm_t = alloc(es_attn, nc, "pm_t", [128, 2 * 128], F32)
    for ob in range(2):
        p.dma("sync", pm_t[:, ob * 128:(ob + 1) * 128], pmask_d[ob], writes=(pm_t.name,))
    pmt = alloc(es_attn, nc, "pmt", [128, 2 * 1024], F32)
    for ob in range(2):
        for i in range(8):
            p.v("tensor_copy", pmt[:, ob * 1024 + i * 128: ob * 1024 + (i + 1) * 128],
                pm_t[:, ob * 128:(ob + 1) * 128],
                reads=(pm_t.name,), writes=(pmt.name,))

    bias_c = alloc(es_attn, nc, "bias_c", [128, 1], F32)
    with ExitStack() as esb:
        sqb = alloc(esb, nc, "sqb", [128, NTOK], F32)
        nr = alloc(esb, nc, "nr", [1, NTOK], F32)
        mxv = alloc(esb, nc, "mxv", [1, 4], F32)
        ps_nb = palloc(esb, nc, "ps_nb", [1, 512])
        ps_bc = palloc(esb, nc, "ps_bc", [128, 1])
        first = {0: True, 1: True}
        for (src_t, W, idx) in ((q_t, NTOK, 0), (k_t, NTOK, 1), (kb_t, 256, 1)):
            for fc in range(NCH):
                p.act(sqb[:, 0:W], src_t[:, fc * W:(fc + 1) * W], AF.Square,
                      reads=(src_t.name,), writes=(sqb.name,))
                for par in range(2):
                    for ss in range(W // 512 if W >= 512 else 1):
                        wseg = min(512, W)
                        p.mm_group([(ps_nb[:, 0:wseg], parcol[:, par:par + 1],
                                     sqb[:, ss * 512: ss * 512 + wseg], True, True)],
                                   reads=(parcol.name, sqb.name), writes=(ps_nb.name,))
                        if first[idx]:
                            p.v("reduce_max", mxv[:, idx:idx + 1], ps_nb[:, 0:wseg],
                                axis=AX.X, reads=(ps_nb.name,), writes=(mxv.name,))
                            first[idx] = False
                        else:
                            p.v("reduce_max", nr[:, 0:1], ps_nb[:, 0:wseg], axis=AX.X,
                                reads=(ps_nb.name,), writes=(nr.name,))
                            p.v("tensor_max", mxv[:, idx:idx + 1], mxv[:, idx:idx + 1],
                                nr[:, 0:1], reads=(mxv.name, nr.name), writes=(mxv.name,))
        p.v("tensor_mul", mxv[:, 0:1], mxv[:, 0:1], mxv[:, 1:2],
            reads=(mxv.name,), writes=(mxv.name,))
        p.act(mxv[:, 0:1], mxv[:, 0:1], AF.Sqrt, reads=(mxv.name,), writes=(mxv.name,))
        p.v("tensor_scalar_mul", mxv[:, 0:1], mxv[:, 0:1], -0.125,
            reads=(mxv.name,), writes=(mxv.name,))
        p.mm_group([(ps_bc[:, :], cst.ones_row()[:, :], mxv[:, 0:1], True, True)],
                   reads=(cst.ones_row().name, mxv.name), writes=(ps_bc.name,))
        p.v("tensor_copy", bias_c[:, :], ps_bc[:, :], reads=(ps_bc.name,), writes=(bias_c.name,))
    p.barrier()

    with ExitStack() as es1:
        ptile = alloc(es1, nc, "ptile", [128, 1024], BF16)
        sc64 = alloc(es1, nc, "sc64", [64, 1024], F32)
        nrm = alloc(es1, nc, "nrm", [64, 1024], F32)
        ps_sp = palloc(es1, nc, "ps_sp", [128, 1024])
        ps_sc = palloc(es1, nc, "ps_sc", [128, 1024])
        ps_ov = palloc(es1, nc, "ps_ov", [65, 1024])
        ps_nb2 = palloc(es1, nc, "ps_nb2", [64, 1024])
        for w in range(NTOK // 128):
            ob = w // 8
            t0 = w * 128
            if w % 8 == 0:
                kprev = lambda h: kb_t[64 * (h % 2):64 * (h % 2) + 64,
                                       (h // 2) * 256 + ob * 128: (h // 2) * 256 + ob * 128 + 128]
                vprev_base = ("vb", ob * 1040)
            else:
                tp0 = (w - 1) * 128
                kprev = lambda h: k_t[64 * (h % 2):64 * (h % 2) + 64,
                                      (h // 2) * NTOK + tp0: (h // 2) * NTOK + tp0 + 128]
                vprev_base = ("v", (w - 1) * 1040)
            for hg in range(2):
                for (ps_x, ksel, masksel) in (
                    (ps_sp, "prev", ("pmt", ob * 1024)),
                    (ps_sc, "cur", ("cur", 0)),
                ):
                    mm = []
                    for i in [0, 2, 4, 6, 1, 3, 5, 7]:
                        h = hg * 8 + i
                        r0 = 64 * (h % 2)
                        col = (i % 2) * 512 + (i // 2) * 128
                        if ksel == "prev":
                            lhs = kprev(h)
                        else:
                            lhs = k_t[r0:r0 + 64, (h // 2) * NTOK + t0:
                                      (h // 2) * NTOK + t0 + 128]
                        rhs = q_t[r0:r0 + 64, (h // 2) * NTOK + t0:
                                  (h // 2) * NTOK + t0 + 128]
                        mm.append((ps_x[:, col:col + 128], lhs, rhs, True, True))
                    p.mm_group(mm, reads=(k_t.name, kb_t.name, q_t.name),
                               writes=(ps_x.name,))
                p.v("tensor_add", ps_sp[:, :], ps_sp[:, :], pmt[:, ob * 1024:(ob + 1) * 1024],
                    reads=(ps_sp.name, pmt.name), writes=(ps_sp.name,))
                p.v("tensor_add", ps_sc[:, :], ps_sc[:, :], curmask[:, :],
                    reads=(ps_sc.name, curmask.name), writes=(ps_sc.name,))
                p.act(ptile[:, :], ps_sp[:, :], AF.Exp, bias=bias_c[:, 0:1], scale=0.125,
                      reads=(ps_sp.name, bias_c.name), writes=(ptile.name,))
                mm = []
                for i in range(8):
                    h = hg * 8 + i
                    col = (i % 2) * 512 + (i // 2) * 128
                    if vprev_base[0] == "vb":
                        vsl = vb_t[:, vprev_base[1] + 65 * h: vprev_base[1] + 65 * h + 65]
                        vread = vb_t.name
                    else:
                        vsl = v_t[:, vprev_base[1] + 65 * h: vprev_base[1] + 65 * h + 65]
                        vread = v_t.name
                    mm.append((ps_ov[:, col:col + 128], vsl, ptile[:, col:col + 128],
                               True, False))
                p.mm_group(mm, reads=(vread, ptile.name), writes=(ps_ov.name,))
                p.act(ptile[:, :], ps_sc[:, :], AF.Exp, bias=bias_c[:, 0:1], scale=0.125,
                      reads=(ps_sc.name, bias_c.name), writes=(ptile.name,))
                mm = []
                for i in range(8):
                    h = hg * 8 + i
                    col = (i % 2) * 512 + (i // 2) * 128
                    vsl = v_t[:, w * 1040 + 65 * h: w * 1040 + 65 * h + 65]
                    mm.append((ps_ov[:, col:col + 128], vsl, ptile[:, col:col + 128],
                               False, True))
                p.mm_group(mm, reads=(v_t.name, ptile.name), writes=(ps_ov.name,))
                p.v("reciprocal", nrm[0:1, :], ps_ov[64:65, :],
                    reads=(ps_ov.name,), writes=(nrm.name,))
                p.mm_group([(ps_nb2[:, hh * 512:(hh + 1) * 512],
                             cst.ones_row()[:, 0:64], nrm[0:1, hh * 512:(hh + 1) * 512],
                             True, True) for hh in range(2)],
                           reads=(cst.ones_row().name, nrm.name), writes=(ps_nb2.name,))
                p.v("tensor_copy", sc64[:, :], ps_ov[0:64, :],
                    reads=(ps_ov.name,), writes=(sc64.name,))
                p.v("tensor_mul", nrm[:, :], sc64[:, :], ps_nb2[:, :],
                    reads=(sc64.name, ps_nb2.name), writes=(nrm.name,))
                for i in range(8):
                    h = hg * 8 + i
                    r0 = 64 * (h % 2)
                    col = (i % 2) * 512 + (i // 2) * 128
                    dst = outfm[r0:r0 + 64, (h // 2) * NTOK + t0:(h // 2) * NTOK + t0 + 128]
                    if r0 == 0:
                        p.v("tensor_copy", dst, nrm[:, col:col + 128],
                            reads=(nrm.name,), writes=(outfm.name,))
                    else:
                        p.dma("gpsimd", dst, nrm[:, col:col + 128],
                              reads=(nrm.name,), writes=(outfm.name,))
    es_attn.close()
    p.barrier()

    with ExitStack() as es2:
        x_t = alloc(es2, nc, "x_t", [128, NCH * NTOK], F32)
        emit_fm_load(p, x_t, x_d, NTOK)
        xn3_t = alloc(es2, nc, "xn3_t", [128, NCH * NTOK], BF16)
        emit_fm_load(p, xn3_t, xn3_d, NTOK)
        wstg = WBatch(p, es2, "wo3", 4 * D, double=False)
        wo3_t = load_w_fm(p, es2, cst, wo3_d, "wo3", D, D, stager=wstg)
        g3_t = alloc(es2, nc, "g3_t", [128, 512], BF16)
        sc_s = alloc(es2, nc, "sc_s", [128, 512], F32)
        sc_g = alloc(es2, nc, "sc_g", [128, 512], F32)
        ps_s = palloc(es2, nc, "ps_s", [128, 512])
        for oc in range(NCH):
            for tt in range(NTOK // 512):
                t0 = tt * 512
                mm = [(ps_s[:, :], wo3_t[:, fc * D + oc * 128: fc * D + (oc + 1) * 128],
                       fmslice(outfm, NTOK, fc, t0, 512), fc == 0, fc == NCH - 1)
                      for fc in range(NCH)]
                p.mm_group(mm, reads=(wo3_t.name, outfm.name), writes=(ps_s.name,))
                p.dma("sync", g3_t[:, :], g3_d[oc * 128:(oc + 1) * 128, t0:t0 + 512],
                      writes=(g3_t.name,))
                p.v("tensor_add", sc_s[:, :], ps_s[:, :],
                    fmslice(xn3_t, NTOK, oc, t0, 512),
                    reads=(ps_s.name, xn3_t.name), writes=(sc_s.name,))
                p.v("tensor_mul", sc_g[:, :], sc_s[:, :], g3_t[:, :],
                    reads=(sc_s.name, g3_t.name), writes=(sc_g.name,))
                xs = fmslice(x_t, NTOK, oc, t0, 512)
                p.v("tensor_add", xs, xs, sc_g[:, :],
                    reads=(x_t.name, sc_g.name), writes=(x_t.name,))
        emit_fm_store(p, x4_d, x_t, NTOK)
    p.barrier()
    return nc, p



def _core_blocks(c):
    pp = c % 4
    return c // 4, [pp, 7 - pp]


def _owner(g):
    pp = g if g < 4 else 7 - g
    return pp, (0 if g < 4 else 1)


_ROPE_THETAS = (10000.0) ** (np.arange(32) / 31.0)


def _rope_tables(c, scale):
    b, blks = _core_blocks(c)
    pos = np.concatenate([np.arange(g * 1024, (g + 1) * 1024) for g in blks])
    ang = pos[:, None] / _ROPE_THETAS[None, :]
    s, co = np.sin(ang).T, np.cos(ang).T
    A = np.concatenate([s, -s, s, -s], 0)
    B = np.concatenate([co, co, co, co], 0)
    return (np.stack([A, B]) * scale).astype(np.float32)


def kernel(**ins):
    import ml_dtypes
    x_full = np.asarray(ins["L0_attn_x"], np.float32)

    def x_core(c, arr):
        b, blks = _core_blocks(c)
        return np.ascontiguousarray(
            np.concatenate([arr[b, g * 1024:(g + 1) * 1024] for g in blks], 0).T
        ).astype(np.float32)

    wgt = np.ascontiguousarray(ins["L0_ffn_w_gate"].T)
    wut = np.ascontiguousarray(ins["L0_ffn_w_up"].T)
    wdt = np.ascontiguousarray(ins["L0_ffn_w_down"].T)
    in_a = [{"x": x_core(c, x_full), "wq": ins["L0_attn_wq"], "wk": ins["L0_attn_wk"],
             "wv": ins["L0_attn_wv"], "wgt": wgt, "wut": wut, "wdt": wdt}
            for c in range(8)]
    res_a = run_launch(build_l0a(), in_a)

    allst = {0: np.zeros((16, 128, 520), np.float32),
             1: np.zeros((16, 128, 520), np.float32)}
    for c in range(8):
        b, blks = _core_blocks(c)
        st = np.asarray(res_a.results[c]["states"])
        for ob, g in enumerate(blks):
            par = np.zeros((128, 520), np.float32)
            for j in range(8):
                par[0:64, j * 65:(j + 1) * 65] = st[ob, 2 * j]
                par[64:128, j * 65:(j + 1) * 65] = st[ob, 2 * j + 1]
            allst[b][g] = par
    win = np.ascontiguousarray(ins["L1_attn_W_in"].T)
    b_in = np.asarray(ins["L1_attn_b_in"]).reshape(-1, 1).astype(np.float32)
    bv_row = np.asarray(ins["L1_attn_b_in"])[2048:].reshape(1, 1024).astype(np.float32)
    qall = np.ascontiguousarray(np.asarray(ins["L1_ffn_q_proj"]).transpose(1, 0, 2).reshape(1024, 1024))
    kall = np.ascontiguousarray(np.asarray(ins["L1_ffn_k_proj"]).transpose(1, 0, 2).reshape(1024, 1024))
    vall = np.ascontiguousarray(np.asarray(ins["L1_ffn_v_proj"]).transpose(1, 0, 2).reshape(1024, 1024))
    in_b = []
    for c in range(8):
        b, blks = _core_blocks(c)
        pm = np.zeros((2, 16, 128, 1), np.float32)
        for ob, g in enumerate(blks):
            pm[ob, 0:g] = 1.0
        r = res_a.results[c]
        in_b.append({"x": in_a[c]["x"], "g0": r["g0"], "acc": r["acc"],
                     "qelu": r["qelu"], "allst": allst[b], "pmask": pm,
                     "wo": ins["L0_attn_wo"], "win": win, "bin": b_in,
                     "bv_row": bv_row, "ropeq": _rope_tables(c, 0.125),
                     "ropek": _rope_tables(c, 1.0),
                     "sim": ins["L1_ffn_sim_matrix"],
                     "gates": np.asarray(ins["L1_ffn_gates"]).reshape(1, 16).astype(np.float32),
                     "qall": qall, "kall": kall, "vall": vall})
    res_b = run_launch(build_l0b(), in_b)

    def gather_global(key, shape, tokrows):
        out = {}
        for b in range(2):
            buf = np.zeros(shape, ml_dtypes.bfloat16)
            for g in range(8):
                pp, ob = _owner(g)
                r = res_b.results[b * 4 + pp]
                if tokrows:
                    buf[g * 1024:(g + 1) * 1024] = r[key][ob * 1024:(ob + 1) * 1024]
                else:
                    buf[:, g * 1024:(g + 1) * 1024] = r[key][:, ob * 1024:(ob + 1) * 1024]
            out[b] = buf
        return out

    Kg = gather_global("k1", (1024, 8192), False)
    Qg = gather_global("q1", (1024, 8192), False)
    Vg = gather_global("v1", (8192, 1040), True)
    kmg = gather_global("km", (64, 8192), False)
    vmg = gather_global("vm", (8192, 65), True)
    diag = np.full((128, 1920), -30000.0, np.float32)
    for kl in range(128):
        diag[kl, kl + 896:] = 0.0
    blocked = np.full((128, 1920), -30000.0, np.float32)
    fullp = np.zeros((128, 1920), np.float32)
    in_f = []
    for c in range(8):
        b, blks = _core_blocks(c)
        p4 = c % 4
        hs = slice(4 * p4 * 64, 4 * (p4 + 1) * 64)
        ms = np.zeros((2, 8, 128, 1920), np.float32)
        for ob, g in enumerate(blks):
            for s in range(8):
                ms[ob, s] = fullp if s < g else (diag if s == g else blocked)
        in_f.append({"kh": np.ascontiguousarray(Kg[b][hs]),
                     "qh": np.ascontiguousarray(Qg[b][hs]),
                     "vh": np.ascontiguousarray(Vg[b][:, 65 * 4 * p4:65 * 4 * (p4 + 1)]),
                     "kmall": kmg[b], "vmall": vmg[b],
                     "qm": res_b.results[c]["qm"],
                     "mstrips": ms.astype(ml_dtypes.bfloat16)})
    res_f = run_launch(build_l1f(), in_f)

    oall = np.ascontiguousarray(np.asarray(ins["L1_ffn_o_proj"]).reshape(1024, 1024))
    woutT = np.ascontiguousarray(ins["L1_attn_W_out"].T)
    bout = np.asarray(ins["L1_attn_b_out"]).reshape(-1, 1).astype(np.float32)
    in_c = []
    for c in range(8):
        b, blks = _core_blocks(c)
        aoh = np.zeros((1024, 2048), ml_dtypes.bfloat16)
        for p4 in range(4):
            rr = res_f.results[b * 4 + p4]["ao"]
            for ob, g in enumerate(blks):
                aoh[p4 * 256:(p4 + 1) * 256, ob * 1024:(ob + 1) * 1024] = \
                    rr[:, g * 1024:(g + 1) * 1024]
        in_c.append({"x": res_b.results[c]["x1"], "aoh": aoh,
                     "fao": res_f.results[c]["fao"], "rw": res_b.results[c]["rw"],
                     "oall": oall, "woutT": woutT, "bout": bout,
                     "wq2": ins["L2_attn_Wq"], "wk2": ins["L2_attn_Wk"],
                     "wv2": ins["L2_attn_Wv"], "gateW": ins["L2_ffn_gate_W"]})
    res_c = run_launch(build_l1c(), in_c)

    b1c = np.asarray(ins["L2_ffn_b1"]).reshape(-1, 1).astype(np.float32)
    in_2f = [{"xn2": res_c.results[c]["xn2"], "comb": res_c.results[c]["comb2"],
              "sharedW": ins["L2_ffn_shared_W"],
              "sharedb": np.asarray(ins["L2_ffn_shared_b"]).reshape(-1, 1).astype(np.float32),
              "w1e": ins["L2_ffn_W1"], "b1c": b1c, "w2e": ins["L2_ffn_W2"],
              "b2r": ins["L2_ffn_b2"],
              "gamma": np.asarray(ins["L2_ffn_gamma"]).reshape(-1, 1).astype(np.float32),
              "beta": np.asarray(ins["L2_ffn_beta"]).reshape(-1, 1).astype(np.float32)}
             for c in range(8)]
    res_2f = run_launch(build_l2f(), in_2f)

    gate3 = np.ascontiguousarray(np.asarray(ins["L3_ffn_gate_w"]).T)
    in_2b = []
    for c in range(8):
        b, blks = _core_blocks(c)
        partall = np.stack([np.asarray(res_c.results[b * 4 + r]["part2"])
                            for r in range(4)])
        in_2b.append({"x": res_c.results[c]["x2"], "partall": partall,
                      "qsm": res_c.results[c]["qsm"], "g2": res_2f.results[c]["g2"],
                      "wo2": ins["L2_attn_Wo"],
                      "bo": np.asarray(ins["L2_attn_bo"]).reshape(-1, 1).astype(np.float32),
                      "g3g": np.asarray(ins["L3_attn_g"]).reshape(-1, 1).astype(np.float32),
                      "wqkv3": ins["L3_attn_Wqkv"], "gate3": gate3})
    res_2b = run_launch(build_l2b(), in_2b)

    sw1 = np.ascontiguousarray(ins["L3_ffn_shared_w1"].T)
    sw2 = np.ascontiguousarray(ins["L3_ffn_shared_w2"].T)
    w1e3 = np.ascontiguousarray(np.asarray(ins["L3_ffn_experts_w1"]).transpose(0, 2, 1))
    w2e3 = np.ascontiguousarray(np.asarray(ins["L3_ffn_experts_w2"]).transpose(0, 2, 1))
    in_3f = [{"xn3": res_2b.results[c]["xn3"], "comb": res_2b.results[c]["comb3"],
              "sw1": sw1, "sw2": sw2, "w1e": w1e3, "w2e": w2e3}
             for c in range(8)]
    res_3f = run_launch(build_l3f(), in_3f)

    in_3b = []
    for c in range(8):
        b, blks = _core_blocks(c)
        kb = np.zeros((1024, 256), ml_dtypes.bfloat16)
        vb = np.zeros((256, 1040), ml_dtypes.bfloat16)
        pmw = np.zeros((2, 128, 128), np.float32)
        for ob, g in enumerate(blks):
            if g == 0:
                pmw[ob] = -30000.0
            else:
                pmw[ob] = np.where(np.arange(128)[:, None] > np.arange(128)[None, :],
                                   0.0, -30000.0)
                pp, obp = _owner(g - 1)
                rp = res_2b.results[b * 4 + pp]
                kb[:, ob * 128:(ob + 1) * 128] = \
                    rp["k3"][:, obp * 1024 + 896: obp * 1024 + 1024]
                vb[ob * 128:(ob + 1) * 128] = \
                    rp["v3"][obp * 1024 + 896: obp * 1024 + 1024]
        in_3b.append({"x": res_2b.results[c]["x3"], "xn3": res_2b.results[c]["xn3"],
                      "g3": res_3f.results[c]["g3"], "q3": res_2b.results[c]["q3"],
                      "k3": res_2b.results[c]["k3"], "v3": res_2b.results[c]["v3"],
                      "kb": kb, "vb": vb, "pmask": pmw, "wo3": ins["L3_attn_Wo"]})
    res_3b = run_launch(build_l3b(), in_3b)

    out = np.zeros((2, 8192, 1024), np.float32)
    for c in range(8):
        b, blks = _core_blocks(c)
        x4 = np.asarray(res_3b.results[c]["x4"])
        for ob, g in enumerate(blks):
            out[b, g * 1024:(g + 1) * 1024] = x4[:, ob * 1024:(ob + 1) * 1024].T
    return out


# version 07 (final)
# speedup vs baseline: 1.0215x; 1.0215x over previous
import math
import numpy as np
from contextlib import ExitStack

import concourse.bass as bass
import concourse.mybir as mybir
from concourse.bass_utils import run_bass_kernel_spmd

F32 = mybir.dt.float32
BF16 = mybir.dt.bfloat16
I32 = mybir.dt.int32
AF = mybir.ActivationFunctionType
ALU = mybir.AluOpType
AX = mybir.AxisListType

D = 1024
NTOK = 2048
BLK = 1024
CH = 128
NCH = D // CH
NCORE = 8


_ENGS = ("sync", "gpsimd", "scalar", "vector", "tensor")


class Prog:

    SELF_RAW = True
    SELF_WAW = True

    def __init__(self, nc):
        self.nc = nc
        self.ops = []
        self.cum = {e: 0 for e in _ENGS}
        self.last_write = {}
        self.readers = {}
        self._barrier = {}

    def barrier(self):
        snap = dict(self.cum)
        for e in _ENGS:
            self._barrier[e] = snap

    def op(self, eng, fn, reads=(), writes=(), inc=1):
        waits = {}
        bar = self._barrier.pop(eng, None)
        if bar:
            for peng, pval in bar.items():
                if peng != eng and pval > 0:
                    waits[peng] = max(waits.get(peng, 0), pval)

        def add_wait(peng, pval):
            waits[peng] = max(waits.get(peng, 0), pval)

        for key in reads:
            lw = self.last_write.get(key)
            if lw is not None:
                peng, pval = lw
                if peng != eng or self.SELF_RAW:
                    add_wait(peng, pval)
        for key in writes:
            lw = self.last_write.get(key)
            if lw is not None and (lw[0] != eng or self.SELF_WAW):
                add_wait(*lw)
            for reng, rval in self.readers.get(key, ()):
                if reng != eng:
                    add_wait(reng, rval)

        newval = self.cum[eng] + inc
        self.cum[eng] = newval
        self.ops.append((eng, fn, dict(waits), inc, newval))
        for key in reads:
            self.readers.setdefault(key, []).append((eng, newval))
        for key in writes:
            self.last_write[key] = (eng, newval)
            self.readers[key] = []

    def dma(self, eng, out, in_, reads=(), writes=(), **kw):
        self.op(eng, lambda e: e.dma_start(out=out, in_=in_, **kw),
                reads=reads, writes=writes, inc=16)

    def mm_group(self, mms, reads=(), writes=()):
        def fn(e):
            last = None
            for (o, l, r, st, sp) in mms:
                last = e.matmul(out=o, lhsT=l, rhs=r, start=st, stop=sp)
            return last
        self.op("tensor", fn, reads=reads, writes=writes, inc=1)

    def trans(self, out, in_, ident, reads=(), writes=()):
        self.op("tensor", lambda e: e.transpose(out=out, in_=in_, identity=ident),
                reads=reads, writes=writes, inc=1)

    def act(self, out, in_, func, bias=0.0, scale=1.0, accum_out=None,
            reads=(), writes=()):
        self.op("scalar",
                lambda e: e.activation(out=out, in_=in_, func=func, bias=bias,
                                       scale=scale, accum_out=accum_out),
                reads=reads, writes=writes, inc=1)

    def v(self, name, *args, reads=(), writes=(), eng="vector", **kw):
        self.op(eng, lambda e: getattr(e, name)(*args, **kw),
                reads=reads, writes=writes, inc=1)

    def build(self, block, final_waits=True):
        nc = self.nc
        sems = {}
        self._sem_stack = ExitStack()
        for e in _ENGS:
            if self.cum[e] > 0:
                sems[e] = self._sem_stack.enter_context(nc.semaphore("sem_" + e))
        per_eng = {e: [] for e in _ENGS}
        for (eng, fn, waits, inc, newval) in self.ops:
            per_eng[eng].append((fn, waits, inc, newval))

        def runner(eng_name):
            def run(eng):
                waited = {}
                for (fn, waits, inc, newval) in per_eng[eng_name]:
                    for peng, pval in waits.items():
                        if waited.get(peng, 0) >= pval:
                            continue
                        eng.wait_ge(sems[peng], pval)
                        waited[peng] = pval
                    ins = fn(eng)
                    ins.then_inc(sems[eng_name], inc)
                if final_waits and eng_name == "sync":
                    for peng in _ENGS:
                        if self.cum[peng] > 0 and peng != "sync":
                            eng.wait_ge(sems[peng], self.cum[peng])
                    if self.cum["sync"] > 0:
                        eng.wait_ge(sems["sync"], self.cum["sync"])
            return run

        if "sync" not in sems and final_waits:
            sems["sync"] = self._sem_stack.enter_context(nc.semaphore("sem_sync"))
        for e in _ENGS:
            if per_eng[e] or (e == "sync" and final_waits):
                getattr(block, e)(runner(e))


def ceil_div(a, b):
    return (a + b - 1) // b



def alloc(es, nc, name, shape, dtype):
    return es.enter_context(nc.sbuf_tensor(name, shape, dtype))


def palloc(es, nc, name, shape, dtype=F32):
    return es.enter_context(nc.psum_tensor(name, shape, dtype))


class Consts:

    def __init__(self, p, es):
        self.p, self.es, self.nc = p, es, p.nc
        self.loaded = {}

    def get(self, name, arr, dtype=None):
        if name in self.loaded:
            return self.loaded[name]
        nc = self.nc
        if dtype is BF16:
            import ml_dtypes
            arr = arr.astype(ml_dtypes.bfloat16)
        t = self.es.enter_context(nc.sbuf_tensor("c_" + name, list(arr.shape),
                  BF16 if dtype is BF16 else F32, side="right"))
        dr = nc.inline_tensor(np.ascontiguousarray(arr), "ci_" + name)
        self.p.dma("sync", t[:, :], dr.ap(), writes=(("c_" + name),))
        self.loaded[name] = t
        return t

    def ones_col(self):
        return self.get("ones_col", np.ones((128, 1), np.float32))

    def ones_row(self):
        return self.get("ones_row", np.ones((1, 128), np.float32))

    def ident_bf(self):
        return self.get("ident_bf", np.eye(128, dtype=np.float32), BF16)

    def ident_f32(self):
        return self.get("ident_f32", np.eye(128, dtype=np.float32))

    def colv(self, value, parts=128):
        name = f"cv_{parts}_{value!r}".replace("-", "m").replace(".", "_")
        if name in self.loaded:
            return self.loaded[name]
        t = self.es.enter_context(self.nc.sbuf_tensor(name, [parts, 1], F32, side="right"))
        self.p.v("memset", t[:, :], float(value), writes=(t.name,))
        self.loaded[name] = t
        return t



def fmslice(t, W, fc, t0, n):
    return t[:, fc * W + t0: fc * W + t0 + n]


def emit_ln(p, es, cst, x_t, xn_t, W, eps=1e-5, tag="", stats=None):
    nc = p.nc
    es = ExitStack()
    kx, kxn = x_t.name, xn_t.name
    sq = alloc(es, nc, f"ln_sq{tag}", [128, 512], F32)
    t1 = alloc(es, nc, f"ln_t1{tag}", [128, 512], F32)
    r_mean = alloc(es, nc, f"ln_rm{tag}", [1, 512], F32)
    r_var = alloc(es, nc, f"ln_rv{tag}", [1, 512], F32)
    r_rstd = alloc(es, nc, f"ln_rr{tag}", [1, 512], F32)
    ps_sx = palloc(es, nc, f"ln_psx{tag}", [1, 512])
    ps_sq = palloc(es, nc, f"ln_psq{tag}", [1, 512])
    ps_mb = palloc(es, nc, f"ln_pmb{tag}", [128, 512])
    ps_rb = palloc(es, nc, f"ln_prb{tag}", [128, 512])
    oc = cst.ones_col(); orow = cst.ones_row()
    for j in range(W // 512):
        t0 = j * 512
        mms_x, mms_q = [], []
        for fc in range(NCH):
            xs = fmslice(x_t, W, fc, t0, 512)
            p.act(sq[:, :], xs, AF.Square, reads=(kx,), writes=(sq.name,))
            p.mm_group([(ps_sx[:, :], oc[:, :], xs, fc == 0, fc == NCH - 1)],
                       reads=(kx, oc.name), writes=(ps_sx.name,))
            p.mm_group([(ps_sq[:, :], oc[:, :], sq[:, :], fc == 0, fc == NCH - 1)],
                       reads=(sq.name, oc.name), writes=(ps_sq.name,))
        p.v("tensor_scalar_mul", r_mean[:, :], ps_sx[:, :], 1.0 / D,
            reads=(ps_sx.name,), writes=(r_mean.name,))
        p.v("tensor_mul", r_var[:, :], r_mean[:, :], r_mean[:, :],
            reads=(r_mean.name,), writes=(r_var.name,))
        p.v("scalar_tensor_tensor", r_var[:, :], ps_sq[:, :], 1.0 / D,
            r_var[:, :], reads=(ps_sq.name, r_var.name), writes=(r_var.name,),
            op0=ALU.mult, op1=ALU.subtract)
        if stats is not None:
            mrow_t, vrow_t = stats[0], stats[1]
            p.v("tensor_copy", mrow_t[:, t0:t0 + 512], r_mean[:, :],
                reads=(r_mean.name,), writes=(mrow_t.name,))
            if vrow_t is not None:
                p.v("tensor_copy", vrow_t[:, t0:t0 + 512], r_var[:, :],
                    reads=(r_var.name,), writes=(vrow_t.name,))
        epsc = cst.colv(eps, 1)
        p.act(r_var[:, :], r_var[:, :], AF.Sqrt, bias=epsc[0:1, :],
              reads=(r_var.name, epsc.name), writes=(r_var.name,))
        p.v("reciprocal", r_rstd[:, :], r_var[:, :],
            reads=(r_var.name,), writes=(r_rstd.name,))
        if stats is not None and len(stats) == 3:
            p.v("tensor_copy", stats[2][:, t0:t0 + 512], r_rstd[:, :],
                reads=(r_rstd.name,), writes=(stats[2].name,))
        p.mm_group([(ps_mb[:, :], orow[:, :], r_mean[:, :], True, True)],
                   reads=(r_mean.name, orow.name), writes=(ps_mb.name,))
        p.mm_group([(ps_rb[:, :], orow[:, :], r_rstd[:, :], True, True)],
                   reads=(r_rstd.name, orow.name), writes=(ps_rb.name,))
        for fc in range(NCH):
            xs = fmslice(x_t, W, fc, t0, 512)
            xns = fmslice(xn_t, W, fc, t0, 512)
            p.v("tensor_sub", t1[:, :], xs, ps_mb[:, :],
                reads=(kx, ps_mb.name), writes=(t1.name,))
            p.v("tensor_mul", xns, t1[:, :], ps_rb[:, :],
                reads=(t1.name, ps_rb.name), writes=(kxn,))
    es.close()
    p.barrier()


class WBatch:

    def __init__(self, p, es, name, max_cols, dtype=F32, cvt_eng="vector",
                 double=True):
        self.p = p
        self.max_cols = max_cols
        self.cvt_eng = cvt_eng
        n = 2 if double else 1
        self.stage = [alloc(es, p.nc, f"wb_{name}{i}", [128, max_cols], dtype)
                      for i in range(n)]
        self.i = 0

    def load(self, dst_bf, dst_c0, dram2d, rows, cols, eng="sync"):
        st = self.stage[self.i]
        self.i = (self.i + 1) % len(self.stage)
        nch = rows // 128
        self.p.dma(eng, st[:, 0:nch * cols],
                   dram2d.rearrange("(k q) c -> q k c", q=128),
                   writes=(st.name,))
        self.p.v("tensor_copy", dst_bf[:, dst_c0:dst_c0 + nch * cols],
                 st[:, 0:nch * cols], reads=(st.name,), writes=(dst_bf.name,),
                 eng=self.cvt_eng)

    def load_flat(self, dst_bf, dst_c0, dram2d, cols, eng="sync"):
        st = self.stage[self.i]
        self.i = (self.i + 1) % len(self.stage)
        self.p.dma(eng, st[:, 0:cols], dram2d, writes=(st.name,))
        self.p.v("tensor_copy", dst_bf[:, dst_c0:dst_c0 + cols],
                 st[:, 0:cols], reads=(st.name,), writes=(dst_bf.name,),
                 eng=self.cvt_eng)


class WStage:

    def __init__(self, p, es, name, max_cols):
        self.p, self.nc = p, p.nc
        self.stage = alloc(es, p.nc, f"wst_{name}", [128, max_cols], F32)

    def load(self, dst_bf, dst_c0, dram_ap, cols, dma_eng="sync"):
        self.p.dma(dma_eng, self.stage[:, 0:cols], dram_ap,
                   writes=(self.stage.name,))
        self.p.v("tensor_copy", dst_bf[:, dst_c0:dst_c0 + cols],
                 self.stage[:, 0:cols],
                 reads=(self.stage.name,), writes=(dst_bf.name,))


class WStageI:

    def __init__(self, p, es, name, max_cols):
        self.p, self.nc = p, p.nc
        self.stage = alloc(es, p.nc, f"wsi_{name}", [128, max_cols], I32)

    def load(self, dst_bf, dst_c0, dram_ap, cols, dma_eng="sync"):
        self.p.dma(dma_eng, self.stage[:, 0:cols], dram_ap,
                   writes=(self.stage.name,))
        self.p.v("tensor_copy", dst_bf[:, dst_c0:dst_c0 + cols],
                 self.stage[:, 0:cols],
                 reads=(self.stage.name,), writes=(dst_bf.name,))


def load_w_fm(p, es, cst, dram, name, rows, cols, stager=None, cls=None):
    nc = p.nc
    nr = rows // 128
    t = alloc(es, nc, f"w_{name}", [128, nr * cols], BF16)
    step = max(1, stager.max_cols // cols)
    for k0 in range(0, nr, step):
        k1 = min(nr, k0 + step)
        stager.load(t, k0 * cols, dram[k0 * 128:k1 * 128, 0:cols],
                    (k1 - k0) * 128, cols)
    return t



def emit_elu_evac(p, ps_ap, out_ap, sc_r, sc_m, kps, kout):
    p.act(sc_r[:, :], ps_ap, AF.Relu, reads=(kps,), writes=(sc_r.name,))
    p.v("tensor_sub", sc_m[:, :], ps_ap, sc_r[:, :],
        reads=(kps, sc_r.name), writes=(sc_m.name,))
    p.act(sc_m[:, :], sc_m[:, :], AF.Exp, reads=(sc_m.name,), writes=(sc_m.name,))
    p.v("tensor_add", out_ap, sc_m[:, :], sc_r[:, :],
        reads=(sc_m.name, sc_r.name), writes=(kout,))


def build_l0a(dbg=None):
    nc = bass.Bass("TRN2", target_bir_lowering=False, debug=False)
    HID = 3072
    NHC = HID // 128
    x_d = nc.dram_tensor("x", [D, NTOK], F32, kind="ExternalInput").ap()
    wq_d = nc.dram_tensor("wq", [D, D], F32, kind="ExternalInput").ap()
    wk_d = nc.dram_tensor("wk", [D, D], F32, kind="ExternalInput").ap()
    wv_d = nc.dram_tensor("wv", [D, D], F32, kind="ExternalInput").ap()
    wgt_d = nc.dram_tensor("wgt", [24, 128, D], I32, kind="ExternalInput").ap()
    wut_d = nc.dram_tensor("wut", [24, 128, D], I32, kind="ExternalInput").ap()
    wdt_d = nc.dram_tensor("wdt", [8, 128, HID], I32, kind="ExternalInput").ap()

    g0_d = nc.dram_tensor("g0", [D, NTOK], BF16, kind="ExternalOutput").ap()
    qelu_d = nc.dram_tensor("qelu", [2, D, BLK], BF16, kind="ExternalOutput").ap()
    states_d = nc.dram_tensor("states", [2, 16, 64, 65], F32, kind="ExternalOutput").ap()
    acc_d = nc.dram_tensor("acc", [2, 16, 65, BLK], BF16, kind="ExternalOutput").ap()

    p = Prog(nc)
    es0 = ExitStack()
    cst = Consts(p, es0)
    cst.ones_col(); cst.ones_row(); cst.ident_bf()
    mask_np = (np.arange(128)[:, None] <= np.arange(128)[None, :]).astype(np.float32)
    maskT = cst.get("l0mask", np.tile(mask_np, (1, 16)))
    xn = alloc(es0, nc, "xn", [128, NCH * NTOK], BF16)

    with ExitStack() as es1:
        x_t = alloc(es1, nc, "x_t", [128, NCH * NTOK], F32)
        for fc in range(NCH):
            p.dma("sync", x_t[:, fc * NTOK:(fc + 1) * NTOK],
                  x_d[fc * 128:(fc + 1) * 128, :], writes=(x_t.name,))
        emit_ln(p, es1, cst, x_t, xn, NTOK)
    p.barrier()
    if dbg == "ln":
        xn_d = nc.dram_tensor("xn_dbg", [D, NTOK], BF16, kind="ExternalOutput").ap()
        for fc in range(NCH):
            p.dma("gpsimd", xn_d[fc * 128:(fc + 1) * 128, :],
                  xn[:, fc * NTOK:(fc + 1) * NTOK], reads=(xn.name,))
        return nc, p

    with ExitStack() as es2:
        h_all = alloc(es2, nc, "h_all", [128, NHC * NTOK], BF16)
        wg_t = alloc(es2, nc, "wg_t", [128, NCH * 128], BF16)
        wu_t = alloc(es2, nc, "wu_t", [128, NCH * 128], BF16)
        sgate = alloc(es2, nc, "sgate", [128, 512], F32)
        sti = WBatch(p, es2, "ffn", 24 * 128, dtype=I32)
        qs = ["sync", "scalar"]
        ps_gs = [palloc(es2, nc, f"ps_g{i}", [128, 512]) for i in range(2)]
        ps_us = [palloc(es2, nc, f"ps_u{i}", [128, 512]) for i in range(2)]
        for hc in range(NHC):
            sti.load_flat(wg_t, 0, wgt_d[hc], D, eng=qs[hc % 2])
            sti.load_flat(wu_t, 0, wut_d[hc], D, eng=qs[(hc + 1) % 2])
            for tt in range(NTOK // 512):
                t0 = tt * 512
                ps_g, ps_u = ps_gs[tt % 2], ps_us[tt % 2]
                mg, mu = [], []
                for fc in range(NCH):
                    xs = fmslice(xn, NTOK, fc, t0, 512)
                    mg.append((ps_g[:, :], wg_t[:, fc * 128:(fc + 1) * 128], xs,
                               fc == 0, fc == NCH - 1))
                    mu.append((ps_u[:, :], wu_t[:, fc * 128:(fc + 1) * 128], xs,
                               fc == 0, fc == NCH - 1))
                p.mm_group(mg, reads=(wg_t.name, xn.name), writes=(ps_g.name,))
                p.mm_group(mu, reads=(wu_t.name, xn.name), writes=(ps_u.name,))
                p.act(sgate[:, :], ps_g[:, :], AF.Silu, scale=1.0 / 127.0,
                      reads=(ps_g.name,), writes=(sgate.name,))
                p.v("tensor_mul", h_all[:, hc * NTOK + t0: hc * NTOK + t0 + 512],
                    sgate[:, :], ps_u[:, :],
                    reads=(sgate.name, ps_u.name), writes=(h_all.name,))
        wd_t = alloc(es2, nc, "wd_t", [128, NHC * 128], BF16)
        g0t = alloc(es2, nc, "g0t", [128, 512], BF16)
        ps_os = [palloc(es2, nc, f"ps_o{i}", [128, 512]) for i in range(2)]
        for oc in range(NCH):
            sti.load_flat(wd_t, 0, wdt_d[oc], HID, eng=qs[oc % 2])
            for tt in range(NTOK // 512):
                t0 = tt * 512
                ps_o = ps_os[tt % 2]
                mo = [(ps_o[:, :], wd_t[:, hc * 128:(hc + 1) * 128],
                       h_all[:, hc * NTOK + t0: hc * NTOK + t0 + 512],
                       hc == 0, hc == NHC - 1) for hc in range(NHC)]
                p.mm_group(mo, reads=(wd_t.name, h_all.name), writes=(ps_o.name,))
                p.act(g0t[:, :], ps_o[:, :], AF.Silu, scale=1.0 / (127.0 * 127.0),
                      reads=(ps_o.name,), writes=(g0t.name,))
                p.dma("gpsimd", g0_d[oc * 128:(oc + 1) * 128, t0:t0 + 512],
                      g0t[:, :], reads=(g0t.name,))
    p.barrier()
    if dbg == "ffn":
        return nc, p
    import os as _os
    _skip = set((_os.environ.get("L0SKIP") or "").split(","))
    _nch_lim = int(_os.environ.get("L0NCHUNK") or 8)
    _nblk = int(_os.environ.get("L0NBLK") or 2)

    with ExitStack() as es3:
        wstg = WBatch(p, es3, "qkv", 4 * D, double=False)
        wq_t = load_w_fm(p, es3, cst, wq_d, "wq", D, D, stager=wstg)
        wk_t = load_w_fm(p, es3, cst, wk_d, "wk", D, D, stager=wstg)
        wv_t = load_w_fm(p, es3, cst, wv_d, "wv", D, D, stager=wstg)
        qelu_t = alloc(es3, nc, "qelu_t", [128, NCH * BLK], BF16)
        kelu_t = alloc(es3, nc, "kelu_t", [128, NCH * BLK], BF16)
        vaug = alloc(es3, nc, "vaug", [128, 1040], BF16)
        ktm = alloc(es3, nc, "ktm", [128, 1024], BF16)
        abf = alloc(es3, nc, "abf", [128, 2048], BF16)
        acc_sb = alloc(es3, nc, "acc_sb", [65, 2048], BF16)
        state_f = alloc(es3, nc, "state_f", [128, 520], F32)
        state_b = alloc(es3, nc, "state_b", [128, 520], BF16)
        sc_r = alloc(es3, nc, "sc_r", [128, 512], F32)
        sc_m = alloc(es3, nc, "sc_m", [128, 512], F32)
        big1 = palloc(es3, nc, "big1", [128, 2048])
        big2 = palloc(es3, nc, "big2", [128, 1536])
        ps_tr = palloc(es3, nc, "ps_tr", [128, 128], BF16)
        p.v("memset", vaug[:, :], 1.0, writes=(vaug.name,))

        for b in range(_nblk):
            tb = b * BLK
            for (w_t, dst) in ((wq_t, qelu_t), (wk_t, kelu_t)):
                for oc in range(NCH):
                    for hh in range(2):
                        t0 = tb + hh * 512
                        mm = [(big1[:, 0:512],
                               w_t[:, fc * D + oc * 128: fc * D + (oc + 1) * 128],
                               fmslice(xn, NTOK, fc, t0, 512),
                               fc == 0, fc == NCH - 1) for fc in range(NCH)]
                        p.mm_group(mm, reads=(w_t.name, xn.name), writes=(big1.name,))
                        emit_elu_evac(p, big1[:, 0:512],
                                      dst[:, oc * BLK + hh * 512: oc * BLK + (hh + 1) * 512],
                                      sc_r, sc_m, big1.name, dst.name)
            p.v("memset", state_f[:, :], 0.0, writes=(state_f.name,))
            p.v("memset", state_b[:, :], 0.0, writes=(state_b.name,))
            if dbg == "proj":
                for fc in range(NCH):
                    p.dma("gpsimd", qelu_d[b, fc * 128:(fc + 1) * 128, :],
                          qelu_t[:, fc * BLK:(fc + 1) * BLK], reads=(qelu_t.name,))
                continue
            HORDER = [h for h in range(16) if h % 2 == 0] + \
                     [h for h in range(16) if h % 2 == 1]
            acol = lambda h: (h % 2) * 1024 + (h // 2) * 128
            for c in range(_nch_lim):
                tl = tb + c * 128
                for hh in range(2 if "v" not in _skip else 0):
                    mm = [(big2[:, hh * 512:(hh + 1) * 512],
                           fmslice(xn, NTOK, fc, tl, 128),
                           wv_t[:, fc * D + hh * 512: fc * D + (hh + 1) * 512],
                           fc == 0, fc == NCH - 1) for fc in range(NCH)]
                    p.mm_group(mm, reads=(xn.name, wv_t.name), writes=(big2.name,))
                if "vaug" in _skip:
                    pass
                else:
                    p.v("tensor_copy",
                        vaug[:, :].rearrange("p (h e) -> p h e", e=65)[:, :, 0:64],
                        big2[:, 0:1024].rearrange("p (h d) -> p h d", d=64),
                        reads=(big2.name,), writes=(vaug.name,))
                for fc in range(NCH if "ktm" not in _skip else 0):
                    p.trans(ps_tr[:, :], kelu_t[:, fc * BLK + c * 128: fc * BLK + (c + 1) * 128],
                            cst.ident_bf()[:, :],
                            reads=(kelu_t.name, cst.ident_bf().name), writes=(ps_tr.name,))
                    p.v("tensor_copy", ktm[:, fc * 128:(fc + 1) * 128], ps_tr[:, :],
                        reads=(ps_tr.name,), writes=(ktm.name,))
                if "at" in _skip:
                    continue
                mm = []
                horder = [h for h in range(16) if h % 2 == 0] +                          [h for h in range(16) if h % 2 == 1]
                acol = lambda h: (h % 2) * 1024 + (h // 2) * 128
                for h in horder:
                    fc, r0 = h // 2, 64 * (h % 2)
                    lhs = kelu_t[r0:r0 + 64, (fc) * BLK + c * 128: (fc) * BLK + (c + 1) * 128]
                    rhs = qelu_t[r0:r0 + 64, (fc) * BLK + c * 128: (fc) * BLK + (c + 1) * 128]
                    mm.append((big1[:, acol(h): acol(h) + 128], lhs, rhs, True, True))
                p.mm_group(mm, reads=(kelu_t.name, qelu_t.name), writes=(big1.name,))
                if "atmask" not in _skip:
                    p.v("tensor_mul", abf[:, :], big1[:, :], maskT[:, :],
                        reads=(big1.name, maskT.name), writes=(abf.name,))
                if "out" in _skip:
                    continue
                mm = []
                for h in range(16):
                    fc, r0 = h // 2, 64 * (h % 2)
                    mm.append((big1[0:65, acol(h): acol(h) + 128],
                               vaug[:, h * 65:(h + 1) * 65],
                               abf[:, acol(h): acol(h) + 128], True, False))
                    mm.append((big1[0:65, acol(h): acol(h) + 128],
                               state_b[r0:r0 + 64, (h // 2) * 65:(h // 2) * 65 + 65],
                               qelu_t[r0:r0 + 64, fc * BLK + c * 128: fc * BLK + (c + 1) * 128],
                               False, True))
                p.mm_group(mm, reads=(vaug.name, abf.name, state_b.name, qelu_t.name),
                           writes=(big1.name,))
                p.v("tensor_copy", acc_sb[:, :], big1[0:65, :],
                    reads=(big1.name,), writes=(acc_sb.name,))
                if "accdma" not in _skip:
                    for par in (0, 1):
                        p.dma("gpsimd",
                              acc_d[b, par::2, :, c * 128:(c + 1) * 128].rearrange("j r q -> r j q"),
                              acc_sb[:, par * 1024:(par + 1) * 1024].rearrange("r (j q) -> r j q", q=128),
                              reads=(acc_sb.name,))
                if "st" in _skip:
                    continue
                mm = []
                for h in range(16):
                    r0 = 64 * (h % 2)
                    j = h // 2
                    mm.append((big2[r0:r0 + 64, j * 65:(j + 1) * 65],
                               ktm[:, h * 64:(h + 1) * 64],
                               vaug[:, h * 65:(h + 1) * 65], True, True))
                p.mm_group(mm, reads=(ktm.name, vaug.name), writes=(big2.name,))
                p.v("tensor_add", state_f[:, :], state_f[:, :], big2[:, 0:520],
                    reads=(state_f.name, big2.name), writes=(state_f.name,))
                p.v("tensor_copy", state_b[:, :], state_f[:, :],
                    reads=(state_f.name,), writes=(state_b.name,))
            if "stdma" in _skip:
                continue
            p.dma("gpsimd", states_d[b, 0::2].rearrange("h d e -> d h e"),
                  state_f[0:64, :].rearrange("d (j e) -> d j e", e=65),
                  reads=(state_f.name,))
            p.dma("gpsimd", states_d[b, 1::2].rearrange("h d e -> d h e"),
                  state_f[64:128, :].rearrange("d (j e) -> d j e", e=65),
                  reads=(state_f.name,))
            for fc in range(NCH):
                p.dma("gpsimd", qelu_d[b, fc * 128:(fc + 1) * 128, :],
                      qelu_t[:, fc * BLK:(fc + 1) * BLK], reads=(qelu_t.name,))
    p.barrier()
    return nc, p


def run_launch(nc_p, in_map_list, core_ids=None):
    nc, p = nc_p
    with nc.Block() as block:
        p.build(block)
    return run_bass_kernel_spmd(nc, in_map_list,
                                core_ids=core_ids or list(range(NCORE)))



def emit_fm_load(p, t, dram, W, name=None):
    for fc in range(NCH):
        p.dma("sync", t[:, fc * W:(fc + 1) * W], dram[fc * 128:(fc + 1) * 128, :],
              writes=(t.name,))


def emit_fm_store(p, dram, t, W, eng="gpsimd"):
    for fc in range(NCH):
        p.dma(eng, dram[fc * 128:(fc + 1) * 128, :], t[:, fc * W:(fc + 1) * W],
              reads=(t.name,))




def emit_l1ffn_gating(p, es, cst, nc, x_t, mrow, vrow, sim_d, gates_d,
                      rw_fm, rwb_fm, rw_d):
    simn = alloc(es, nc, "g_simn", [128, NCH * 16], F32)
    sim_t = alloc(es, nc, "g_sim", [128, NCH * 16], F32)
    for fc in range(NCH):
        p.dma("sync", sim_t[:, fc * 16:(fc + 1) * 16],
              sim_d[fc * 128:(fc + 1) * 128, :], writes=(sim_t.name,))
    sq16 = alloc(es, nc, "g_sq16", [128, 16], F32)
    row16a = alloc(es, nc, "g_r16", [1, 16], F32)
    negs1 = alloc(es, nc, "g_ns1", [1, 16], F32)
    rb16 = alloc(es, nc, "g_rb16", [128, 16], F32)
    grow = alloc(es, nc, "g_grow", [1, 16], F32)
    sgb = alloc(es, nc, "g_sgb", [128, 16], F32)
    normsq = alloc(es, nc, "g_nsq", [1, NTOK], F32)
    st17 = alloc(es, nc, "g_st17", [17, NTOK], F32)
    tm17 = alloc(es, nc, "g_tm17", [128, 17], F32)
    colA = alloc(es, nc, "g_colA", [128, 1], F32)
    colB = alloc(es, nc, "g_colB", [128, 1], F32)
    t16a = alloc(es, nc, "g_t16a", [128, 16], F32)
    t16b = alloc(es, nc, "g_t16b", [128, 16], F32)
    t16c = alloc(es, nc, "g_t16c", [128, 16], F32)
    eqa = alloc(es, nc, "g_eqa", [128, 16], F32)
    eqb = alloc(es, nc, "g_eqb", [128, 16], F32)
    esp = ExitStack()
    ps16 = palloc(esp, nc, "g_ps16", [1, 16])
    ps128 = palloc(esp, nc, "g_ps128", [128, 16])
    ps_l = palloc(esp, nc, "g_psl", [16, 512])
    ps_t17 = palloc(esp, nc, "g_pst17", [128, 17])
    ps_t16 = palloc(esp, nc, "g_pst16", [16, 128])
    for fc in range(NCH):
        p.act(sq16[:, :], sim_t[:, fc * 16:(fc + 1) * 16], AF.Square,
              reads=(sim_t.name,), writes=(sq16.name,))
        p.mm_group([(ps16[:, :], cst.ones_col()[:, :], sq16[:, :],
                     fc == 0, fc == NCH - 1)],
                   reads=(sq16.name, cst.ones_col().name), writes=(ps16.name,))
    p.act(row16a[:, :], ps16[:, :], AF.Sqrt, reads=(ps16.name,), writes=(row16a.name,))
    p.v("tensor_scalar_max", row16a[:, :], row16a[:, :], 1e-12,
        reads=(row16a.name,), writes=(row16a.name,))
    p.v("reciprocal", row16a[:, :], row16a[:, :],
        reads=(row16a.name,), writes=(row16a.name,))
    p.mm_group([(ps128[:, :], cst.ones_row()[:, :], row16a[:, :], True, True)],
               reads=(cst.ones_row().name, row16a.name), writes=(ps128.name,))
    p.v("tensor_copy", rb16[:, :], ps128[:, :], reads=(ps128.name,), writes=(rb16.name,))
    for fc in range(NCH):
        p.v("tensor_mul", simn[:, fc * 16:(fc + 1) * 16],
            sim_t[:, fc * 16:(fc + 1) * 16], rb16[:, :],
            reads=(sim_t.name, rb16.name), writes=(simn.name,))
    for fc in range(NCH):
        p.mm_group([(ps16[:, :], cst.ones_col()[:, :], simn[:, fc * 16:(fc + 1) * 16],
                     fc == 0, fc == NCH - 1)],
                   reads=(simn.name, cst.ones_col().name), writes=(ps16.name,))
    p.v("tensor_scalar_mul", negs1[:, :], ps16[:, :], -1.0,
        reads=(ps16.name,), writes=(negs1.name,))
    p.dma("sync", grow[:, :], gates_d, writes=(grow.name,))
    p.act(grow[:, :], grow[:, :], AF.Sigmoid, reads=(grow.name,), writes=(grow.name,))
    p.mm_group([(ps128[:, :], cst.ones_row()[:, :], grow[:, :], True, True)],
               reads=(cst.ones_row().name, grow.name), writes=(ps128.name,))
    p.v("tensor_copy", sgb[:, :], ps128[:, :], reads=(ps128.name,), writes=(sgb.name,))
    p.v("tensor_scalar_mul", normsq[:, :], vrow[:, :], float(D),
        reads=(vrow.name,), writes=(normsq.name,))
    for tt in range(NTOK // 512):
        t0 = tt * 512
        mm = [(ps_l[:, :], simn[:, fc * 16:(fc + 1) * 16],
               fmslice(x_t, NTOK, fc, t0, 512), fc == 0, False)
              for fc in range(NCH)]
        mm.append((ps_l[:, :], negs1[:, :], mrow[:, t0:t0 + 512], False, True))
        p.mm_group(mm, reads=(simn.name, x_t.name, negs1.name, mrow.name),
                   writes=(ps_l.name,))
        p.v("tensor_copy", st17[0:16, t0:t0 + 512], ps_l[:, :],
            reads=(ps_l.name,), writes=(st17.name,))
    p.dma("gpsimd", st17[16:17, :], normsq[:, :],
          reads=(normsq.name,), writes=(st17.name,))
    for tc in range(NTOK // 128):
        t0 = tc * 128
        p.trans(ps_t17[:, :], st17[:, t0:t0 + 128], cst.ident_f32()[0:17, 0:17],
                reads=(st17.name, cst.ident_f32().name), writes=(ps_t17.name,))
        p.v("tensor_copy", tm17[:, :], ps_t17[:, :],
            reads=(ps_t17.name,), writes=(tm17.name,))
        p.act(colA[:, :], tm17[:, 16:17], AF.Sqrt, reads=(tm17.name,), writes=(colA.name,))
        p.v("tensor_scalar_max", colA[:, :], colA[:, :], 1e-12,
            reads=(colA.name,), writes=(colA.name,))
        p.v("reciprocal", colA[:, :], colA[:, :], reads=(colA.name,), writes=(colA.name,))
        p.v("tensor_scalar_mul", t16a[:, :], tm17[:, 0:16], colA[:, :],
            reads=(tm17.name, colA.name), writes=(t16a.name,))
        p.v("tensor_sub", t16a[:, :], t16a[:, :], sgb[:, :],
            reads=(t16a.name, sgb.name), writes=(t16a.name,))
        p.v("tensor_relu", t16b[:, :], t16a[:, :],
            reads=(t16a.name,), writes=(t16b.name,))
        p.v("tensor_single_scalar", t16c[:, :], t16b[:, :], 0.0, op=ALU.is_gt,
            reads=(t16b.name,), writes=(t16c.name,))
        p.v("reduce_sum", colA[:, :], t16c[:, :], axis=AX.X,
            reads=(t16c.name,), writes=(colA.name,))
        p.v("tensor_single_scalar", colA[:, :], colA[:, :], 0.0, op=ALU.is_equal,
            reads=(colA.name,), writes=(colA.name,))
        p.v("reduce_max", colB[:, :], t16a[:, :], axis=AX.X,
            reads=(t16a.name,), writes=(colB.name,))
        p.v("tensor_single_scalar", eqa[:, :], t16a[:, :], colB[:, :], op=ALU.is_equal,
            reads=(t16a.name, colB.name), writes=(eqa.name,))
        p.v("scalar_tensor_tensor", eqb[:, :], eqa[:, :], -1e30, t16a[:, :],
            op0=ALU.mult, op1=ALU.add,
            reads=(eqa.name, t16a.name), writes=(eqb.name,))
        p.v("reduce_max", colB[:, :], eqb[:, :], axis=AX.X,
            reads=(eqb.name,), writes=(colB.name,))
        p.v("tensor_single_scalar", eqb[:, :], eqb[:, :], colB[:, :], op=ALU.is_equal,
            reads=(eqb.name, colB.name), writes=(eqb.name,))
        p.v("tensor_add", eqa[:, :], eqa[:, :], eqb[:, :],
            reads=(eqa.name, eqb.name), writes=(eqa.name,))
        p.v("tensor_scalar_mul", eqa[:, :], eqa[:, :], colA[:, :],
            reads=(eqa.name, colA.name), writes=(eqa.name,))
        p.v("tensor_max", t16c[:, :], t16c[:, :], eqa[:, :],
            reads=(t16c.name, eqa.name), writes=(t16c.name,))
        p.act(t16b[:, :], t16b[:, :], AF.Exp, reads=(t16b.name,), writes=(t16b.name,))
        p.v("tensor_mul", t16b[:, :], t16b[:, :], t16c[:, :],
            reads=(t16b.name, t16c.name), writes=(t16b.name,))
        p.v("reduce_sum", colA[:, :], t16b[:, :], axis=AX.X,
            reads=(t16b.name,), writes=(colA.name,))
        p.v("reciprocal", colA[:, :], colA[:, :], reads=(colA.name,), writes=(colA.name,))
        p.v("tensor_scalar_mul", t16b[:, :], t16b[:, :], colA[:, :],
            reads=(t16b.name, colA.name), writes=(t16b.name,))
        p.trans(ps_t16[:, :], t16b[:, :], cst.ident_f32()[:, :],
                reads=(t16b.name, cst.ident_f32().name), writes=(ps_t16.name,))
        p.v("tensor_copy", rw_fm[:, t0:t0 + 128], ps_t16[:, :],
            reads=(ps_t16.name,), writes=(rw_fm.name,))
    p.v("tensor_copy", rwb_fm[:, :], rw_fm[:, :],
        reads=(rw_fm.name,), writes=(rwb_fm.name,))
    for hh in range(NTOK // 512):
        p.dma("gpsimd", rw_d[:, hh * 512:(hh + 1) * 512],
              rw_fm[:, hh * 512:(hh + 1) * 512], reads=(rw_fm.name,))
    esp.close()
    p.barrier()


def build_l0b():
    nc = bass.Bass("TRN2", target_bir_lowering=False, debug=False)
    x_d = nc.dram_tensor("x", [D, NTOK], F32, kind="ExternalInput").ap()
    g0_d = nc.dram_tensor("g0", [D, NTOK], BF16, kind="ExternalInput").ap()
    acc_d = nc.dram_tensor("acc", [2, 16, 65, BLK], BF16, kind="ExternalInput").ap()
    qelu_d = nc.dram_tensor("qelu", [2, D, BLK], BF16, kind="ExternalInput").ap()
    allst_d = nc.dram_tensor("allst", [16, 128, 520], F32, kind="ExternalInput").ap()
    pmask_d = nc.dram_tensor("pmask", [2, 16, 128, 1], F32, kind="ExternalInput").ap()
    wo_d = nc.dram_tensor("wo", [D, D], F32, kind="ExternalInput").ap()
    win_d = nc.dram_tensor("win", [D, 3072], F32, kind="ExternalInput").ap()
    bin_d = nc.dram_tensor("bin", [3072, 1], F32, kind="ExternalInput").ap()
    bv_row_d = nc.dram_tensor("bv_row", [1, D], F32, kind="ExternalInput").ap()
    ropeq_d = nc.dram_tensor("ropeq", [2, 128, NTOK], F32, kind="ExternalInput").ap()
    ropek_d = nc.dram_tensor("ropek", [2, 128, NTOK], F32, kind="ExternalInput").ap()
    sim_d = nc.dram_tensor("sim", [D, 16], F32, kind="ExternalInput").ap()
    gates_d = nc.dram_tensor("gates", [1, 16], F32, kind="ExternalInput").ap()
    qall_d = nc.dram_tensor("qall", [D, D], F32, kind="ExternalInput").ap()
    kall_d = nc.dram_tensor("kall", [D, D], F32, kind="ExternalInput").ap()
    vall_d = nc.dram_tensor("vall", [D, D], F32, kind="ExternalInput").ap()

    x1_d = nc.dram_tensor("x1", [D, NTOK], F32, kind="ExternalOutput").ap()
    q1_d = nc.dram_tensor("q1", [D, NTOK], BF16, kind="ExternalOutput").ap()
    k1_d = nc.dram_tensor("k1", [D, NTOK], BF16, kind="ExternalOutput").ap()
    v1_d = nc.dram_tensor("v1", [NTOK, 1040], BF16, kind="ExternalOutput").ap()
    qm_d = nc.dram_tensor("qm", [64, NTOK], BF16, kind="ExternalOutput").ap()
    km_d = nc.dram_tensor("km", [64, NTOK], BF16, kind="ExternalOutput").ap()
    vm_d = nc.dram_tensor("vm", [NTOK, 65], BF16, kind="ExternalOutput").ap()
    rw_d = nc.dram_tensor("rw", [16, NTOK], F32, kind="ExternalOutput").ap()

    p = Prog(nc)
    es0 = ExitStack()
    cst = Consts(p, es0)
    cst.ones_col(); cst.ones_row(); cst.ident_bf(); cst.ident_f32()
    cst.colv(1e-5, 1); cst.colv(1e-6, 1)
    perm = np.zeros((128, 128), np.float32)
    for blk in range(2):
        for i in range(32):
            perm[blk * 64 + i, blk * 64 + 32 + i] = 1.0
            perm[blk * 64 + 32 + i, blk * 64 + i] = 1.0
    permT = cst.get("rope_perm", perm, BF16)
    ehot_np = np.zeros((16, 16 * 64), np.float32)
    for e in range(16):
        ehot_np[e, e * 64:(e + 1) * 64] = 1.0
    ehot = cst.get("ehot16", ehot_np, BF16)

    xn1 = alloc(es0, nc, "xn1", [128, NCH * NTOK], BF16)
    rw_fm = alloc(es0, nc, "rw_fm", [16, NTOK], F32)
    rwb_fm = alloc(es0, nc, "rwb_fm", [16, NTOK], BF16)
    es_mid = ExitStack()
    x_t = alloc(es_mid, nc, "x_t", [128, NCH * NTOK], F32)
    emit_fm_load(p, x_t, x_d, NTOK)

    attn_fm = alloc(es_mid, nc, "attn_fm", [128, NCH * NTOK], BF16)
    with ExitStack() as es1:
        pf_f = alloc(es1, nc, "pf_f", [128, 520], F32)
        pf_b = alloc(es1, nc, "pf_b", [128, 520], BF16)
        gst = alloc(es1, nc, "gst", [128, 520], F32)
        pm = alloc(es1, nc, "pm", [128, 1], F32)
        qb_t = alloc(es1, nc, "qb_t", [128, NCH * BLK], BF16)
        acc_t = alloc(es1, nc, "acc_t", [65, BLK], BF16)
        sc65 = alloc(es1, nc, "sc65", [65, BLK], F32)
        drow = alloc(es1, nc, "drow", [1, BLK], F32)
        amh = alloc(es1, nc, "amh", [64, BLK], BF16)
        ps2 = palloc(es1, nc, "ps2", [65, BLK])
        psb = palloc(es1, nc, "psb", [64, BLK])
        for ob in range(2):
            p.v("memset", pf_f[:, :], 0.0, writes=(pf_f.name,))
            for gb in range(16):
                p.dma("sync", gst[:, :], allst_d[gb], writes=(gst.name,))
                p.dma("sync", pm[:, :], pmask_d[ob, gb], writes=(pm.name,))
                p.v("scalar_tensor_tensor", pf_f[:, :], gst[:, :], pm[:, :],
                    pf_f[:, :], op0=ALU.mult, op1=ALU.add,
                    reads=(gst.name, pm.name, pf_f.name), writes=(pf_f.name,))
            p.v("tensor_copy", pf_b[:, :], pf_f[:, :],
                reads=(pf_f.name,), writes=(pf_b.name,))
            for fc in range(NCH):
                p.dma("sync", qb_t[:, fc * BLK:(fc + 1) * BLK],
                      qelu_d[ob, fc * 128:(fc + 1) * 128, :], writes=(qb_t.name,))
            for h in range(16):
                fc, r0 = h // 2, 64 * (h % 2)
                p.dma("sync", acc_t[:, :], acc_d[ob, h], writes=(acc_t.name,))
                mm = []
                for hh in range(2):
                    mm.append((ps2[:, hh * 512:(hh + 1) * 512],
                               pf_b[r0:r0 + 64, (h // 2) * 65:(h // 2) * 65 + 65],
                               qb_t[r0:r0 + 64, fc * BLK + hh * 512: fc * BLK + (hh + 1) * 512],
                               True, True))
                p.mm_group(mm, reads=(pf_b.name, qb_t.name), writes=(ps2.name,))
                p.v("tensor_add", sc65[:, :], acc_t[:, :], ps2[:, :],
                    reads=(acc_t.name, ps2.name), writes=(sc65.name,))
                p.v("tensor_scalar_add", drow[:, :], sc65[64:65, :], 1e-6,
                    reads=(sc65.name,), writes=(drow.name,))
                p.v("reciprocal", drow[:, :], drow[:, :],
                    reads=(drow.name,), writes=(drow.name,))
                mm = [(psb[:, hh * 512:(hh + 1) * 512],
                       cst.ones_row()[:, 0:64],
                       drow[:, hh * 512:(hh + 1) * 512], True, True)
                      for hh in range(2)]
                p.mm_group(mm, reads=(cst.ones_row().name, drow.name), writes=(psb.name,))
                p.v("tensor_mul", amh[:, :], sc65[0:64, :], psb[:, :],
                    reads=(sc65.name, psb.name), writes=(amh.name,))
                p.dma("gpsimd",
                      attn_fm[r0:r0 + 64, fc * NTOK + ob * BLK: fc * NTOK + (ob + 1) * BLK],
                      amh[:, :], reads=(amh.name,), writes=(attn_fm.name,))
    p.barrier()

    with ExitStack() as es2:
        mrow = alloc(es2, nc, "mrow", [1, NTOK], F32)
        vrow = alloc(es2, nc, "vrow", [1, NTOK], F32)
        es2a = ExitStack()
        wstg = WBatch(p, es2a, "wo", 4 * D, double=False)
        wo_t = load_w_fm(p, es2a, cst, wo_d, "wo", D, D, stager=wstg)
        g0_t = alloc(es2a, nc, "g0_t", [128, 512], BF16)
        sc_a = alloc(es2a, nc, "sc_a", [128, 512], F32)
        ps_s = palloc(es2a, nc, "ps_s", [128, 512])
        for oc in range(NCH):
            for tt in range(NTOK // 512):
                t0 = tt * 512
                mm = [(ps_s[:, :], wo_t[:, fc * D + oc * 128: fc * D + (oc + 1) * 128],
                       fmslice(attn_fm, NTOK, fc, t0, 512), fc == 0, fc == NCH - 1)
                      for fc in range(NCH)]
                p.mm_group(mm, reads=(wo_t.name, attn_fm.name), writes=(ps_s.name,))
                p.dma("gpsimd", g0_t[:, :], g0_d[oc * 128:(oc + 1) * 128, t0:t0 + 512],
                      writes=(g0_t.name,))
                p.v("tensor_mul", sc_a[:, :], ps_s[:, :], g0_t[:, :],
                    reads=(ps_s.name, g0_t.name), writes=(sc_a.name,))
                xs = fmslice(x_t, NTOK, oc, t0, 512)
                p.v("tensor_add", xs, xs, sc_a[:, :],
                    reads=(x_t.name, sc_a.name), writes=(x_t.name,))
        emit_fm_store(p, x1_d, x_t, NTOK)
        es2a.close()
        p.barrier()
        emit_ln(p, es2, cst, x_t, xn1, NTOK, tag="1", stats=(mrow, vrow))
        emit_l1ffn_gating(p, es2, cst, nc, x_t, mrow, vrow, sim_d, gates_d,
                          rw_fm, rwb_fm, rw_d)
    es_mid.close()
    p.barrier()

    with ExitStack() as es3:
        wstg = WBatch(p, es3, "win", 3072, double=False)
        win_t = alloc(es3, nc, "win_t", [128, NCH * 3072], BF16)
        for fc in range(NCH):
            wstg.load(win_t, fc * 3072, win_d[fc * 128:(fc + 1) * 128, 0:3072],
                      128, 3072)
        bin_t = alloc(es3, nc, "bin_t", [128, 24], F32)
        for bc in range(24):
            p.dma("sync", bin_t[:, bc:bc + 1], bin_d[bc * 128:(bc + 1) * 128, :],
                  writes=(bin_t.name,))
        ropq = alloc(es3, nc, "ropq", [128, 2 * NTOK], F32)
        ropk = alloc(es3, nc, "ropk", [128, 2 * NTOK], F32)
        for i in range(2):
            p.dma("sync", ropq[:, i * NTOK:(i + 1) * NTOK], ropeq_d[i], writes=(ropq.name,))
            p.dma("sync", ropk[:, i * NTOK:(i + 1) * NTOK], ropek_d[i], writes=(ropk.name,))
        prj = alloc(es3, nc, "prj", [128, 512], BF16)
        sc_1 = alloc(es3, nc, "sc_1", [128, 512], F32)
        sc_2 = alloc(es3, nc, "sc_2", [128, 512], F32)
        qk_t = alloc(es3, nc, "qk_t", [128, 512], BF16)
        ps_p = palloc(es3, nc, "ps_p", [128, 512])
        ps_r = palloc(es3, nc, "ps_r", [128, 512])
        for (part, rop, out_d) in ((0, ropq, q1_d), (1, ropk, k1_d)):
            for oc in range(NCH):
                bc = part * 8 + oc
                for tt in range(NTOK // 512):
                    t0 = tt * 512
                    mm = [(ps_p[:, :],
                           win_t[:, fc * 3072 + part * 1024 + oc * 128:
                                 fc * 3072 + part * 1024 + (oc + 1) * 128],
                           fmslice(xn1, NTOK, fc, t0, 512), fc == 0, fc == NCH - 1)
                          for fc in range(NCH)]
                    p.mm_group(mm, reads=(win_t.name, xn1.name), writes=(ps_p.name,))
                    p.act(prj[:, :], ps_p[:, :], AF.Identity, bias=bin_t[:, bc:bc + 1],
                          reads=(ps_p.name, bin_t.name), writes=(prj.name,))
                    p.mm_group([(ps_r[:, :], permT[:, :], prj[:, :], True, True)],
                               reads=(permT.name, prj.name), writes=(ps_r.name,))
                    p.v("tensor_mul", sc_1[:, :], prj[:, :], rop[:, t0:t0 + 512],
                        reads=(prj.name, rop.name), writes=(sc_1.name,))
                    p.v("tensor_mul", sc_2[:, :], ps_r[:, :], rop[:, NTOK + t0:NTOK + t0 + 512],
                        reads=(ps_r.name, rop.name), writes=(sc_2.name,))
                    p.v("tensor_add", qk_t[:, :], sc_1[:, :], sc_2[:, :],
                        reads=(sc_1.name, sc_2.name), writes=(qk_t.name,))
                    p.dma("gpsimd", out_d[oc * 128:(oc + 1) * 128, t0:t0 + 512],
                          qk_t[:, :], reads=(qk_t.name,))
        vb_t = alloc(es3, nc, "vb_t", [128, D], F32)
        bvr = alloc(es3, nc, "bvr", [1, D], F32)
        p.dma("sync", bvr[:, :], bv_row_d, writes=(bvr.name,))
        ps_vb = palloc(es3, nc, "ps_vb", [128, 512])
        for hh in range(2):
            p.mm_group([(ps_vb[:, :], cst.ones_row()[:, :], bvr[:, hh * 512:(hh + 1) * 512],
                         True, True)], reads=(cst.ones_row().name, bvr.name),
                       writes=(ps_vb.name,))
            p.v("tensor_copy", vb_t[:, hh * 512:(hh + 1) * 512], ps_vb[:, :],
                reads=(ps_vb.name,), writes=(vb_t.name,))
        va_t = alloc(es3, nc, "va_t", [128, 1040], BF16)
        p.v("memset", va_t[:, :], 1.0, writes=(va_t.name,))
        ps_v = palloc(es3, nc, "ps_v", [128, 1024])
        for tc in range(NTOK // 128):
            t0 = tc * 128
            for hh in range(2):
                mm = [(ps_v[:, hh * 512:(hh + 1) * 512],
                       fmslice(xn1, NTOK, fc, t0, 128),
                       win_t[:, fc * 3072 + 2048 + hh * 512: fc * 3072 + 2048 + (hh + 1) * 512],
                       fc == 0, fc == NCH - 1) for fc in range(NCH)]
                p.mm_group(mm, reads=(xn1.name, win_t.name), writes=(ps_v.name,))
            p.v("tensor_add",
                va_t[:, :].rearrange("p (h e) -> p h e", e=65)[:, :, 0:64],
                ps_v[:, :].rearrange("p (h d) -> p h d", d=64),
                vb_t[:, :].rearrange("p (h d) -> p h d", d=64),
                reads=(ps_v.name, vb_t.name), writes=(va_t.name,))
            p.dma("gpsimd", v1_d[t0:t0 + 128, :], va_t[:, :], reads=(va_t.name,))
    p.barrier()

    with ExitStack() as es4:
        wstg2 = WBatch(p, es4, "eproj", 4 * D, double=False)
        pe_t = alloc(es4, nc, "pe_t", [64, 16 * NTOK], BF16)
        w_t = alloc(es4, nc, "w_t", [128, NCH * D], BF16)
        macc = alloc(es4, nc, "macc", [64, NTOK], F32)
        tmix = alloc(es4, nc, "tmix", [64, 512], F32)
        mix_b = alloc(es4, nc, "mix_b", [64, NTOK], BF16)
        vtile = alloc(es4, nc, "vtile", [128, 65], BF16)
        p.v("memset", vtile[:, :], 1.0, writes=(vtile.name,))
        ps_e = palloc(es4, nc, "ps_e", [128, 512])
        ps_rbm = palloc(es4, nc, "ps_rbm", [64, 512])
        ps_vt = palloc(es4, nc, "ps_vt", [128, 64], BF16)
        for (pi, w_d, out_d) in ((0, qall_d, qm_d), (1, kall_d, km_d), (2, vall_d, vm_d)):
            wstg2.load(w_t, 0, w_d[0:512, 0:D], 512, D)
            wstg2.load(w_t, 4 * D, w_d[512:D, 0:D], 512, D)
            for e in range(16):
                for tt in range(NTOK // 512):
                    t0 = tt * 512
                    mm = [(ps_e[0:64, :], w_t[:, fc * D + e * 64: fc * D + (e + 1) * 64],
                           fmslice(xn1, NTOK, fc, t0, 512), fc == 0, fc == NCH - 1)
                          for fc in range(NCH)]
                    p.mm_group(mm, reads=(w_t.name, xn1.name), writes=(ps_e.name,))
                    p.v("tensor_copy", pe_t[:, e * NTOK + t0: e * NTOK + t0 + 512],
                        ps_e[0:64, :], reads=(ps_e.name,), writes=(pe_t.name,))
            for tt in range(NTOK // 512):
                t0 = tt * 512
                for e in range(16):
                    p.mm_group([(ps_rbm[:, :], ehot[:, e * 64:(e + 1) * 64],
                                 rwb_fm[:, t0:t0 + 512], True, True)],
                               reads=(ehot.name, rwb_fm.name), writes=(ps_rbm.name,))
                    src = pe_t[:, e * NTOK + t0: e * NTOK + t0 + 512]
                    if e == 0:
                        p.v("tensor_mul", macc[:, t0:t0 + 512], src, ps_rbm[:, :],
                            reads=(pe_t.name, ps_rbm.name), writes=(macc.name,))
                    else:
                        p.v("tensor_mul", tmix[:, :], src, ps_rbm[:, :],
                            reads=(pe_t.name, ps_rbm.name), writes=(tmix.name,))
                        p.v("tensor_add", macc[:, t0:t0 + 512], macc[:, t0:t0 + 512],
                            tmix[:, :], reads=(macc.name, tmix.name), writes=(macc.name,))
            if pi == 0:
                p.v("tensor_scalar_mul", mix_b[:, :], macc[:, :], 0.125,
                    reads=(macc.name,), writes=(mix_b.name,))
            else:
                p.v("tensor_copy", mix_b[:, :], macc[:, :],
                    reads=(macc.name,), writes=(mix_b.name,))
            if pi < 2:
                for hh in range(NTOK // 512):
                    p.dma("gpsimd", out_d[:, hh * 512:(hh + 1) * 512],
                          mix_b[:, hh * 512:(hh + 1) * 512], reads=(mix_b.name,))
            else:
                for tc in range(NTOK // 128):
                    t0 = tc * 128
                    p.trans(ps_vt[:, :], mix_b[:, t0:t0 + 128],
                            cst.ident_bf()[0:64, 0:64],
                            reads=(mix_b.name, cst.ident_bf().name), writes=(ps_vt.name,))
                    p.v("tensor_copy", vtile[:, 0:64], ps_vt[:, 0:64],
                        reads=(ps_vt.name,), writes=(vtile.name,))
                    p.dma("gpsimd", out_d[t0:t0 + 128, :], vtile[:, :],
                          reads=(vtile.name,))
    p.barrier()
    return nc, p



def build_l1f():
    nc = bass.Bass("TRN2", target_bir_lowering=False, debug=False)
    S = 8192
    kh_d = nc.dram_tensor("kh", [256, S], BF16, kind="ExternalInput").ap()
    qh_d = nc.dram_tensor("qh", [256, S], BF16, kind="ExternalInput").ap()
    vh_d = nc.dram_tensor("vh", [S, 4 * 65], BF16, kind="ExternalInput").ap()
    kmall_d = nc.dram_tensor("kmall", [64, S], BF16, kind="ExternalInput").ap()
    vmall_d = nc.dram_tensor("vmall", [S, 65], BF16, kind="ExternalInput").ap()
    qm_d = nc.dram_tensor("qm", [64, NTOK], BF16, kind="ExternalInput").ap()
    mstrips_d = nc.dram_tensor("mstrips", [2, 8, 128, 1920], BF16, kind="ExternalInput").ap()

    ao_d = nc.dram_tensor("ao", [256, S], BF16, kind="ExternalOutput").ap()
    fao_d = nc.dram_tensor("fao", [64, NTOK], BF16, kind="ExternalOutput").ap()

    p = Prog(nc)
    es0 = ExitStack()
    cst = Consts(p, es0)
    cst.ones_col(); cst.ones_row()
    parc = np.zeros((128, 2), np.float32)
    parc[0:64, 0] = 1.0; parc[64:128, 1] = 1.0
    parcol = cst.get("parcol", parc)
    mast = np.full((128, 896), -30000.0, np.float32)
    for kl in range(128):
        mast[kl, kl + 384:] = 0.0
    master = cst.get("master", mast)

    kh_t = alloc(es0, nc, "kh_t", [128, 2 * S], BF16)
    qh_t = alloc(es0, nc, "qh_t", [128, 2 * S], BF16)
    vh_t = alloc(es0, nc, "vh_t", [128, (S // 128) * 260], BF16)
    for hp in range(2):
        p.dma("sync", kh_t[:, hp * S:(hp + 1) * S], kh_d[hp * 128:(hp + 1) * 128, :],
              writes=(kh_t.name,))
        p.dma("sync", qh_t[:, hp * S:(hp + 1) * S], qh_d[hp * 128:(hp + 1) * 128, :],
              writes=(qh_t.name,))
    for tc in range(S // 128):
        p.dma("sync", vh_t[:, tc * 260:(tc + 1) * 260], vh_d[tc * 128:(tc + 1) * 128, :],
              writes=(vh_t.name,))

    bias_col = alloc(es0, nc, "bias_col", [128, 4], F32)
    with ExitStack() as esb:
        sqb = alloc(esb, nc, "sqb", [128, S], F32)
        nrow = alloc(esb, nc, "nrow", [1, S], F32)
        mx = alloc(esb, nc, "mx", [1, 8], F32)
        ps_nb = palloc(esb, nc, "ps_nb", [1, 512])
        ps_bc = palloc(esb, nc, "ps_bc", [128, 4])
        for (src_t, off) in ((kh_t, 0), (qh_t, 4)):
            for hp in range(2):
                p.act(sqb[:, :], src_t[:, hp * S:(hp + 1) * S], AF.Square,
                      reads=(src_t.name,), writes=(sqb.name,))
                for par in range(2):
                    hb = hp * 2 + par
                    for ss in range(S // 512):
                        p.mm_group([(ps_nb[:, :], parcol[:, par:par + 1],
                                     sqb[:, ss * 512:(ss + 1) * 512], True, True)],
                                   reads=(parcol.name, sqb.name), writes=(ps_nb.name,))
                        if ss == 0:
                            p.v("reduce_max", mx[:, off + hb:off + hb + 1], ps_nb[:, :],
                                axis=AX.X, reads=(ps_nb.name,), writes=(mx.name,))
                        else:
                            p.v("reduce_max", nrow[:, 0:1], ps_nb[:, :], axis=AX.X,
                                reads=(ps_nb.name,), writes=(nrow.name,))
                            p.v("tensor_max", mx[:, off + hb:off + hb + 1],
                                mx[:, off + hb:off + hb + 1], nrow[:, 0:1],
                                reads=(mx.name, nrow.name), writes=(mx.name,))
        p.v("tensor_mul", mx[:, 0:4], mx[:, 0:4], mx[:, 4:8],
            reads=(mx.name,), writes=(mx.name,))
        p.act(mx[:, 0:4], mx[:, 0:4], AF.Sqrt, reads=(mx.name,), writes=(mx.name,))
        p.v("tensor_scalar_mul", mx[:, 0:4], mx[:, 0:4], -1.0,
            reads=(mx.name,), writes=(mx.name,))
        p.mm_group([(ps_bc[:, :], cst.ones_row()[:, :], mx[:, 0:4], True, True)],
                   reads=(cst.ones_row().name, mx.name), writes=(ps_bc.name,))
        p.v("tensor_copy", bias_col[:, :], ps_bc[:, :],
            reads=(ps_bc.name,), writes=(bias_col.name,))
    p.barrier()

    with ExitStack() as es1:
        pt = [alloc(es1, nc, f"pt{i}", [128, 512], BF16) for i in range(3)]
        sc64 = alloc(es1, nc, "sc64", [64, 512], F32)
        drow = alloc(es1, nc, "drow", [1, 512], F32)
        aot = alloc(es1, nc, "aot", [64, 512], BF16)
        ps_s = [palloc(es1, nc, f"ps_s{i}", [128, 512]) for i in range(3)]
        ps_o = palloc(es1, nc, "ps_o", [65, 512])
        ps_b = palloc(es1, nc, "ps_b", [64, 512])

        def emit_qk(hb, qt, j):
            hp, par = hb // 2, hb % 2
            r0 = 64 * par
            q_ap = qh_t[r0:r0 + 64, hp * S + qt * 512: hp * S + (qt + 1) * 512]
            k_ap = kh_t[r0:r0 + 64, hp * S + j * 128: hp * S + (j + 1) * 128]
            psx = ps_s[j % 3]
            p.mm_group([(psx[:, :], k_ap, q_ap, True, True)],
                       reads=(kh_t.name, qh_t.name), writes=(psx.name,))

        def emit_softmax_pv(hb, qt, j, nchunks):
            psx = ps_s[j % 3]
            ptx = pt[j % 3]
            jp = j - 4 * qt
            if jp >= 0:
                off = 384 - 128 * jp
                p.v("tensor_add", psx[:, :], psx[:, :], master[:, off:off + 512],
                    reads=(psx.name, master.name), writes=(psx.name,))
            p.act(ptx[:, :], psx[:, :], AF.Exp, bias=bias_col[:, hb:hb + 1],
                  reads=(psx.name, bias_col.name), writes=(ptx.name,))
            p.mm_group([(ps_o[:, :], vh_t[:, j * 260 + 65 * hb: j * 260 + 65 * hb + 65],
                         ptx[:, :], j == 0, j == nchunks - 1)],
                       reads=(vh_t.name, ptx.name), writes=(ps_o.name,))

        for hb in range(4):
            for qt in range(16):
                nchunks = 4 * qt + 4
                emit_qk(hb, qt, 0)
                if nchunks > 1:
                    emit_qk(hb, qt, 1)
                for j in range(nchunks):
                    if j + 2 < nchunks:
                        emit_qk(hb, qt, j + 2)
                    emit_softmax_pv(hb, qt, j, nchunks)
                p.v("reciprocal", drow[:, :], ps_o[64:65, :],
                    reads=(ps_o.name,), writes=(drow.name,))
                p.mm_group([(ps_b[:, :], cst.ones_row()[:, 0:64], drow[:, :], True, True)],
                           reads=(cst.ones_row().name, drow.name), writes=(ps_b.name,))
                p.v("tensor_copy", sc64[:, :], ps_o[0:64, :],
                    reads=(ps_o.name,), writes=(sc64.name,))
                p.v("tensor_mul", aot[:, :], sc64[:, :], ps_b[:, :],
                    reads=(sc64.name, ps_b.name), writes=(aot.name,))
                p.dma("gpsimd", ao_d[hb * 64:(hb + 1) * 64, qt * 512:(qt + 1) * 512],
                      aot[:, :], reads=(aot.name,))
    p.barrier()

    with ExitStack() as es2:
        km_t = alloc(es2, nc, "km_t", [64, S], BF16)
        vm_t = alloc(es2, nc, "vm_t", [128, (S // 128) * 65], BF16)
        qm_t = alloc(es2, nc, "qm_t", [64, NTOK], BF16)
        p.dma("sync", km_t[:, :], kmall_d, writes=(km_t.name,))
        p.dma("sync", qm_t[:, :], qm_d, writes=(qm_t.name,))
        for tc in range(S // 128):
            p.dma("sync", vm_t[:, tc * 65:(tc + 1) * 65], vmall_d[tc * 128:(tc + 1) * 128, :],
                  writes=(vm_t.name,))
        fb_col = alloc(es2, nc, "fb_col", [128, 1], F32)
        with ExitStack() as esb2:
            sqf = alloc(esb2, nc, "sqf", [64, S], F32)
            nrow2 = alloc(esb2, nc, "nrow2", [1, S], F32)
            mx2 = alloc(esb2, nc, "mx2", [1, 2], F32)
            ps_n2 = palloc(esb2, nc, "ps_n2", [1, 512])
            ps_b2 = palloc(esb2, nc, "ps_b2", [128, 1])
            for (src_t, W, idx) in ((km_t, S, 0), (qm_t, NTOK, 1)):
                p.act(sqf[:, 0:W], src_t[:, :], AF.Square,
                      reads=(src_t.name,), writes=(sqf.name,))
                for ss in range(W // 512):
                    p.mm_group([(ps_n2[:, :], cst.ones_col()[0:64, :],
                                 sqf[:, ss * 512:(ss + 1) * 512], True, True)],
                               reads=(cst.ones_col().name, sqf.name), writes=(ps_n2.name,))
                    if ss == 0:
                        p.v("reduce_max", mx2[:, idx:idx + 1], ps_n2[:, :], axis=AX.X,
                            reads=(ps_n2.name,), writes=(mx2.name,))
                    else:
                        p.v("reduce_max", nrow2[:, 0:1], ps_n2[:, :], axis=AX.X,
                            reads=(ps_n2.name,), writes=(nrow2.name,))
                        p.v("tensor_max", mx2[:, idx:idx + 1], mx2[:, idx:idx + 1],
                            nrow2[:, 0:1], reads=(mx2.name, nrow2.name), writes=(mx2.name,))
            p.v("tensor_mul", mx2[:, 0:1], mx2[:, 0:1], mx2[:, 1:2],
                reads=(mx2.name,), writes=(mx2.name,))
            p.act(mx2[:, 0:1], mx2[:, 0:1], AF.Sqrt, reads=(mx2.name,), writes=(mx2.name,))
            p.v("tensor_scalar_mul", mx2[:, 0:1], mx2[:, 0:1], -1.0,
                reads=(mx2.name,), writes=(mx2.name,))
            p.mm_group([(ps_b2[:, :], cst.ones_row()[:, :], mx2[:, 0:1], True, True)],
                       reads=(cst.ones_row().name, mx2.name), writes=(ps_b2.name,))
            p.v("tensor_copy", fb_col[:, :], ps_b2[:, :],
                reads=(ps_b2.name,), writes=(fb_col.name,))
        p.barrier()
        strip = [alloc(es2, nc, f"strip{i}", [128, 1920], BF16) for i in range(2)]
        ptf = [alloc(es2, nc, f"ptf{i}", [128, 512], BF16) for i in range(3)]
        sc64f = alloc(es2, nc, "sc64f", [64, 512], F32)
        drowf = alloc(es2, nc, "drowf", [1, 512], F32)
        faot = alloc(es2, nc, "faot", [64, 512], BF16)
        ps_sf = [palloc(es2, nc, f"ps_sf{i}", [128, 512]) for i in range(3)]
        ps_of = palloc(es2, nc, "ps_of", [65, 512])
        ps_bf = palloc(es2, nc, "ps_bf", [64, 512])
        for ob in range(2):
            for qt in range(2):
                q_ap = qm_t[:, ob * BLK + qt * 512: ob * BLK + (qt + 1) * 512]

                def emit_qk_f(idx):
                    s, kc = idx // 8, idx % 8
                    if kc == 0:
                        p.dma("gpsimd", strip[s % 2][:, :], mstrips_d[ob, s],
                              writes=(strip[s % 2].name,))
                    k_ap = km_t[:, s * 1024 + kc * 128: s * 1024 + (kc + 1) * 128]
                    psx = ps_sf[idx % 3]
                    p.mm_group([(psx[:, :], k_ap, q_ap, True, True)],
                               reads=(km_t.name, qm_t.name), writes=(psx.name,))

                def emit_sm_pv_f(idx):
                    s, kc = idx // 8, idx % 8
                    psx = ps_sf[idx % 3]
                    ptx = ptf[idx % 3]
                    off = 896 - 128 * kc + 512 * qt
                    p.v("tensor_add", psx[:, :], psx[:, :],
                        strip[s % 2][:, off:off + 512],
                        reads=(psx.name, strip[s % 2].name), writes=(psx.name,))
                    p.act(ptx[:, :], psx[:, :], AF.Exp, bias=fb_col[:, 0:1],
                          reads=(psx.name, fb_col.name), writes=(ptx.name,))
                    tc = idx
                    p.mm_group([(ps_of[:, :], vm_t[:, tc * 65:(tc + 1) * 65],
                                 ptx[:, :], idx == 0, idx == 63)],
                               reads=(vm_t.name, ptx.name), writes=(ps_of.name,))

                emit_qk_f(0)
                emit_qk_f(1)
                for idx in range(64):
                    if idx + 2 < 64:
                        emit_qk_f(idx + 2)
                    emit_sm_pv_f(idx)
                p.v("reciprocal", drowf[:, :], ps_of[64:65, :],
                    reads=(ps_of.name,), writes=(drowf.name,))
                p.mm_group([(ps_bf[:, :], cst.ones_row()[:, 0:64], drowf[:, :], True, True)],
                           reads=(cst.ones_row().name, drowf.name), writes=(ps_bf.name,))
                p.v("tensor_copy", sc64f[:, :], ps_of[0:64, :],
                    reads=(ps_of.name,), writes=(sc64f.name,))
                p.v("tensor_mul", faot[:, :], sc64f[:, :], ps_bf[:, :],
                    reads=(sc64f.name, ps_bf.name), writes=(faot.name,))
                p.dma("gpsimd", fao_d[:, ob * BLK + qt * 512: ob * BLK + (qt + 1) * 512],
                      faot[:, :], reads=(faot.name,))
    p.barrier()
    return nc, p



def build_l1c():
    nc = bass.Bass("TRN2", target_bir_lowering=False, debug=False)
    x_d = nc.dram_tensor("x", [D, NTOK], F32, kind="ExternalInput").ap()
    aoh_d = nc.dram_tensor("aoh", [D, NTOK], BF16, kind="ExternalInput").ap()
    fao_d = nc.dram_tensor("fao", [64, NTOK], BF16, kind="ExternalInput").ap()
    rw_d = nc.dram_tensor("rw", [16, NTOK], F32, kind="ExternalInput").ap()
    oall_d = nc.dram_tensor("oall", [D, D], F32, kind="ExternalInput").ap()
    woutT_d = nc.dram_tensor("woutT", [D, D], F32, kind="ExternalInput").ap()
    bout_d = nc.dram_tensor("bout", [D, 1], F32, kind="ExternalInput").ap()
    wq2_d = nc.dram_tensor("wq2", [D, 512], F32, kind="ExternalInput").ap()
    wk2_d = nc.dram_tensor("wk2", [D, 512], F32, kind="ExternalInput").ap()
    wv2_d = nc.dram_tensor("wv2", [D, 512], F32, kind="ExternalInput").ap()
    gateW_d = nc.dram_tensor("gateW", [D, 8], F32, kind="ExternalInput").ap()
    x2_d = nc.dram_tensor("x2", [D, NTOK], F32, kind="ExternalOutput").ap()
    part2_d = nc.dram_tensor("part2", [512, 66], F32, kind="ExternalOutput").ap()
    qsm_d = nc.dram_tensor("qsm", [512, NTOK], BF16, kind="ExternalOutput").ap()
    xn2_d = nc.dram_tensor("xn2", [D, NTOK], BF16, kind="ExternalOutput").ap()
    comb2_d = nc.dram_tensor("comb2", [8, NTOK], BF16, kind="ExternalOutput").ap()

    p = Prog(nc)
    es0 = ExitStack()
    cst = Consts(p, es0)
    cst.ones_col(); cst.ones_row(); cst.ident_bf(); cst.ident_f32()
    cst.colv(1e-5, 1)
    ehot_np = np.zeros((16, 16 * 64), np.float32)
    for e in range(16):
        ehot_np[e, e * 64:(e + 1) * 64] = 1.0
    ehot = cst.get("ehot16", ehot_np, BF16)
    ehot8_np = np.zeros((8, 8 * 128), np.float32)
    for e in range(8):
        ehot8_np[e, e * 128:(e + 1) * 128] = 1.0
    ehot8 = cst.get("ehot8", ehot8_np, BF16)

    xn2 = alloc(es0, nc, "xn2s", [128, NCH * NTOK], BF16)
    mrow2 = alloc(es0, nc, "mrow2", [1, NTOK], F32)
    rrow2 = alloc(es0, nc, "rrow2", [1, NTOK], F32)
    combine_bf = alloc(es0, nc, "combine_bf", [8, NTOK], BF16)

    es_mid = ExitStack()
    x_t = alloc(es_mid, nc, "x_t", [128, NCH * NTOK], F32)
    emit_fm_load(p, x_t, x_d, NTOK)
    with ExitStack() as es1:
        aoh_t = alloc(es1, nc, "aoh_t", [128, NCH * NTOK], BF16)
        emit_fm_load(p, aoh_t, aoh_d, NTOK)
        fao_t = alloc(es1, nc, "fao_t", [128, NTOK], BF16)
        p.dma("sync", fao_t[0:64, :], fao_d, writes=(fao_t.name,))
        p.dma("sync", fao_t[64:128, :], fao_d, writes=(fao_t.name,))
        rwb = alloc(es1, nc, "rwb", [16, NTOK], BF16)
        rwst = alloc(es1, nc, "rwst", [16, 512], F32)
        for tt in range(NTOK // 512):
            p.dma("sync", rwst[:, :], rw_d[:, tt * 512:(tt + 1) * 512],
                  writes=(rwst.name,))
            p.v("tensor_copy", rwb[:, tt * 512:(tt + 1) * 512], rwst[:, :],
                reads=(rwst.name,), writes=(rwb.name,))
        bout_t = alloc(es1, nc, "bout_t", [128, 8], F32)
        for fc in range(NCH):
            p.dma("sync", bout_t[:, fc:fc + 1], bout_d[fc * 128:(fc + 1) * 128, :],
                  writes=(bout_t.name,))
        wstg = WBatch(p, es1, "wout", NCH * 128, double=False)
        wout_c = alloc(es1, nc, "wout_c", [128, NCH * 128], BF16)
        oall_c = alloc(es1, nc, "oall_c", [128, NCH * 128], BF16)
        a_t = alloc(es1, nc, "a_t", [128, 8 * NTOK], BF16)
        ps_rb = palloc(es1, nc, "ps_rb", [128, 512])
        ps_s1 = palloc(es1, nc, "ps_s1", [128, 512])
        ps_g1 = palloc(es1, nc, "ps_g1", [128, 512])
        sc_s = alloc(es1, nc, "sc_s", [128, 512], F32)
        sc_g = alloc(es1, nc, "sc_g", [128, 512], F32)
        for tt in range(NTOK // 512):
            t0 = tt * 512
            for e in range(16):
                r0 = 64 * (e % 2)
                p.mm_group([(ps_rb[r0:r0 + 64, :], ehot[:, e * 64:(e + 1) * 64],
                             rwb[:, t0:t0 + 512], True, True)],
                           reads=(ehot.name, rwb.name), writes=(ps_rb.name,))
                asl = a_t[r0:r0 + 64, (e // 2) * NTOK + t0:(e // 2) * NTOK + t0 + 512]
                p.v("tensor_mul", asl,
                    fao_t[r0:r0 + 64, t0:t0 + 512], ps_rb[r0:r0 + 64, :],
                    reads=(fao_t.name, ps_rb.name), writes=(a_t.name,))
        for oc in range(NCH):
            wstg.load(wout_c, 0, woutT_d[0:D, oc * 128:(oc + 1) * 128], D, 128)
            wstg.load(oall_c, 0, oall_d[0:D, oc * 128:(oc + 1) * 128], D, 128)
            for tt in range(NTOK // 512):
                t0 = tt * 512
                mm = []
                for e in range(0, 16, 2):
                    mm.append((ps_g1[:, :],
                               oall_c[0:64, (e // 2) * 128:(e // 2) * 128 + 128],
                               a_t[0:64, (e // 2) * NTOK + t0:(e // 2) * NTOK + t0 + 512],
                               e == 0, False))
                for fc in range(NCH):
                    mm.append((ps_s1[:, :], wout_c[:, fc * 128:(fc + 1) * 128],
                               fmslice(aoh_t, NTOK, fc, t0, 512), fc == 0, fc == NCH - 1))
                for e in range(1, 16, 2):
                    mm.append((ps_g1[:, :],
                               oall_c[64:128, (e // 2) * 128:(e // 2) * 128 + 128],
                               a_t[64:128, (e // 2) * NTOK + t0:(e // 2) * NTOK + t0 + 512],
                               False, e == 15))
                p.mm_group(mm, reads=(oall_c.name, a_t.name, wout_c.name, aoh_t.name),
                           writes=(ps_g1.name, ps_s1.name))
                p.act(sc_g[:, :], ps_g1[:, :], AF.Silu,
                      reads=(ps_g1.name,), writes=(sc_g.name,))
                p.act(sc_s[:, :], ps_s1[:, :], AF.Identity, bias=bout_t[:, oc:oc + 1],
                      reads=(ps_s1.name, bout_t.name), writes=(sc_s.name,))
                p.v("tensor_mul", sc_s[:, :], sc_s[:, :], sc_g[:, :],
                    reads=(sc_s.name, sc_g.name), writes=(sc_s.name,))
                xs = fmslice(x_t, NTOK, oc, t0, 512)
                p.v("tensor_add", xs, xs, sc_s[:, :],
                    reads=(x_t.name, sc_s.name), writes=(x_t.name,))
        emit_fm_store(p, x2_d, x_t, NTOK)
    p.barrier()
    emit_ln(p, None, cst, x_t, xn2, NTOK, tag="2", stats=(mrow2, None, rrow2))

    with ExitStack() as esg:
        emit_topk_gating(p, esg, cst, nc, x_t, mrow2, rrow2, gateW_d, 8,
                         combine_bf, mode="l2")
    es_mid.close()
    p.barrier()

    with ExitStack() as es2:
        wstg = WBatch(p, es2, "wqkv2", NCH * 512, double=False)
        wq2_t = load_w_fm(p, es2, cst, wq2_d, "wq2", D, 512, stager=wstg)
        wk2_t = load_w_fm(p, es2, cst, wk2_d, "wk2", D, 512, stager=wstg)
        wv2_t = load_w_fm(p, es2, cst, wv2_d, "wv2", D, 512, stager=wstg)
        qsm_fm = alloc(es2, nc, "qsm_fm", [128, 4 * NTOK], BF16)
        qe = alloc(es2, nc, "qe", [128, 512], F32)
        qsm_t = alloc(es2, nc, "qsm_t", [128, 512], BF16)
        ncol = alloc(es2, nc, "ncol", [128, 2], F32)
        ps_q = palloc(es2, nc, "ps_q", [128, 512])
        ps_tq = palloc(es2, nc, "ps_tq", [128, 128], BF16)
        for tc in range(NTOK // 128):
            t0 = tc * 128
            mm = [(ps_q[:, :], fmslice(xn2, NTOK, fc, t0, 128),
                   wq2_t[:, fc * 512:(fc + 1) * 512], fc == 0, fc == NCH - 1)
                  for fc in range(NCH)]
            p.mm_group(mm, reads=(xn2.name, wq2_t.name), writes=(ps_q.name,))
            for h in range(8):
                hsl = slice(h * 64, (h + 1) * 64)
                p.v("reduce_max", ncol[:, 0:1], ps_q[:, hsl], axis=AX.X, negate=True,
                    reads=(ps_q.name,), writes=(ncol.name,))
                p.act(qe[:, hsl], ps_q[:, hsl], AF.Exp, bias=ncol[:, 0:1],
                      accum_out=ncol[:, 1:2],
                      reads=(ps_q.name, ncol.name), writes=(qe.name, ncol.name))
                p.v("reciprocal", ncol[:, 1:2], ncol[:, 1:2],
                    reads=(ncol.name,), writes=(ncol.name,))
                p.v("tensor_scalar", qsm_t[:, hsl], qe[:, hsl], ncol[:, 1:2], 0.125,
                    op0=ALU.mult, op1=ALU.mult,
                    reads=(qe.name, ncol.name), writes=(qsm_t.name,))
            for h in range(8):
                r0 = 64 * (h % 2)
                out_ps = ps_tq[r0:r0 + 64, 0:128]
                p.trans(out_ps, qsm_t[:, h * 64:(h + 1) * 64], cst.ident_bf()[:, :],
                        reads=(qsm_t.name, cst.ident_bf().name), writes=(ps_tq.name,))
                p.v("tensor_copy",
                    qsm_fm[r0:r0 + 64, (h // 2) * NTOK + t0:(h // 2) * NTOK + t0 + 128],
                    out_ps, reads=(ps_tq.name,), writes=(qsm_fm.name,))
        for c4 in range(4):
            p.dma("gpsimd", qsm_d[c4 * 128:(c4 + 1) * 128, :],
                  qsm_fm[:, c4 * NTOK:(c4 + 1) * NTOK], reads=(qsm_fm.name,))
        k_t = alloc(es2, nc, "k_t", [128, 4 * NTOK], F32)
        ekbf = alloc(es2, nc, "ekbf", [128, 4 * NTOK], BF16)
        mlocs = alloc(es2, nc, "mlocs", [128, 4], F32)
        zlocs = alloc(es2, nc, "zlocs", [128, 4], F32)
        negm = alloc(es2, nc, "negm", [128, 1], F32)
        ps_k = palloc(es2, nc, "ps_k", [128, 512])
        for c4 in range(4):
            for tt in range(NTOK // 512):
                t0 = tt * 512
                mm = [(ps_k[:, :], fmslice(xn2, NTOK, fc, t0, 512),
                       wk2_t[:, fc * 512 + c4 * 128: fc * 512 + (c4 + 1) * 128],
                       fc == 0, fc == NCH - 1) for fc in range(NCH)]
                mm = [(ps_k[:, :], wk2_t[:, fc * 512 + c4 * 128: fc * 512 + (c4 + 1) * 128],
                       fmslice(xn2, NTOK, fc, t0, 512), fc == 0, fc == NCH - 1)
                      for fc in range(NCH)]
                p.mm_group(mm, reads=(wk2_t.name, xn2.name), writes=(ps_k.name,))
                p.v("tensor_copy", k_t[:, c4 * NTOK + t0: c4 * NTOK + t0 + 512],
                    ps_k[:, :], reads=(ps_k.name,), writes=(k_t.name,))
            p.v("reduce_max", negm[:, 0:1], k_t[:, c4 * NTOK:(c4 + 1) * NTOK],
                axis=AX.X, negate=True, reads=(k_t.name,), writes=(negm.name,))
            p.v("tensor_scalar_mul", mlocs[:, c4:c4 + 1], negm[:, 0:1], -1.0,
                reads=(negm.name,), writes=(mlocs.name,))
            p.act(ekbf[:, c4 * NTOK:(c4 + 1) * NTOK], k_t[:, c4 * NTOK:(c4 + 1) * NTOK],
                  AF.Exp, bias=negm[:, 0:1], accum_out=zlocs[:, c4:c4 + 1],
                  reads=(k_t.name, negm.name), writes=(ekbf.name, zlocs.name))
        vtm = alloc(es2, nc, "vtm", [128, (NTOK // 128) * 512], BF16)
        ps_v = palloc(es2, nc, "ps_v", [128, 512])
        for tc in range(NTOK // 128):
            t0 = tc * 128
            mm = [(ps_v[:, :], fmslice(xn2, NTOK, fc, t0, 128),
                   wv2_t[:, fc * 512:(fc + 1) * 512], fc == 0, fc == NCH - 1)
                  for fc in range(NCH)]
            p.mm_group(mm, reads=(xn2.name, wv2_t.name), writes=(ps_v.name,))
            p.v("tensor_copy", vtm[:, tc * 512:(tc + 1) * 512], ps_v[:, :],
                reads=(ps_v.name,), writes=(vtm.name,))
        ektm = alloc(es2, nc, "ektm", [128, (NTOK // 128) * 512], BF16)
        ps_te = palloc(es2, nc, "ps_te", [128, 128], BF16)
        for c4 in range(4):
            for tc in range(NTOK // 128):
                p.trans(ps_te[:, :], ekbf[:, c4 * NTOK + tc * 128: c4 * NTOK + (tc + 1) * 128],
                        cst.ident_bf()[:, :],
                        reads=(ekbf.name, cst.ident_bf().name), writes=(ps_te.name,))
                p.v("tensor_copy", ektm[:, tc * 512 + c4 * 128: tc * 512 + (c4 + 1) * 128],
                    ps_te[:, :], reads=(ps_te.name,), writes=(ektm.name,))
        ctx_t = alloc(es2, nc, "ctx_t", [64, 512], F32)
        ps_cx = palloc(es2, nc, "ps_cx", [64, 512])
        mm = []
        for h in range(8):
            for tc in range(NTOK // 128):
                mm.append((ps_cx[:, h * 64:(h + 1) * 64],
                           ektm[:, tc * 512 + h * 64: tc * 512 + (h + 1) * 64],
                           vtm[:, tc * 512 + h * 64: tc * 512 + (h + 1) * 64],
                           tc == 0, tc == NTOK // 128 - 1))
        p.mm_group(mm, reads=(ektm.name, vtm.name), writes=(ps_cx.name,))
        p.v("tensor_copy", ctx_t[:, :], ps_cx[:, :], reads=(ps_cx.name,), writes=(ctx_t.name,))
        for h in range(8):
            p.dma("gpsimd", part2_d[h * 64:(h + 1) * 64, 0:64],
                  ctx_t[:, h * 64:(h + 1) * 64], reads=(ctx_t.name,))
        for c4 in range(4):
            p.dma("gpsimd", part2_d[c4 * 128:(c4 + 1) * 128, 64:65],
                  mlocs[:, c4:c4 + 1], reads=(mlocs.name,),
                  allow_slow_non_contiguous=True)
            p.dma("gpsimd", part2_d[c4 * 128:(c4 + 1) * 128, 65:66],
                  zlocs[:, c4:c4 + 1], reads=(zlocs.name,),
                  allow_slow_non_contiguous=True)
    p.barrier()

    emit_fm_store(p, xn2_d, xn2, NTOK)
    for hh in range(NTOK // 512):
        p.dma("gpsimd", comb2_d[:, hh * 512:(hh + 1) * 512],
              combine_bf[:, hh * 512:(hh + 1) * 512], reads=(combine_bf.name,))
    p.barrier()
    return nc, p


def emit_topk_gating(p, es, cst, nc, x_t, mrow, rrow, gateW_d, E, out_bf, mode):
    gw_t = alloc(es, nc, "tg_gw", [128, NCH * E], F32)
    for fc in range(NCH):
        p.dma("sync", gw_t[:, fc * E:(fc + 1) * E],
              gateW_d[fc * 128:(fc + 1) * 128, :], writes=(gw_t.name,))
    negg = alloc(es, nc, "tg_negg", [1, E], F32)
    stE = alloc(es, nc, "tg_stE", [E + 1, NTOK], F32)
    tmE = alloc(es, nc, "tg_tmE", [128, E + 1], F32)
    cA = alloc(es, nc, "tg_cA", [128, 1], F32)
    cB = alloc(es, nc, "tg_cB", [128, 1], F32)
    cC = alloc(es, nc, "tg_cC", [128, 1], F32)
    tEa = alloc(es, nc, "tg_tEa", [128, E], F32)
    tEb = alloc(es, nc, "tg_tEb", [128, E], F32)
    eqa = alloc(es, nc, "tg_eqa", [128, E], F32)
    eqb = alloc(es, nc, "tg_eqb", [128, E], F32)
    outf = alloc(es, nc, "tg_outf", [E, NTOK], F32)
    esp = ExitStack()
    ps_E = palloc(esp, nc, "tg_psE", [E, 512])
    ps_e1 = palloc(esp, nc, "tg_pse1", [1, E])
    ps_tE = palloc(esp, nc, "tg_pstE", [128, E + 1])
    ps_bE = palloc(esp, nc, "tg_psbE", [E, 128])
    for fc in range(NCH):
        p.mm_group([(ps_e1[:, :], cst.ones_col()[:, :], gw_t[:, fc * E:(fc + 1) * E],
                     fc == 0, fc == NCH - 1)],
                   reads=(cst.ones_col().name, gw_t.name), writes=(ps_e1.name,))
    p.v("tensor_scalar_mul", negg[:, :], ps_e1[:, :], -1.0,
        reads=(ps_e1.name,), writes=(negg.name,))
    for tt in range(NTOK // 512):
        t0 = tt * 512
        mm = [(ps_E[:, :], gw_t[:, fc * E:(fc + 1) * E],
               fmslice(x_t, NTOK, fc, t0, 512), fc == 0, False)
              for fc in range(NCH)]
        mm.append((ps_E[:, :], negg[:, :], mrow[:, t0:t0 + 512], False, True))
        p.mm_group(mm, reads=(gw_t.name, x_t.name, negg.name, mrow.name),
                   writes=(ps_E.name,))
        p.v("tensor_copy", stE[0:E, t0:t0 + 512], ps_E[:, :],
            reads=(ps_E.name,), writes=(stE.name,))
    p.dma("gpsimd", stE[E:E + 1, :], rrow[:, :], reads=(rrow.name,), writes=(stE.name,))
    for tc in range(NTOK // 128):
        t0 = tc * 128
        p.trans(ps_tE[:, :], stE[:, t0:t0 + 128], cst.ident_f32()[0:E + 1, 0:E + 1],
                reads=(stE.name, cst.ident_f32().name), writes=(ps_tE.name,))
        p.v("tensor_copy", tmE[:, :], ps_tE[:, :], reads=(ps_tE.name,), writes=(tmE.name,))
        p.v("tensor_scalar_mul", tEa[:, :], tmE[:, 0:E], tmE[:, E:E + 1],
            reads=(tmE.name,), writes=(tEa.name,))
        if mode == "l2":
            p.v("reduce_max", cA[:, :], tEa[:, :], axis=AX.X, negate=True,
                reads=(tEa.name,), writes=(cA.name,))
            p.act(tEb[:, :], tEa[:, :], AF.Exp, bias=cA[:, :], accum_out=cB[:, :],
                  reads=(tEa.name, cA.name), writes=(tEb.name, cB.name))
            p.v("reciprocal", cB[:, :], cB[:, :], reads=(cB.name,), writes=(cB.name,))
            p.v("tensor_scalar_mul", tEb[:, :], tEb[:, :], cB[:, :],
                reads=(tEb.name, cB.name), writes=(tEb.name,))
            p.v("reduce_max", cA[:, :], tEb[:, :], axis=AX.X,
                reads=(tEb.name,), writes=(cA.name,))
            p.v("tensor_single_scalar", eqa[:, :], tEb[:, :], cA[:, :], op=ALU.is_equal,
                reads=(tEb.name, cA.name), writes=(eqa.name,))
            p.v("scalar_tensor_tensor", eqb[:, :], eqa[:, :], -1e30, tEb[:, :],
                op0=ALU.mult, op1=ALU.add, reads=(eqa.name, tEb.name), writes=(eqb.name,))
            p.v("reduce_max", cB[:, :], eqb[:, :], axis=AX.X,
                reads=(eqb.name,), writes=(cB.name,))
            p.v("tensor_single_scalar", eqb[:, :], eqb[:, :], cB[:, :], op=ALU.is_equal,
                reads=(eqb.name, cB.name), writes=(eqb.name,))
            p.v("tensor_add", cC[:, :], cA[:, :], cB[:, :],
                reads=(cA.name, cB.name), writes=(cC.name,))
            p.v("reciprocal", cC[:, :], cC[:, :], reads=(cC.name,), writes=(cC.name,))
            p.v("tensor_add", eqa[:, :], eqa[:, :], eqb[:, :],
                reads=(eqa.name, eqb.name), writes=(eqa.name,))
            p.v("tensor_mul", eqa[:, :], eqa[:, :], tEb[:, :],
                reads=(eqa.name, tEb.name), writes=(eqa.name,))
            p.v("tensor_scalar_mul", eqa[:, :], eqa[:, :], cC[:, :],
                reads=(eqa.name, cC.name), writes=(eqa.name,))
        else:
            p.v("reduce_max", cA[:, :], tEa[:, :], axis=AX.X,
                reads=(tEa.name,), writes=(cA.name,))
            p.v("tensor_single_scalar", eqa[:, :], tEa[:, :], cA[:, :], op=ALU.is_equal,
                reads=(tEa.name, cA.name), writes=(eqa.name,))
            p.v("scalar_tensor_tensor", eqb[:, :], eqa[:, :], -1e30, tEa[:, :],
                op0=ALU.mult, op1=ALU.add, reads=(eqa.name, tEa.name), writes=(eqb.name,))
            p.v("reduce_max", cB[:, :], eqb[:, :], axis=AX.X,
                reads=(eqb.name,), writes=(cB.name,))
            p.v("tensor_single_scalar", eqb[:, :], eqb[:, :], cB[:, :], op=ALU.is_equal,
                reads=(eqb.name, cB.name), writes=(eqb.name,))
            p.v("tensor_sub", cC[:, :], cA[:, :], cB[:, :],
                reads=(cA.name, cB.name), writes=(cC.name,))
            p.act(cC[:, :], cC[:, :], AF.Sigmoid, reads=(cC.name,), writes=(cC.name,))
            p.v("tensor_scalar_mul", eqa[:, :], eqa[:, :], cC[:, :],
                reads=(eqa.name, cC.name), writes=(eqa.name,))
            p.v("tensor_scalar", cC[:, :], cC[:, :], -1.0, 1.0, op0=ALU.mult, op1=ALU.add,
                reads=(cC.name,), writes=(cC.name,))
            p.v("tensor_scalar_mul", eqb[:, :], eqb[:, :], cC[:, :],
                reads=(eqb.name, cC.name), writes=(eqb.name,))
            p.v("tensor_add", eqa[:, :], eqa[:, :], eqb[:, :],
                reads=(eqa.name, eqb.name), writes=(eqa.name,))
        p.trans(ps_bE[:, :], eqa[:, :], cst.ident_f32()[:, :],
                reads=(eqa.name, cst.ident_f32().name), writes=(ps_bE.name,))
        p.v("tensor_copy", outf[:, t0:t0 + 128], ps_bE[:, :],
            reads=(ps_bE.name,), writes=(outf.name,))
    p.v("tensor_copy", out_bf[:, :], outf[:, :], reads=(outf.name,), writes=(out_bf.name,))
    esp.close()
    p.barrier()



def emit_moe_experts(p, es, cst, nc, xn, comb_bf, w1e_d, w2e_d, acc,
                     act_fn, b1col_t=None, E=8, HID=4096):
    NHC = HID // 128
    ehot8 = cst.get("ehot8", None) if False else cst.loaded.get("ehot8")
    st = WBatch(p, es, "moe", 2048)
    w1t = alloc(es, nc, "moe_w1t", [128, NCH * 512], BF16)
    w2c = alloc(es, nc, "moe_w2c", [128, NHC * 128], BF16)
    h_t = alloc(es, nc, "moe_ht", [128, NHC * 1024], BF16)
    cbs = alloc(es, nc, "moe_cbs", [128, 1024], BF16)
    sc = alloc(es, nc, "moe_sc", [128, 512], F32)
    ps_h = palloc(es, nc, "moe_psh", [128, 512])
    ps_h2 = palloc(es, nc, "moe_psh2", [128, 512])
    ps_o = palloc(es, nc, "moe_pso", [128, 512])
    ps_o2 = palloc(es, nc, "moe_pso2", [128, 512])
    ps_cb = palloc(es, nc, "moe_pscb", [128, 512])
    qs = ["sync", "scalar"]
    for e in range(E):
        for tp in range(2):
            tbase = tp * 1024
            for tq in range(2):
                t0 = tbase + tq * 512
                p.mm_group([(ps_cb[:, :], ehot8[:, e * 128:(e + 1) * 128],
                             comb_bf[:, t0:t0 + 512], True, True)],
                           reads=(ehot8.name, comb_bf.name), writes=(ps_cb.name,))
                p.v("tensor_copy", cbs[:, tq * 512:(tq + 1) * 512], ps_cb[:, :],
                    reads=(ps_cb.name,), writes=(cbs.name,))
            for hg in range(HID // 512):
                st.load_flat(w1t, 0, w1e_d[e, hg, :, 0:2048], 2048, eng=qs[hg % 2])
                st.load_flat(w1t, 2048, w1e_d[e, hg, :, 2048:4096], 2048,
                             eng=qs[(hg + 1) % 2])
                for hc4 in range(4):
                    hc = hg * 4 + hc4
                    for tq in range(2):
                        t0 = tbase + tq * 512
                        psx = ps_h if (hc4 + tq) % 2 == 0 else ps_h2
                        mm = [(psx[:, :],
                               w1t[:, fc * 512 + hc4 * 128: fc * 512 + (hc4 + 1) * 128],
                               fmslice(xn, NTOK, fc, t0, 512), fc == 0, fc == NCH - 1)
                              for fc in range(NCH)]
                        p.mm_group(mm, reads=(w1t.name, xn.name), writes=(psx.name,))
                        if b1col_t is not None:
                            p.act(h_t[:, hc * 1024 + tq * 512: hc * 1024 + (tq + 1) * 512],
                                  psx[:, :], act_fn,
                                  bias=b1col_t[:, e * NHC + hc: e * NHC + hc + 1],
                                  reads=(psx.name, b1col_t.name), writes=(h_t.name,))
                        else:
                            p.act(h_t[:, hc * 1024 + tq * 512: hc * 1024 + (tq + 1) * 512],
                                  psx[:, :], act_fn,
                                  reads=(psx.name,), writes=(h_t.name,))
            for oc in range(NCH):
                st.load_flat(w2c, 0, w2e_d[e, oc, :, 0:2048], 2048, eng=qs[oc % 2])
                st.load_flat(w2c, 2048, w2e_d[e, oc, :, 2048:4096], 2048,
                             eng=qs[(oc + 1) % 2])
                for tq in range(2):
                    t0 = tbase + tq * 512
                    psx = ps_o if (oc + tq) % 2 == 0 else ps_o2
                    mm = [(psx[:, :], w2c[:, hc * 128:(hc + 1) * 128],
                           h_t[:, hc * 1024 + tq * 512: hc * 1024 + (tq + 1) * 512],
                           hc == 0, hc == NHC - 1) for hc in range(NHC)]
                    p.mm_group(mm, reads=(w2c.name, h_t.name), writes=(psx.name,))
                    p.v("tensor_mul", sc[:, :], psx[:, :], cbs[:, tq * 512:(tq + 1) * 512],
                        reads=(psx.name, cbs.name), writes=(sc.name,))
                    accs = fmslice(acc, NTOK, oc, t0, 512)
                    p.v("tensor_add", accs, accs, sc[:, :],
                        reads=(acc.name + f":{oc}", sc.name),
                        writes=(acc.name + f":{oc}",))



def build_l2f():
    nc = bass.Bass("TRN2", target_bir_lowering=False, debug=False)
    xn2_d = nc.dram_tensor("xn2", [D, NTOK], BF16, kind="ExternalInput").ap()
    comb_d = nc.dram_tensor("comb", [8, NTOK], BF16, kind="ExternalInput").ap()
    sharedW_d = nc.dram_tensor("sharedW", [D, D], F32, kind="ExternalInput").ap()
    sharedb_d = nc.dram_tensor("sharedb", [D, 1], F32, kind="ExternalInput").ap()
    w1e_d = nc.dram_tensor("w1e", [8, 8, 128, 4096], F32, kind="ExternalInput").ap()
    b1c_d = nc.dram_tensor("b1c", [8 * 4096, 1], F32, kind="ExternalInput").ap()
    w2e_d = nc.dram_tensor("w2e", [8, 8, 128, 4096], F32, kind="ExternalInput").ap()
    b2r_d = nc.dram_tensor("b2r", [8, D], F32, kind="ExternalInput").ap()
    gamma_d = nc.dram_tensor("gamma", [D, 1], F32, kind="ExternalInput").ap()
    beta_d = nc.dram_tensor("beta", [D, 1], F32, kind="ExternalInput").ap()
    g2_d = nc.dram_tensor("g2", [D, NTOK], BF16, kind="ExternalOutput").ap()

    p = Prog(nc)
    es0 = ExitStack()
    cst = Consts(p, es0)
    cst.ones_col(); cst.ones_row()
    cst.colv(1e-5, 1)
    ehot8_np = np.zeros((8, 8 * 128), np.float32)
    for e in range(8):
        ehot8_np[e, e * 128:(e + 1) * 128] = 1.0
    cst.get("ehot8", ehot8_np, BF16)

    xn2 = alloc(es0, nc, "xn2s", [128, NCH * NTOK], BF16)
    emit_fm_load(p, xn2, xn2_d, NTOK)
    comb_bf = alloc(es0, nc, "comb_bf", [8, NTOK], BF16)
    p.dma("sync", comb_bf[:, :], comb_d, writes=(comb_bf.name,))
    acc = alloc(es0, nc, "acc", [128, NCH * NTOK], F32)

    with ExitStack() as es1:
        wstg = WBatch(p, es1, "shared", 4 * D, double=False)
        shw_t = load_w_fm(p, es1, cst, sharedW_d, "sharedW", D, D, stager=wstg)
        shb_t = alloc(es1, nc, "shb_t", [128, 8], F32)
        for fc in range(NCH):
            p.dma("sync", shb_t[:, fc:fc + 1], sharedb_d[fc * 128:(fc + 1) * 128, :],
                  writes=(shb_t.name,))
        b2f = alloc(es1, nc, "b2f", [8, D], F32)
        b2t = alloc(es1, nc, "b2t", [8, D], BF16)
        p.dma("sync", b2f[:, :], b2r_d, writes=(b2f.name,))
        p.v("tensor_copy", b2t[:, :], b2f[:, :], reads=(b2f.name,), writes=(b2t.name,))
        gsc = alloc(es1, nc, "gsc", [128, 512], F32)
        ps_sh = palloc(es1, nc, "ps_sh", [128, 512])
        ps_bt = palloc(es1, nc, "ps_bt", [128, 512])
        for oc in range(NCH):
            for tt in range(NTOK // 512):
                t0 = tt * 512
                mm = [(ps_sh[:, :], shw_t[:, fc * D + oc * 128: fc * D + (oc + 1) * 128],
                       fmslice(xn2, NTOK, fc, t0, 512), fc == 0, fc == NCH - 1)
                      for fc in range(NCH)]
                p.mm_group(mm, reads=(shw_t.name, xn2.name), writes=(ps_sh.name,))
                p.act(gsc[:, :], ps_sh[:, :], AF.Gelu, bias=shb_t[:, oc:oc + 1],
                      reads=(ps_sh.name, shb_t.name), writes=(gsc.name,))
                p.mm_group([(ps_bt[:, :], b2t[:, oc * 128:(oc + 1) * 128],
                             comb_bf[:, t0:t0 + 512], True, True)],
                           reads=(b2t.name, comb_bf.name), writes=(ps_bt.name,))
                p.v("scalar_tensor_tensor", fmslice(acc, NTOK, oc, t0, 512),
                    gsc[:, :], 0.5, ps_bt[:, :], op0=ALU.mult, op1=ALU.add,
                    reads=(gsc.name, ps_bt.name), writes=(acc.name,))
    p.barrier()

    with ExitStack() as es2:
        b1col = alloc(es2, nc, "b1col", [128, 8 * 32], F32)
        for e in range(8):
            for hc in range(32):
                p.dma("sync", b1col[:, e * 32 + hc: e * 32 + hc + 1],
                      b1c_d[e * 4096 + hc * 128: e * 4096 + (hc + 1) * 128, :],
                      writes=(b1col.name,))
        emit_moe_experts(p, es2, cst, nc, xn2, comb_bf, w1e_d, w2e_d, acc,
                         AF.Gelu, b1col_t=b1col)
    p.barrier()

    with ExitStack() as es3:
        gam_t = alloc(es3, nc, "gam_t", [128, 8], F32)
        bet_t = alloc(es3, nc, "bet_t", [128, 8], F32)
        for fc in range(NCH):
            p.dma("sync", gam_t[:, fc:fc + 1], gamma_d[fc * 128:(fc + 1) * 128, :],
                  writes=(gam_t.name,))
            p.dma("sync", bet_t[:, fc:fc + 1], beta_d[fc * 128:(fc + 1) * 128, :],
                  writes=(bet_t.name,))
        sq = alloc(es3, nc, "lnf_sq", [128, 512], F32)
        t1 = alloc(es3, nc, "lnf_t1", [128, 512], F32)
        t2 = alloc(es3, nc, "lnf_t2", [128, 512], F32)
        g2t = alloc(es3, nc, "g2t", [128, 512], BF16)
        r_mean = alloc(es3, nc, "lnf_rm", [1, 512], F32)
        r_var = alloc(es3, nc, "lnf_rv", [1, 512], F32)
        r_rstd = alloc(es3, nc, "lnf_rr", [1, 512], F32)
        ps_sx = palloc(es3, nc, "lnf_psx", [1, 512])
        ps_sq = palloc(es3, nc, "lnf_psq", [1, 512])
        ps_mb = palloc(es3, nc, "lnf_pmb", [128, 512])
        ps_rb = palloc(es3, nc, "lnf_prb", [128, 512])
        oc1 = cst.ones_col(); orow = cst.ones_row()
        epsc = cst.colv(1e-5, 1)
        for tt in range(NTOK // 512):
            t0 = tt * 512
            for fc in range(NCH):
                xs = fmslice(acc, NTOK, fc, t0, 512)
                p.act(sq[:, :], xs, AF.Square, reads=(acc.name,), writes=(sq.name,))
                p.mm_group([(ps_sx[:, :], oc1[:, :], xs, fc == 0, fc == NCH - 1)],
                           reads=(acc.name, oc1.name), writes=(ps_sx.name,))
                p.mm_group([(ps_sq[:, :], oc1[:, :], sq[:, :], fc == 0, fc == NCH - 1)],
                           reads=(sq.name, oc1.name), writes=(ps_sq.name,))
            p.v("tensor_scalar_mul", r_mean[:, :], ps_sx[:, :], 1.0 / D,
                reads=(ps_sx.name,), writes=(r_mean.name,))
            p.v("tensor_mul", r_var[:, :], r_mean[:, :], r_mean[:, :],
                reads=(r_mean.name,), writes=(r_var.name,))
            p.v("scalar_tensor_tensor", r_var[:, :], ps_sq[:, :], 1.0 / D,
                r_var[:, :], op0=ALU.mult, op1=ALU.subtract,
                reads=(ps_sq.name, r_var.name), writes=(r_var.name,))
            p.act(r_var[:, :], r_var[:, :], AF.Sqrt, bias=epsc[0:1, :],
                  reads=(r_var.name, epsc.name), writes=(r_var.name,))
            p.v("reciprocal", r_rstd[:, :], r_var[:, :],
                reads=(r_var.name,), writes=(r_rstd.name,))
            p.mm_group([(ps_mb[:, :], orow[:, :], r_mean[:, :], True, True)],
                       reads=(r_mean.name, orow.name), writes=(ps_mb.name,))
            p.mm_group([(ps_rb[:, :], orow[:, :], r_rstd[:, :], True, True)],
                       reads=(r_rstd.name, orow.name), writes=(ps_rb.name,))
            for fc in range(NCH):
                xs = fmslice(acc, NTOK, fc, t0, 512)
                p.v("tensor_sub", t1[:, :], xs, ps_mb[:, :],
                    reads=(acc.name, ps_mb.name), writes=(t1.name,))
                p.v("tensor_mul", t2[:, :], t1[:, :], ps_rb[:, :],
                    reads=(t1.name, ps_rb.name), writes=(t2.name,))
                p.v("tensor_scalar_mul", t2[:, :], t2[:, :], gam_t[:, fc:fc + 1],
                    reads=(t2.name, gam_t.name), writes=(t2.name,))
                p.act(g2t[:, :], t2[:, :], AF.Silu, bias=bet_t[:, fc:fc + 1],
                      reads=(t2.name, bet_t.name), writes=(g2t.name,))
                p.dma("gpsimd", g2_d[fc * 128:(fc + 1) * 128, t0:t0 + 512],
                      g2t[:, :], reads=(g2t.name,))
    p.barrier()
    return nc, p



def build_l2b():
    nc = bass.Bass("TRN2", target_bir_lowering=False, debug=False)
    x_d = nc.dram_tensor("x", [D, NTOK], F32, kind="ExternalInput").ap()
    partall_d = nc.dram_tensor("partall", [4, 512, 66], F32, kind="ExternalInput").ap()
    qsm_d = nc.dram_tensor("qsm", [512, NTOK], BF16, kind="ExternalInput").ap()
    g2_d = nc.dram_tensor("g2", [D, NTOK], BF16, kind="ExternalInput").ap()
    wo2_d = nc.dram_tensor("wo2", [512, D], F32, kind="ExternalInput").ap()
    bo_d = nc.dram_tensor("bo", [D, 1], F32, kind="ExternalInput").ap()
    g3g_d = nc.dram_tensor("g3g", [D, 1], F32, kind="ExternalInput").ap()
    wqkv3_d = nc.dram_tensor("wqkv3", [D, 3072], F32, kind="ExternalInput").ap()
    gate3_d = nc.dram_tensor("gate3", [D, 8], F32, kind="ExternalInput").ap()

    x3_d = nc.dram_tensor("x3", [D, NTOK], F32, kind="ExternalOutput").ap()
    xn3_d = nc.dram_tensor("xn3", [D, NTOK], BF16, kind="ExternalOutput").ap()
    q3_d = nc.dram_tensor("q3", [D, NTOK], BF16, kind="ExternalOutput").ap()
    k3_d = nc.dram_tensor("k3", [D, NTOK], BF16, kind="ExternalOutput").ap()
    v3_d = nc.dram_tensor("v3", [NTOK, 1040], BF16, kind="ExternalOutput").ap()
    comb3_d = nc.dram_tensor("comb3", [8, NTOK], BF16, kind="ExternalOutput").ap()

    p = Prog(nc)
    es0 = ExitStack()
    cst = Consts(p, es0)
    cst.ones_col(); cst.ones_row(); cst.ident_bf(); cst.ident_f32()
    cst.colv(1e-5, 1); cst.colv(1e-6, 1)

    xn3 = alloc(es0, nc, "xn3s", [128, NCH * NTOK], BF16)
    mrow3 = alloc(es0, nc, "mrow3", [1, NTOK], F32)
    rrow3 = alloc(es0, nc, "rrow3", [1, NTOK], F32)
    comb3 = alloc(es0, nc, "comb3s", [8, NTOK], BF16)

    es_mid = ExitStack()
    x_t = alloc(es_mid, nc, "x_t", [128, NCH * NTOK], F32)
    emit_fm_load(p, x_t, x_d, NTOK)

    with ExitStack() as es1:
        pr = [alloc(es1, nc, f"pr{r}", [128, 4 * 66], F32) for r in range(4)]
        for r in range(4):
            for c4 in range(4):
                p.dma("sync", pr[r][:, c4 * 66:(c4 + 1) * 66],
                      partall_d[r, c4 * 128:(c4 + 1) * 128, :], writes=(pr[r].name,))
        ctxn = alloc(es1, nc, "ctxn", [128, 4 * 64], BF16)
        mg = alloc(es1, nc, "mg", [128, 1], F32)
        zg = alloc(es1, nc, "zg", [128, 1], F32)
        sr = alloc(es1, nc, "sr", [128, 1], F32)
        cacc = alloc(es1, nc, "cacc", [128, 64], F32)
        for c4 in range(4):
            p.v("tensor_copy", mg[:, :], pr[0][:, c4 * 66 + 64: c4 * 66 + 65],
                reads=(pr[0].name,), writes=(mg.name,))
            for r in range(1, 4):
                p.v("tensor_max", mg[:, :], mg[:, :], pr[r][:, c4 * 66 + 64: c4 * 66 + 65],
                    reads=(mg.name, pr[r].name), writes=(mg.name,))
            p.v("memset", zg[:, :], 0.0, writes=(zg.name,))
            p.v("memset", cacc[:, :], 0.0, writes=(cacc.name,))
            for r in range(4):
                p.v("tensor_sub", sr[:, :], pr[r][:, c4 * 66 + 64: c4 * 66 + 65], mg[:, :],
                    reads=(pr[r].name, mg.name), writes=(sr.name,))
                p.act(sr[:, :], sr[:, :], AF.Exp, reads=(sr.name,), writes=(sr.name,))
                p.v("scalar_tensor_tensor", zg[:, :],
                    pr[r][:, c4 * 66 + 65: c4 * 66 + 66], sr[:, :], zg[:, :],
                    op0=ALU.mult, op1=ALU.add,
                    reads=(pr[r].name, sr.name, zg.name), writes=(zg.name,))
                p.v("scalar_tensor_tensor", cacc[:, :],
                    pr[r][:, c4 * 66: c4 * 66 + 64], sr[:, :], cacc[:, :],
                    op0=ALU.mult, op1=ALU.add,
                    reads=(pr[r].name, sr.name, cacc.name), writes=(cacc.name,))
            p.v("reciprocal", zg[:, :], zg[:, :], reads=(zg.name,), writes=(zg.name,))
            p.v("tensor_scalar_mul", ctxn[:, c4 * 64:(c4 + 1) * 64], cacc[:, :], zg[:, :],
                reads=(cacc.name, zg.name), writes=(ctxn.name,))
        qsm_t = alloc(es1, nc, "qsm_t", [128, 4 * NTOK], BF16)
        for c4 in range(4):
            p.dma("sync", qsm_t[:, c4 * NTOK:(c4 + 1) * NTOK],
                  qsm_d[c4 * 128:(c4 + 1) * 128, :], writes=(qsm_t.name,))
        outfm = alloc(es1, nc, "outfm", [128, 4 * NTOK], BF16)
        ps_of = [palloc(es1, nc, f"ps_of{j}", [128, 512]) for j in range(4)]
        for tt in range(NTOK // 512):
            t0 = tt * 512
            mm = []
            for par in range(2):
                for j in range(4):
                    h = 2 * j + par
                    r0 = 64 * par
                    mm.append((ps_of[j][r0:r0 + 64, :],
                               ctxn[r0:r0 + 64, j * 64:(j + 1) * 64],
                               qsm_t[r0:r0 + 64, j * NTOK + t0: j * NTOK + t0 + 512],
                               True, True))
            p.mm_group(mm, reads=(ctxn.name, qsm_t.name),
                       writes=tuple(t.name for t in ps_of))
            for j in range(4):
                p.v("tensor_copy", outfm[:, j * NTOK + t0: j * NTOK + t0 + 512],
                    ps_of[j][:, :], reads=(ps_of[j].name,), writes=(outfm.name,))
        wstg = WBatch(p, es1, "wo2", 4 * D, double=False)
        wo2_t = alloc(es1, nc, "wo2_t", [128, 4 * D], BF16)
        wstg.load(wo2_t, 0, wo2_d[0:512, 0:D], 512, D)
        bo_t = alloc(es1, nc, "bo_t", [128, 8], F32)
        for fc in range(NCH):
            p.dma("sync", bo_t[:, fc:fc + 1], bo_d[fc * 128:(fc + 1) * 128, :],
                  writes=(bo_t.name,))
        g2_t = alloc(es1, nc, "g2_t", [128, 512], BF16)
        sc_s = alloc(es1, nc, "sc_s", [128, 512], F32)
        sc_g = alloc(es1, nc, "sc_g", [128, 512], F32)
        ps_s = palloc(es1, nc, "ps_s", [128, 512])
        for oc in range(NCH):
            for tt in range(NTOK // 512):
                t0 = tt * 512
                mm = [(ps_s[:, :], wo2_t[:, hc * D + oc * 128: hc * D + (oc + 1) * 128],
                       outfm[:, hc * NTOK + t0: hc * NTOK + t0 + 512],
                       hc == 0, hc == 3) for hc in range(4)]
                p.mm_group(mm, reads=(wo2_t.name, outfm.name), writes=(ps_s.name,))
                p.act(sc_s[:, :], ps_s[:, :], AF.Identity, bias=bo_t[:, oc:oc + 1],
                      reads=(ps_s.name, bo_t.name), writes=(sc_s.name,))
                p.dma("sync", g2_t[:, :], g2_d[oc * 128:(oc + 1) * 128, t0:t0 + 512],
                      writes=(g2_t.name,))
                p.v("tensor_mul", sc_g[:, :], sc_s[:, :], g2_t[:, :],
                    reads=(sc_s.name, g2_t.name), writes=(sc_g.name,))
                xs = fmslice(x_t, NTOK, oc, t0, 512)
                p.v("tensor_add", xs, xs, sc_g[:, :],
                    reads=(x_t.name, sc_g.name), writes=(x_t.name,))
        emit_fm_store(p, x3_d, x_t, NTOK)
    p.barrier()
    emit_ln(p, None, cst, x_t, xn3, NTOK, tag="3", stats=(mrow3, None, rrow3))
    emit_fm_store(p, xn3_d, xn3, NTOK)

    with ExitStack() as esg:
        emit_topk_gating(p, esg, cst, nc, x_t, mrow3, rrow3, gate3_d, 8,
                         comb3, mode="l3")
    for hh in range(NTOK // 512):
        p.dma("gpsimd", comb3_d[:, hh * 512:(hh + 1) * 512],
              comb3[:, hh * 512:(hh + 1) * 512], reads=(comb3.name,))
    es_mid.close()
    p.barrier()

    with ExitStack() as es2:
        xn3b = alloc(es2, nc, "xn3b", [128, NCH * NTOK], BF16)
        g3c = alloc(es2, nc, "g3c", [128, 8], F32)
        for fc in range(NCH):
            p.dma("sync", g3c[:, fc:fc + 1], g3g_d[fc * 128:(fc + 1) * 128, :],
                  writes=(g3c.name,))
        sq = alloc(es2, nc, "r_sq", [128, 512], F32)
        rrow = alloc(es2, nc, "r_row", [1, 512], F32)
        t1 = alloc(es2, nc, "r_t1", [128, 512], F32)
        ps_sx = palloc(es2, nc, "r_psx", [1, 512])
        ps_rb = palloc(es2, nc, "r_prb", [128, 512])
        epsc = cst.colv(1e-6, 1)
        for tt in range(NTOK // 512):
            t0 = tt * 512
            for fc in range(NCH):
                p.act(sq[:, :], fmslice(xn3, NTOK, fc, t0, 512), AF.Square,
                      reads=(xn3.name,), writes=(sq.name,))
                p.mm_group([(ps_sx[:, :], cst.ones_col()[:, :], sq[:, :],
                             fc == 0, fc == NCH - 1)],
                           reads=(sq.name, cst.ones_col().name), writes=(ps_sx.name,))
            p.v("tensor_scalar_mul", rrow[:, :], ps_sx[:, :], 1.0 / D,
                reads=(ps_sx.name,), writes=(rrow.name,))
            p.act(rrow[:, :], rrow[:, :], AF.Sqrt, bias=epsc[0:1, :],
                  reads=(rrow.name, epsc.name), writes=(rrow.name,))
            p.v("reciprocal", rrow[:, :], rrow[:, :], reads=(rrow.name,), writes=(rrow.name,))
            p.mm_group([(ps_rb[:, :], cst.ones_row()[:, :], rrow[:, :], True, True)],
                       reads=(cst.ones_row().name, rrow.name), writes=(ps_rb.name,))
            for fc in range(NCH):
                p.v("tensor_mul", t1[:, :], fmslice(xn3, NTOK, fc, t0, 512), ps_rb[:, :],
                    reads=(xn3.name, ps_rb.name), writes=(t1.name,))
                p.v("tensor_scalar_mul", fmslice(xn3b, NTOK, fc, t0, 512),
                    t1[:, :], g3c[:, fc:fc + 1],
                    reads=(t1.name, g3c.name), writes=(xn3b.name,))
        p.barrier()
        wstg = WBatch(p, es2, "wqkv3", 3072, double=False)
        w3_t = alloc(es2, nc, "w3_t", [128, NCH * 3072], BF16)
        for fc in range(NCH):
            wstg.load(w3_t, fc * 3072, wqkv3_d[fc * 128:(fc + 1) * 128, 0:3072],
                      128, 3072)
        qk_t = alloc(es2, nc, "qk_t", [128, 512], BF16)
        ps_p = palloc(es2, nc, "ps_p", [128, 512])
        for (part, out_d) in ((0, q3_d), (1, k3_d)):
            for oc in range(NCH):
                for tt in range(NTOK // 512):
                    t0 = tt * 512
                    mm = [(ps_p[:, :],
                           w3_t[:, fc * 3072 + part * 1024 + oc * 128:
                                fc * 3072 + part * 1024 + (oc + 1) * 128],
                           fmslice(xn3b, NTOK, fc, t0, 512), fc == 0, fc == NCH - 1)
                          for fc in range(NCH)]
                    p.mm_group(mm, reads=(w3_t.name, xn3b.name), writes=(ps_p.name,))
                    p.v("tensor_copy", qk_t[:, :], ps_p[:, :],
                        reads=(ps_p.name,), writes=(qk_t.name,))
                    p.dma("gpsimd", out_d[oc * 128:(oc + 1) * 128, t0:t0 + 512],
                          qk_t[:, :], reads=(qk_t.name,))
        va_t = alloc(es2, nc, "va_t", [128, 1040], BF16)
        p.v("memset", va_t[:, :], 1.0, writes=(va_t.name,))
        ps_v = palloc(es2, nc, "ps_v", [128, 1024])
        for tc in range(NTOK // 128):
            t0 = tc * 128
            for hh in range(2):
                mm = [(ps_v[:, hh * 512:(hh + 1) * 512],
                       fmslice(xn3b, NTOK, fc, t0, 128),
                       w3_t[:, fc * 3072 + 2048 + hh * 512: fc * 3072 + 2048 + (hh + 1) * 512],
                       fc == 0, fc == NCH - 1) for fc in range(NCH)]
                p.mm_group(mm, reads=(xn3b.name, w3_t.name), writes=(ps_v.name,))
            p.v("tensor_copy",
                va_t[:, :].rearrange("p (h e) -> p h e", e=65)[:, :, 0:64],
                ps_v[:, :].rearrange("p (h d) -> p h d", d=64),
                reads=(ps_v.name,), writes=(va_t.name,))
            p.dma("gpsimd", v3_d[t0:t0 + 128, :], va_t[:, :], reads=(va_t.name,))
    p.barrier()
    return nc, p



def build_l3f():
    nc = bass.Bass("TRN2", target_bir_lowering=False, debug=False)
    xn3_d = nc.dram_tensor("xn3", [D, NTOK], BF16, kind="ExternalInput").ap()
    comb_d = nc.dram_tensor("comb", [8, NTOK], BF16, kind="ExternalInput").ap()
    sw1_d = nc.dram_tensor("sw1", [8, 128, 4096], F32, kind="ExternalInput").ap()
    sw2_d = nc.dram_tensor("sw2", [8, 128, 4096], F32, kind="ExternalInput").ap()
    w1e_d = nc.dram_tensor("w1e", [8, 8, 128, 4096], F32, kind="ExternalInput").ap()
    w2e_d = nc.dram_tensor("w2e", [8, 8, 128, 4096], F32, kind="ExternalInput").ap()
    g3_d = nc.dram_tensor("g3", [D, NTOK], BF16, kind="ExternalOutput").ap()

    p = Prog(nc)
    es0 = ExitStack()
    cst = Consts(p, es0)
    cst.ones_col(); cst.ones_row()
    ehot8_np = np.zeros((8, 8 * 128), np.float32)
    for e in range(8):
        ehot8_np[e, e * 128:(e + 1) * 128] = 1.0
    cst.get("ehot8", ehot8_np, BF16)

    xn3 = alloc(es0, nc, "xn3s", [128, NCH * NTOK], BF16)
    emit_fm_load(p, xn3, xn3_d, NTOK)
    comb_bf = alloc(es0, nc, "comb_bf", [8, NTOK], BF16)
    p.dma("sync", comb_bf[:, :], comb_d, writes=(comb_bf.name,))
    acc = alloc(es0, nc, "acc", [128, NCH * NTOK], F32)

    with ExitStack() as es1:
        st = WBatch(p, es1, "sh3", 2048)
        w1t = alloc(es1, nc, "sh_w1t", [128, NCH * 512], BF16)
        w2c = alloc(es1, nc, "sh_w2c", [128, 32 * 128], BF16)
        h_t = alloc(es1, nc, "sh_ht", [128, 32 * 1024], BF16)
        ps_h = palloc(es1, nc, "sh_psh", [128, 512])
        ps_h2 = palloc(es1, nc, "sh_psh2", [128, 512])
        ps_o = palloc(es1, nc, "sh_pso", [128, 512])
        ps_o2 = palloc(es1, nc, "sh_pso2", [128, 512])
        qs = ["sync", "scalar"]
        for tp in range(2):
            tbase = tp * 1024
            for hg in range(8):
                st.load_flat(w1t, 0, sw1_d[hg, :, 0:2048], 2048, eng=qs[hg % 2])
                st.load_flat(w1t, 2048, sw1_d[hg, :, 2048:4096], 2048,
                             eng=qs[(hg + 1) % 2])
                for hc4 in range(4):
                    hc = hg * 4 + hc4
                    for tq in range(2):
                        t0 = tbase + tq * 512
                        psx = ps_h if (hc4 + tq) % 2 == 0 else ps_h2
                        mm = [(psx[:, :],
                               w1t[:, fc * 512 + hc4 * 128: fc * 512 + (hc4 + 1) * 128],
                               fmslice(xn3, NTOK, fc, t0, 512), fc == 0, fc == NCH - 1)
                              for fc in range(NCH)]
                        p.mm_group(mm, reads=(w1t.name, xn3.name), writes=(psx.name,))
                        p.act(h_t[:, hc * 1024 + tq * 512: hc * 1024 + (tq + 1) * 512],
                              psx[:, :], AF.Silu, reads=(psx.name,), writes=(h_t.name,))
            for oc in range(NCH):
                st.load_flat(w2c, 0, sw2_d[oc, :, 0:2048], 2048, eng=qs[oc % 2])
                st.load_flat(w2c, 2048, sw2_d[oc, :, 2048:4096], 2048,
                             eng=qs[(oc + 1) % 2])
                for tq in range(2):
                    t0 = tbase + tq * 512
                    psx = ps_o if (oc + tq) % 2 == 0 else ps_o2
                    mm = [(psx[:, :], w2c[:, hc * 128:(hc + 1) * 128],
                           h_t[:, hc * 1024 + tq * 512: hc * 1024 + (tq + 1) * 512],
                           hc == 0, hc == 31) for hc in range(32)]
                    p.mm_group(mm, reads=(w2c.name, h_t.name), writes=(psx.name,))
                    p.v("tensor_copy", fmslice(acc, NTOK, oc, t0, 512), psx[:, :],
                        reads=(psx.name,), writes=(acc.name + f":{oc}",))
    p.barrier()
    with ExitStack() as es2:
        emit_moe_experts(p, es2, cst, nc, xn3, comb_bf, w1e_d, w2e_d, acc,
                         AF.Silu, b1col_t=None)
    p.barrier()
    with ExitStack() as es3:
        g3t = alloc(es3, nc, "g3t", [128, 512], BF16)
        for oc in range(NCH):
            for tt in range(NTOK // 512):
                t0 = tt * 512
                p.act(g3t[:, :], fmslice(acc, NTOK, oc, t0, 512), AF.Silu,
                      reads=(acc.name,), writes=(g3t.name,))
                p.dma("gpsimd", g3_d[oc * 128:(oc + 1) * 128, t0:t0 + 512],
                      g3t[:, :], reads=(g3t.name,))
    p.barrier()
    return nc, p



def build_l3b():
    nc = bass.Bass("TRN2", target_bir_lowering=False, debug=False)
    x_d = nc.dram_tensor("x", [D, NTOK], F32, kind="ExternalInput").ap()
    xn3_d = nc.dram_tensor("xn3", [D, NTOK], BF16, kind="ExternalInput").ap()
    g3_d = nc.dram_tensor("g3", [D, NTOK], BF16, kind="ExternalInput").ap()
    q3_d = nc.dram_tensor("q3", [D, NTOK], BF16, kind="ExternalInput").ap()
    k3_d = nc.dram_tensor("k3", [D, NTOK], BF16, kind="ExternalInput").ap()
    v3_d = nc.dram_tensor("v3", [NTOK, 1040], BF16, kind="ExternalInput").ap()
    kb_d = nc.dram_tensor("kb", [D, 256], BF16, kind="ExternalInput").ap()
    vb_d = nc.dram_tensor("vb", [256, 1040], BF16, kind="ExternalInput").ap()
    pmask_d = nc.dram_tensor("pmask", [2, 128, 128], F32, kind="ExternalInput").ap()
    wo3_d = nc.dram_tensor("wo3", [D, D], F32, kind="ExternalInput").ap()
    x4_d = nc.dram_tensor("x4", [D, NTOK], F32, kind="ExternalOutput").ap()

    p = Prog(nc)
    es0 = ExitStack()
    cst = Consts(p, es0)
    cst.ones_col(); cst.ones_row()
    parc = np.zeros((128, 2), np.float32)
    parc[0:64, 0] = 1.0; parc[64:128, 1] = 1.0
    parcol = cst.get("parcol", parc)
    curm = np.where(np.arange(128)[:, None] <= np.arange(128)[None, :],
                    0.0, -30000.0).astype(np.float32)
    curmask = cst.get("l3cur", np.tile(curm, (1, 8)))

    outfm = alloc(es0, nc, "outfm", [128, NCH * NTOK], BF16)
    es_attn = ExitStack()
    q_t = alloc(es_attn, nc, "q_t", [128, NCH * NTOK], BF16)
    k_t = alloc(es_attn, nc, "k_t", [128, NCH * NTOK], BF16)
    v_t = alloc(es_attn, nc, "v_t", [128, (NTOK // 128) * 1040], BF16)
    kb_t = alloc(es_attn, nc, "kb_t", [128, NCH * 256], BF16)
    vb_t = alloc(es_attn, nc, "vb_t", [128, 2 * 1040], BF16)
    emit_fm_load(p, q_t, q3_d, NTOK)
    emit_fm_load(p, k_t, k3_d, NTOK)
    emit_fm_load(p, kb_t, kb_d, 256)
    for tc in range(NTOK // 128):
        p.dma("sync", v_t[:, tc * 1040:(tc + 1) * 1040], v3_d[tc * 128:(tc + 1) * 128, :],
              writes=(v_t.name,))
    for bb in range(2):
        p.dma("sync", vb_t[:, bb * 1040:(bb + 1) * 1040], vb_d[bb * 128:(bb + 1) * 128, :],
              writes=(vb_t.name,))
    pm_t = alloc(es_attn, nc, "pm_t", [128, 2 * 128], F32)
    for ob in range(2):
        p.dma("sync", pm_t[:, ob * 128:(ob + 1) * 128], pmask_d[ob], writes=(pm_t.name,))
    pmt = alloc(es_attn, nc, "pmt", [128, 2 * 1024], F32)
    for ob in range(2):
        for i in range(8):
            p.v("tensor_copy", pmt[:, ob * 1024 + i * 128: ob * 1024 + (i + 1) * 128],
                pm_t[:, ob * 128:(ob + 1) * 128],
                reads=(pm_t.name,), writes=(pmt.name,))

    bias_c = alloc(es_attn, nc, "bias_c", [128, 1], F32)
    with ExitStack() as esb:
        sqb = alloc(esb, nc, "sqb", [128, NTOK], F32)
        nr = alloc(esb, nc, "nr", [1, NTOK], F32)
        mxv = alloc(esb, nc, "mxv", [1, 4], F32)
        ps_nb = palloc(esb, nc, "ps_nb", [1, 512])
        ps_bc = palloc(esb, nc, "ps_bc", [128, 1])
        first = {0: True, 1: True}
        for (src_t, W, idx) in ((q_t, NTOK, 0), (k_t, NTOK, 1), (kb_t, 256, 1)):
            for fc in range(NCH):
                p.act(sqb[:, 0:W], src_t[:, fc * W:(fc + 1) * W], AF.Square,
                      reads=(src_t.name,), writes=(sqb.name,))
                for par in range(2):
                    for ss in range(W // 512 if W >= 512 else 1):
                        wseg = min(512, W)
                        p.mm_group([(ps_nb[:, 0:wseg], parcol[:, par:par + 1],
                                     sqb[:, ss * 512: ss * 512 + wseg], True, True)],
                                   reads=(parcol.name, sqb.name), writes=(ps_nb.name,))
                        if first[idx]:
                            p.v("reduce_max", mxv[:, idx:idx + 1], ps_nb[:, 0:wseg],
                                axis=AX.X, reads=(ps_nb.name,), writes=(mxv.name,))
                            first[idx] = False
                        else:
                            p.v("reduce_max", nr[:, 0:1], ps_nb[:, 0:wseg], axis=AX.X,
                                reads=(ps_nb.name,), writes=(nr.name,))
                            p.v("tensor_max", mxv[:, idx:idx + 1], mxv[:, idx:idx + 1],
                                nr[:, 0:1], reads=(mxv.name, nr.name), writes=(mxv.name,))
        p.v("tensor_mul", mxv[:, 0:1], mxv[:, 0:1], mxv[:, 1:2],
            reads=(mxv.name,), writes=(mxv.name,))
        p.act(mxv[:, 0:1], mxv[:, 0:1], AF.Sqrt, reads=(mxv.name,), writes=(mxv.name,))
        p.v("tensor_scalar_mul", mxv[:, 0:1], mxv[:, 0:1], -0.125,
            reads=(mxv.name,), writes=(mxv.name,))
        p.mm_group([(ps_bc[:, :], cst.ones_row()[:, :], mxv[:, 0:1], True, True)],
                   reads=(cst.ones_row().name, mxv.name), writes=(ps_bc.name,))
        p.v("tensor_copy", bias_c[:, :], ps_bc[:, :], reads=(ps_bc.name,), writes=(bias_c.name,))
    p.barrier()

    with ExitStack() as es1:
        ptile = alloc(es1, nc, "ptile", [128, 1024], BF16)
        sc64 = alloc(es1, nc, "sc64", [64, 1024], F32)
        nrm = alloc(es1, nc, "nrm", [64, 1024], F32)
        ps_sp = palloc(es1, nc, "ps_sp", [128, 1024])
        ps_sc = palloc(es1, nc, "ps_sc", [128, 1024])
        ps_ov = palloc(es1, nc, "ps_ov", [65, 1024])
        ps_nb2 = palloc(es1, nc, "ps_nb2", [64, 1024])
        for w in range(NTOK // 128):
            ob = w // 8
            t0 = w * 128
            if w % 8 == 0:
                kprev = lambda h: kb_t[64 * (h % 2):64 * (h % 2) + 64,
                                       (h // 2) * 256 + ob * 128: (h // 2) * 256 + ob * 128 + 128]
                vprev_base = ("vb", ob * 1040)
            else:
                tp0 = (w - 1) * 128
                kprev = lambda h: k_t[64 * (h % 2):64 * (h % 2) + 64,
                                      (h // 2) * NTOK + tp0: (h // 2) * NTOK + tp0 + 128]
                vprev_base = ("v", (w - 1) * 1040)
            for hg in range(2):
                for (ps_x, ksel, masksel) in (
                    (ps_sp, "prev", ("pmt", ob * 1024)),
                    (ps_sc, "cur", ("cur", 0)),
                ):
                    mm = []
                    for i in [0, 2, 4, 6, 1, 3, 5, 7]:
                        h = hg * 8 + i
                        r0 = 64 * (h % 2)
                        col = (i % 2) * 512 + (i // 2) * 128
                        if ksel == "prev":
                            lhs = kprev(h)
                        else:
                            lhs = k_t[r0:r0 + 64, (h // 2) * NTOK + t0:
                                      (h // 2) * NTOK + t0 + 128]
                        rhs = q_t[r0:r0 + 64, (h // 2) * NTOK + t0:
                                  (h // 2) * NTOK + t0 + 128]
                        mm.append((ps_x[:, col:col + 128], lhs, rhs, True, True))
                    p.mm_group(mm, reads=(k_t.name, kb_t.name, q_t.name),
                               writes=(ps_x.name,))
                p.v("tensor_add", ps_sp[:, :], ps_sp[:, :], pmt[:, ob * 1024:(ob + 1) * 1024],
                    reads=(ps_sp.name, pmt.name), writes=(ps_sp.name,))
                p.v("tensor_add", ps_sc[:, :], ps_sc[:, :], curmask[:, :],
                    reads=(ps_sc.name, curmask.name), writes=(ps_sc.name,))
                p.act(ptile[:, :], ps_sp[:, :], AF.Exp, bias=bias_c[:, 0:1], scale=0.125,
                      reads=(ps_sp.name, bias_c.name), writes=(ptile.name,))
                mm = []
                for i in range(8):
                    h = hg * 8 + i
                    col = (i % 2) * 512 + (i // 2) * 128
                    if vprev_base[0] == "vb":
                        vsl = vb_t[:, vprev_base[1] + 65 * h: vprev_base[1] + 65 * h + 65]
                        vread = vb_t.name
                    else:
                        vsl = v_t[:, vprev_base[1] + 65 * h: vprev_base[1] + 65 * h + 65]
                        vread = v_t.name
                    mm.append((ps_ov[:, col:col + 128], vsl, ptile[:, col:col + 128],
                               True, False))
                p.mm_group(mm, reads=(vread, ptile.name), writes=(ps_ov.name,))
                p.act(ptile[:, :], ps_sc[:, :], AF.Exp, bias=bias_c[:, 0:1], scale=0.125,
                      reads=(ps_sc.name, bias_c.name), writes=(ptile.name,))
                mm = []
                for i in range(8):
                    h = hg * 8 + i
                    col = (i % 2) * 512 + (i // 2) * 128
                    vsl = v_t[:, w * 1040 + 65 * h: w * 1040 + 65 * h + 65]
                    mm.append((ps_ov[:, col:col + 128], vsl, ptile[:, col:col + 128],
                               False, True))
                p.mm_group(mm, reads=(v_t.name, ptile.name), writes=(ps_ov.name,))
                p.v("reciprocal", nrm[0:1, :], ps_ov[64:65, :],
                    reads=(ps_ov.name,), writes=(nrm.name,))
                p.mm_group([(ps_nb2[:, hh * 512:(hh + 1) * 512],
                             cst.ones_row()[:, 0:64], nrm[0:1, hh * 512:(hh + 1) * 512],
                             True, True) for hh in range(2)],
                           reads=(cst.ones_row().name, nrm.name), writes=(ps_nb2.name,))
                p.v("tensor_copy", sc64[:, :], ps_ov[0:64, :],
                    reads=(ps_ov.name,), writes=(sc64.name,))
                p.v("tensor_mul", nrm[:, :], sc64[:, :], ps_nb2[:, :],
                    reads=(sc64.name, ps_nb2.name), writes=(nrm.name,))
                for i in range(8):
                    h = hg * 8 + i
                    r0 = 64 * (h % 2)
                    col = (i % 2) * 512 + (i // 2) * 128
                    dst = outfm[r0:r0 + 64, (h // 2) * NTOK + t0:(h // 2) * NTOK + t0 + 128]
                    if r0 == 0:
                        p.v("tensor_copy", dst, nrm[:, col:col + 128],
                            reads=(nrm.name,), writes=(outfm.name,))
                    else:
                        p.dma("gpsimd", dst, nrm[:, col:col + 128],
                              reads=(nrm.name,), writes=(outfm.name,))
    es_attn.close()
    p.barrier()

    with ExitStack() as es2:
        x_t = alloc(es2, nc, "x_t", [128, NCH * NTOK], F32)
        emit_fm_load(p, x_t, x_d, NTOK)
        xn3_t = alloc(es2, nc, "xn3_t", [128, NCH * NTOK], BF16)
        emit_fm_load(p, xn3_t, xn3_d, NTOK)
        wstg = WBatch(p, es2, "wo3", 4 * D, double=False)
        wo3_t = load_w_fm(p, es2, cst, wo3_d, "wo3", D, D, stager=wstg)
        g3_t = alloc(es2, nc, "g3_t", [128, 512], BF16)
        sc_s = alloc(es2, nc, "sc_s", [128, 512], F32)
        sc_g = alloc(es2, nc, "sc_g", [128, 512], F32)
        ps_s = palloc(es2, nc, "ps_s", [128, 512])
        for oc in range(NCH):
            for tt in range(NTOK // 512):
                t0 = tt * 512
                mm = [(ps_s[:, :], wo3_t[:, fc * D + oc * 128: fc * D + (oc + 1) * 128],
                       fmslice(outfm, NTOK, fc, t0, 512), fc == 0, fc == NCH - 1)
                      for fc in range(NCH)]
                p.mm_group(mm, reads=(wo3_t.name, outfm.name), writes=(ps_s.name,))
                p.dma("sync", g3_t[:, :], g3_d[oc * 128:(oc + 1) * 128, t0:t0 + 512],
                      writes=(g3_t.name,))
                p.v("tensor_add", sc_s[:, :], ps_s[:, :],
                    fmslice(xn3_t, NTOK, oc, t0, 512),
                    reads=(ps_s.name, xn3_t.name), writes=(sc_s.name,))
                p.v("tensor_mul", sc_g[:, :], sc_s[:, :], g3_t[:, :],
                    reads=(sc_s.name, g3_t.name), writes=(sc_g.name,))
                xs = fmslice(x_t, NTOK, oc, t0, 512)
                p.v("tensor_add", xs, xs, sc_g[:, :],
                    reads=(x_t.name, sc_g.name), writes=(x_t.name,))
        emit_fm_store(p, x4_d, x_t, NTOK)
    p.barrier()
    return nc, p



def _core_blocks(c):
    pp = c % 4
    return c // 4, [pp, 7 - pp]


def _owner(g):
    pp = g if g < 4 else 7 - g
    return pp, (0 if g < 4 else 1)


_ROPE_THETAS = (10000.0) ** (np.arange(32) / 31.0)


def _rope_tables(c, scale):
    b, blks = _core_blocks(c)
    pos = np.concatenate([np.arange(g * 1024, (g + 1) * 1024) for g in blks])
    ang = pos[:, None] / _ROPE_THETAS[None, :]
    s, co = np.sin(ang).T, np.cos(ang).T
    A = np.concatenate([s, -s, s, -s], 0)
    B = np.concatenate([co, co, co, co], 0)
    return (np.stack([A, B]) * scale).astype(np.float32)


def kernel(**ins):
    import ml_dtypes
    x_full = np.asarray(ins["L0_attn_x"], np.float32)

    def x_core(c, arr):
        b, blks = _core_blocks(c)
        return np.ascontiguousarray(
            np.concatenate([arr[b, g * 1024:(g + 1) * 1024] for g in blks], 0).T
        ).astype(np.float32)

    def _arr_fm(w_t, nhc, csize):
        d_in, ctot = w_t.shape
        k = d_in // 128
        return np.ascontiguousarray(
            w_t.reshape(k, 128, nhc, csize).transpose(2, 1, 0, 3)
               .reshape(nhc, 128, k * csize))

    wgt = _arr_fm(np.asarray(ins["L0_ffn_w_gate"]).T, 24, 128)
    wut = _arr_fm(np.asarray(ins["L0_ffn_w_up"]).T, 24, 128)
    wdt = _arr_fm(np.asarray(ins["L0_ffn_w_down"]).T, 8, 128)
    in_a = [{"x": x_core(c, x_full), "wq": ins["L0_attn_wq"], "wk": ins["L0_attn_wk"],
             "wv": ins["L0_attn_wv"], "wgt": wgt, "wut": wut, "wdt": wdt}
            for c in range(8)]
    res_a = run_launch(build_l0a(), in_a)

    allst = {0: np.zeros((16, 128, 520), np.float32),
             1: np.zeros((16, 128, 520), np.float32)}
    for c in range(8):
        b, blks = _core_blocks(c)
        st = np.asarray(res_a.results[c]["states"])
        for ob, g in enumerate(blks):
            par = np.zeros((128, 520), np.float32)
            for j in range(8):
                par[0:64, j * 65:(j + 1) * 65] = st[ob, 2 * j]
                par[64:128, j * 65:(j + 1) * 65] = st[ob, 2 * j + 1]
            allst[b][g] = par
    win = np.ascontiguousarray(ins["L1_attn_W_in"].T)
    b_in = np.asarray(ins["L1_attn_b_in"]).reshape(-1, 1).astype(np.float32)
    bv_row = np.asarray(ins["L1_attn_b_in"])[2048:].reshape(1, 1024).astype(np.float32)
    qall = np.ascontiguousarray(np.asarray(ins["L1_ffn_q_proj"]).transpose(1, 0, 2).reshape(1024, 1024))
    kall = np.ascontiguousarray(np.asarray(ins["L1_ffn_k_proj"]).transpose(1, 0, 2).reshape(1024, 1024))
    vall = np.ascontiguousarray(np.asarray(ins["L1_ffn_v_proj"]).transpose(1, 0, 2).reshape(1024, 1024))
    in_b = []
    for c in range(8):
        b, blks = _core_blocks(c)
        pm = np.zeros((2, 16, 128, 1), np.float32)
        for ob, g in enumerate(blks):
            pm[ob, 0:g] = 1.0
        r = res_a.results[c]
        in_b.append({"x": in_a[c]["x"], "g0": r["g0"], "acc": r["acc"],
                     "qelu": r["qelu"], "allst": allst[b], "pmask": pm,
                     "wo": ins["L0_attn_wo"], "win": win, "bin": b_in,
                     "bv_row": bv_row, "ropeq": _rope_tables(c, 0.125),
                     "ropek": _rope_tables(c, 1.0),
                     "sim": ins["L1_ffn_sim_matrix"],
                     "gates": np.asarray(ins["L1_ffn_gates"]).reshape(1, 16).astype(np.float32),
                     "qall": qall, "kall": kall, "vall": vall})
    res_b = run_launch(build_l0b(), in_b)

    def gather_global(key, shape, tokrows):
        out = {}
        for b in range(2):
            buf = np.zeros(shape, ml_dtypes.bfloat16)
            for g in range(8):
                pp, ob = _owner(g)
                r = res_b.results[b * 4 + pp]
                if tokrows:
                    buf[g * 1024:(g + 1) * 1024] = r[key][ob * 1024:(ob + 1) * 1024]
                else:
                    buf[:, g * 1024:(g + 1) * 1024] = r[key][:, ob * 1024:(ob + 1) * 1024]
            out[b] = buf
        return out

    Kg = gather_global("k1", (1024, 8192), False)
    Qg = gather_global("q1", (1024, 8192), False)
    Vg = gather_global("v1", (8192, 1040), True)
    kmg = gather_global("km", (64, 8192), False)
    vmg = gather_global("vm", (8192, 65), True)
    diag = np.full((128, 1920), -30000.0, np.float32)
    for kl in range(128):
        diag[kl, kl + 896:] = 0.0
    blocked = np.full((128, 1920), -30000.0, np.float32)
    fullp = np.zeros((128, 1920), np.float32)
    in_f = []
    for c in range(8):
        b, blks = _core_blocks(c)
        p4 = c % 4
        hs = slice(4 * p4 * 64, 4 * (p4 + 1) * 64)
        ms = np.zeros((2, 8, 128, 1920), np.float32)
        for ob, g in enumerate(blks):
            for s in range(8):
                ms[ob, s] = fullp if s < g else (diag if s == g else blocked)
        in_f.append({"kh": np.ascontiguousarray(Kg[b][hs]),
                     "qh": np.ascontiguousarray(Qg[b][hs]),
                     "vh": np.ascontiguousarray(Vg[b][:, 65 * 4 * p4:65 * 4 * (p4 + 1)]),
                     "kmall": kmg[b], "vmall": vmg[b],
                     "qm": res_b.results[c]["qm"],
                     "mstrips": ms.astype(ml_dtypes.bfloat16)})
    res_f = run_launch(build_l1f(), in_f)

    oall = np.ascontiguousarray(np.asarray(ins["L1_ffn_o_proj"]).reshape(1024, 1024))
    woutT = np.ascontiguousarray(ins["L1_attn_W_out"].T)
    bout = np.asarray(ins["L1_attn_b_out"]).reshape(-1, 1).astype(np.float32)
    in_c = []
    for c in range(8):
        b, blks = _core_blocks(c)
        aoh = np.zeros((1024, 2048), ml_dtypes.bfloat16)
        for p4 in range(4):
            rr = res_f.results[b * 4 + p4]["ao"]
            for ob, g in enumerate(blks):
                aoh[p4 * 256:(p4 + 1) * 256, ob * 1024:(ob + 1) * 1024] = \
                    rr[:, g * 1024:(g + 1) * 1024]
        in_c.append({"x": res_b.results[c]["x1"], "aoh": aoh,
                     "fao": res_f.results[c]["fao"], "rw": res_b.results[c]["rw"],
                     "oall": oall, "woutT": woutT, "bout": bout,
                     "wq2": ins["L2_attn_Wq"], "wk2": ins["L2_attn_Wk"],
                     "wv2": ins["L2_attn_Wv"], "gateW": ins["L2_ffn_gate_W"]})
    res_c = run_launch(build_l1c(), in_c)

    w1r2 = np.stack([_arr_fm(np.asarray(ins["L2_ffn_W1"])[e], 8, 512) for e in range(8)])
    w2r2 = np.stack([_arr_fm(np.asarray(ins["L2_ffn_W2"])[e], 8, 128) for e in range(8)])
    b1c = np.asarray(ins["L2_ffn_b1"]).reshape(-1, 1).astype(np.float32)
    in_2f = [{"xn2": res_c.results[c]["xn2"], "comb": res_c.results[c]["comb2"],
              "sharedW": ins["L2_ffn_shared_W"],
              "sharedb": np.asarray(ins["L2_ffn_shared_b"]).reshape(-1, 1).astype(np.float32),
              "w1e": w1r2, "b1c": b1c, "w2e": w2r2,
              "b2r": ins["L2_ffn_b2"],
              "gamma": np.asarray(ins["L2_ffn_gamma"]).reshape(-1, 1).astype(np.float32),
              "beta": np.asarray(ins["L2_ffn_beta"]).reshape(-1, 1).astype(np.float32)}
             for c in range(8)]
    res_2f = run_launch(build_l2f(), in_2f)

    gate3 = np.ascontiguousarray(np.asarray(ins["L3_ffn_gate_w"]).T)
    in_2b = []
    for c in range(8):
        b, blks = _core_blocks(c)
        partall = np.stack([np.asarray(res_c.results[b * 4 + r]["part2"])
                            for r in range(4)])
        in_2b.append({"x": res_c.results[c]["x2"], "partall": partall,
                      "qsm": res_c.results[c]["qsm"], "g2": res_2f.results[c]["g2"],
                      "wo2": ins["L2_attn_Wo"],
                      "bo": np.asarray(ins["L2_attn_bo"]).reshape(-1, 1).astype(np.float32),
                      "g3g": np.asarray(ins["L3_attn_g"]).reshape(-1, 1).astype(np.float32),
                      "wqkv3": ins["L3_attn_Wqkv"], "gate3": gate3})
    res_2b = run_launch(build_l2b(), in_2b)

    sw1 = _arr_fm(np.asarray(ins["L3_ffn_shared_w1"]).T, 8, 512)
    sw2 = _arr_fm(np.asarray(ins["L3_ffn_shared_w2"]).T, 8, 128)
    w1e3 = np.stack([_arr_fm(np.asarray(ins["L3_ffn_experts_w1"])[e].T, 8, 512)
                     for e in range(8)])
    w2e3 = np.stack([_arr_fm(np.asarray(ins["L3_ffn_experts_w2"])[e].T, 8, 128)
                     for e in range(8)])
    in_3f = [{"xn3": res_2b.results[c]["xn3"], "comb": res_2b.results[c]["comb3"],
              "sw1": sw1, "sw2": sw2, "w1e": w1e3, "w2e": w2e3}
             for c in range(8)]
    res_3f = run_launch(build_l3f(), in_3f)

    in_3b = []
    for c in range(8):
        b, blks = _core_blocks(c)
        kb = np.zeros((1024, 256), ml_dtypes.bfloat16)
        vb = np.zeros((256, 1040), ml_dtypes.bfloat16)
        pmw = np.zeros((2, 128, 128), np.float32)
        for ob, g in enumerate(blks):
            if g == 0:
                pmw[ob] = -30000.0
            else:
                pmw[ob] = np.where(np.arange(128)[:, None] > np.arange(128)[None, :],
                                   0.0, -30000.0)
                pp, obp = _owner(g - 1)
                rp = res_2b.results[b * 4 + pp]
                kb[:, ob * 128:(ob + 1) * 128] = \
                    rp["k3"][:, obp * 1024 + 896: obp * 1024 + 1024]
                vb[ob * 128:(ob + 1) * 128] = \
                    rp["v3"][obp * 1024 + 896: obp * 1024 + 1024]
        in_3b.append({"x": res_2b.results[c]["x3"], "xn3": res_2b.results[c]["xn3"],
                      "g3": res_3f.results[c]["g3"], "q3": res_2b.results[c]["q3"],
                      "k3": res_2b.results[c]["k3"], "v3": res_2b.results[c]["v3"],
                      "kb": kb, "vb": vb, "pmask": pmw, "wo3": ins["L3_attn_Wo"]})
    res_3b = run_launch(build_l3b(), in_3b)

    out = np.zeros((2, 8192, 1024), np.float32)
    for c in range(8):
        b, blks = _core_blocks(c)
        x4 = np.asarray(res_3b.results[c]["x4"])
        for ob, g in enumerate(blks):
            out[b, g * 1024:(g + 1) * 1024] = x4[:, ob * 1024:(ob + 1) * 1024].T
    return out
